# version 18
# baseline (speedup 1.0000x reference)
import numpy as np
import concourse.bass as bass
import concourse.mybir as mybir
from concourse.bass_utils import run_bass_kernel_spmd

F32 = mybir.dt.float32
F16 = mybir.dt.float16
BF16 = mybir.dt.bfloat16
U32 = mybir.dt.uint32
AF = mybir.ActivationFunctionType
ALU = mybir.AluOpType
AX = mybir.AxisListType

N_CORES = 8
B_GLOB, L, D = 16, 2048, 1024
B_LOC = 2
H, DK = 16, 64
NPAIR = 8
KC = 8
NCH = 16
NEXP, INTER = 8, 4096
ICH = 32
GTOK = 1024
TOKS = B_LOC * L
EPS = 1e-5

ENGS = ("sync", "gpsimd", "vector", "scalar", "tensor")
CSEM = {"tensor": "sT", "vector": "sV", "scalar": "sS", "gpsimd": "sP", "sync": "sY"}
NDQ = 8
DSEM = {"sync": ["dS%d" % i for i in range(NDQ)],
        "gpsimd": ["dP%d" % i for i in range(NDQ)],
        "scalar": ["dA%d" % i for i in range(NDQ)]}
SEMS = tuple(["sT", "sV", "sS", "sP", "sY"] + DSEM["sync"] + DSEM["gpsimd"]
             + DSEM["scalar"])

TAPS = {}


class Buf(object):
    __slots__ = ("name", "w", "r", "ov")

    def __init__(self, name):
        self.name = name
        self.w = None
        self.r = []
        self.ov = []


class Prog(object):
    def __init__(self):
        self.items = {e: [] for e in ENGS}
        self.cnt = {s: 0 for s in SEMS}
        self.strict_self = True
        self.drot = {e: 0 for e in DSEM}

    def op(self, eng, fn, reads=(), writes=(), extra=(), inc=True, dma=False):
        deps = [d for d in extra if d is not None]
        for b in reads:
            if b.w is not None:
                deps.append(b.w)
        for b in writes:
            for v in (b,) + tuple(b.ov):
                if v.w is not None:
                    deps.append(v.w)
                deps.extend(v.r)
        if dma:
            sem, amt = DSEM[eng][self.drot[eng] % NDQ], 16
            self.drot[eng] += 1
        elif inc:
            sem, amt = CSEM[eng], 1
        else:
            sem, amt = None, 0
        self.items[eng].append((fn, deps, sem, amt))
        tok = None
        if sem is not None:
            self.cnt[sem] += amt
            tok = (sem, self.cnt[sem])
            for b in reads:
                b.r.append(tok)
            for b in writes:
                for v in (b,) + tuple(b.ov):
                    v.w = tok
                    v.r = []
        return tok

    def install(self, block, sems):
        for name in ENGS:
            items = self.items[name]
            own = CSEM[name]

            def body(eng, items=items, own=own, name=name):
                wm = {s: 0 for s in SEMS}
                for fn, deps, sem, amt in items:
                    best = {}
                    for (s, v) in deps:
                        if s == own and (name == "tensor" or not self.strict_self):
                            continue
                        if v > best.get(s, 0):
                            best[s] = v
                    for s in sorted(best):
                        if best[s] > wm[s]:
                            eng.wait_ge(sems[s], best[s])
                            wm[s] = best[s]
                    inst = fn(eng)
                    if sem is not None:
                        inst.then_inc(sems[sem], amt)

            getattr(block, name)(body)


class Build(object):
    def __init__(self):
        self.nc = bass.Bass("TRN2", target_bir_lowering=False)
        self.p = Prog()
        self.bufs = {}
        self.ranges = {}
        self.prep_tok = {}
        self._mk_io()
        self._mk_sbuf()
        self.CAPV = [1e-9, 1e-5, 0.5, -0.5, 0.125]
        for i, v in enumerate(self.CAPV):
            self.nc.const_aps.register(F32, v, self.c_cap[:, i:i + 1])
        self._link_overlaps()
        self._mk_psum()

    def B(self, name):
        if name not in self.bufs:
            self.bufs[name] = Buf(name)
        return self.bufs[name]

    def _mk_io(self):
        nc = self.nc
        di = lambda n, s: nc.dram_tensor(n, s, F32, kind="ExternalInput")
        self.xt = di("xt", [128, KC, TOKS])
        self.wqt, self.wkt = di("wqt", [128, KC, D]), di("wkt", [128, KC, D])
        self.wvt, self.wot = di("wvt", [128, KC, D]), di("wot", [128, KC, D])
        self.awt = di("awt", [128, KC, D])
        self.bq_r, self.bv_r = di("bq_r", [1, D]), di("bv_r", [1, D])
        self.bk_pc, self.bo_pc = di("bk_pc", [128, KC]), di("bo_pc", [128, KC])
        self.l1g, self.l1b = di("l1g", [128, KC]), di("l1b", [128, KC])
        self.l2g, self.l2b = di("l2g", [128, KC]), di("l2b", [128, KC])
        self.gwt = di("gwt", [128, KC, NEXP])
        self.gpt = di("gpt", [NEXP, ICH, 128, KC, 128])
        self.upt = di("upt", [NEXP, ICH, 128, KC, 128])
        self.dpt = di("dpt", [NEXP, KC, 128, ICH, 128])
        self.out = nc.dram_tensor("out", [D, TOKS], F32, kind="ExternalOutput")
        ds = lambda n, s, t: nc.dram_tensor(n, s, t, kind="Internal")
        self.s_xhi = ds("s_xhi", [128, KC, TOKS], BF16)
        self.s_wqhi = ds("s_wqhi", [128, KC, D], BF16)
        self.s_wk = ds("s_wk", [128, KC, D], BF16)
        self.s_wv = ds("s_wv", [128, KC, D], BF16)
        self.s_wo = ds("s_wo", [128, KC, D], BF16)
        self.s_awt = ds("s_awt", [128, KC, D], BF16)
        self.s_q = [ds("s_q0", [NCH, 128, D], BF16), ds("s_q1", [NCH, 128, D], BF16)]
        self.s_k = [ds("s_k0", [KC, 128, L], BF16), ds("s_k1", [KC, 128, L], BF16)]
        self.s_v = [ds("s_v0", [NCH, 128, D], BF16), ds("s_v1", [NCH, 128, D], BF16)]
        self.s_sp = ds("s_sp", [4, 128, KC, GTOK], F32)

    def _mk_sbuf(self):
        nc = self.nc
        self._cur = 16512
        top = nc.sbuf_top

        def A(name, shape, dtype, off=None):
            nbytes = int(np.prod(shape[1:])) * mybir.dt.size(dtype)
            if off is None:
                off = (self._cur + 63) // 64 * 64
                self._cur = off + nbytes
            assert off + nbytes <= top, (name, off, nbytes, top)
            self.ranges[name] = (off, off + nbytes)
            t = nc.alloc_sbuf_tensor_at(name, list(shape), dtype, offset=off)
            setattr(self, name, t)
            return off

        for nm, sh, dt in [
            ("c_iotaF", [128, L], F16), ("c_iotaPC", [128, 16], F32),
            ("c_idn16", [128, 128], BF16), ("c_idnF", [128, 128], F32),
            ("c_ones_row", [1, 128], F32), ("c_ones_col", [128, 1], F32),
            ("c_basis", [8, 1024], F32), ("c_gwhi", [128, 64], BF16),
            ("c_gwlo", [128, 64], BF16), ("gw32", [128, 64], F32),
            ("c_bk", [128, KC], F32), ("c_bo", [128, KC], F32),
            ("c_l1g", [128, KC], F32), ("c_l1b", [128, KC], F32),
            ("c_l2g", [128, KC], F32), ("c_l2b", [128, KC], F32),
            ("c_jmod", [128, 1], F32),
            ("c_cap", [128, 8], F32),
        ]:
            A(nm, sh, dt)

        R1 = (self._cur + 63) // 64 * 64
        A("strmG", [128, KC, GTOK], F32, R1)
        self._cur = R1 + 32768

        R2 = self._cur
        A("wqhi", [128, KC, D], BF16, R2)
        A("wqlo", [128, KC, D], BF16, R2 + 16384)
        A("wvb", [128, KC, D], BF16, R2 + 32768)
        A("full", [128, KC, L], BF16, R2)
        A("hbuf", [128, KC, GTOK], F32, R2)
        A("zbuf", [128, KC, GTOK], BF16, R2 + 32768)
        A("abuf", [128, ICH // 2, GTOK], BF16, R2)
        A("strmB", [128, KC, GTOK], F32, R2)
        A("hbufB", [128, KC, GTOK], F32, R1)
        A("abufB", [128, ICH // 2, GTOK], BF16, R1)
        self._cur = R2 + 49152

        R3 = self._cur
        A("p_sb", [128, L], BF16, R3)
        A("prT", [128, NCH, NPAIR * 20], BF16, R3 + 4096)
        A("ohT", [128, NCH, H * 10], BF16, R3 + 14336)
        A("oh2", [128, L], BF16, R3 + 24576)
        A("kpair", [128, L], BF16, R3 + 28672)
        A("tbuf", [128, KC, GTOK], BF16, R3)
        A("dstage_0", [128, ICH // 2, 128], BF16, R3 + 16384)
        A("dstage_1", [128, ICH // 2, 128], BF16, R3 + 20480)
        A("gstage_0", [128, KC, 128], BF16, R3 + 32768)
        A("gstage_1", [128, KC, 128], BF16, R3 + 34816)
        A("ustage_0", [128, KC, 128], BF16, R3 + 36864)
        A("ustage_1", [128, KC, 128], BF16, R3 + 38912)
        A("gstage_2", [128, KC, 128], BF16, R3 + 24576)
        A("ustage_2", [128, KC, 128], BF16, R3 + 26624)
        A("dstage_2", [128, ICH // 2, 128], BF16, R3 + 28672)
        A("lgsb", [8, GTOK], F32, R3 + 40960)
        A("combfm", [8, GTOK], F32, R3 + 40960)
        A("combbc", [128, GTOK], F32, R3 + 45056)
        A("q2s_1", [128, D], F32, R3 + 40960)
        A("esc_1", [128, D], F32, R3 + 45056)
        A("stat_1", [128, 16 * 8], F32, R3 + 32768)
        A("qevac_1", [128, D], BF16, R3 + 36864)
        self._cur = R3 + 49152

        R4 = self._cur
        for j in range(2):
            A("xc32_%d" % j, [128, KC, 128], F32, R4 + 4096 * j)
            A("xh32_%d" % j, [128, KC, 128], F32, R4 + 8192 + 4096 * j)
            A("xhic_%d" % j, [128, KC, 128], BF16, R4 + 16384 + 2048 * j)
            A("q32_%d" % j, [128, D], F32, R4 + 24576 + 4096 * j)
            A("xhit_%d" % j, [128, KC, 512], BF16, R4 + 8192 * j)
            A("kevac_%d" % j, [128, 512], BF16, R4 + 16384 + 1024 * j)
            A("vevac_%d" % j, [128, D], BF16, R4 + 18432 + 2048 * j)
            A("qc_%d" % j, [128, D], BF16, R4 + 2048 * j)
            A("vc_%d" % j, [128, D], BF16, R4 + 4096 + 2048 * j)
            A("xres_%d" % j, [128, L], F32, R4 + 8192 + 8192 * j)
            A("sqscr_%d" % j, [128, 512], F32, R4 + 2048 * j)
            A("sgscr_%d" % j, [128, 512], F32, R4 + 4096 + 2048 * j)
            A("t32scr_%d" % j, [128, 512], F32, R4 + 8192 + 2048 * j)
            A("tloscr_%d" % j, [128, 512], BF16, R4 + 12288 + 1024 * j)
            A("haccscr_%d" % j, [128, 512], F32, R4 + 14336 + 2048 * j)
            A("aplscr_%d" % j, [128, 512], F32, R4 + 18432 + 2048 * j)
            A("spillscr_%d" % j, [128, 512], F32, R4 + 0 + 2048 * j)
        A("qevac_0", [128, D], BF16, R4 + 40960)
        A("q2s_0", [128, D], F32, R4 + 32768)
        A("esc_0", [128, D], F32, R4 + 36864)
        A("lgtm", [128, 64], F32, R4 + 28672)
        A("combtm", [128, 64], F32, R4 + 28928)
        A("gcols", [128, 24], F32, R4 + 29184)
        A("p_sb2", [128, L], BF16, R4 + 20480)
        A("prT2", [128, NCH, NPAIR * 20], BF16, 201216)
        A("kpair2", [128, L], BF16, R4 + 0)
        A("r1b", [128, GTOK], F32, R4 + 32768)
        A("r2b", [128, GTOK], F32, R4 + 36864)
        self._cur = R4 + 43008

        R5 = self._cur
        A("qs0", [16, L], F32, R5)
        A("bqv_b", [128, D], F32, R5 + 8192)
        A("browscr", [1, D], F32, R5 + 12288)
        A("idxflat", [1, H * 10], F32, R5 + 12288)
        A("thr", [1, 8], F32, R5 + 12928)
        A("s_row", [1, GTOK], F32, R5)
        A("qq_row", [1, GTOK], F32, R5 + 4096)
        A("r_row", [1, GTOK], F32, R5 + 8192)
        A("rowscr", [1, GTOK], F32, R5 + 12992)
        self._cur = R5 + 12992 + 4096

        for nm, sh, dt in [
            ("wstage_0", [128, KC, 128], BF16), ("wstage_1", [128, KC, 128], BF16),
            ("m4", [128, 4], F32), ("mcol", [128, 1], F32), ("mbias", [128, 1], F32),
            ("rscol", [128, 1], F32), ("rv_all", [128, NPAIR], F32),
            ("valid", [128, 1], F32), ("vtmp", [128, 1], F32),
            ("idxT", [128, 16], F32), ("idxf", [16, 10], F32),
            ("i8", [16, 8], U32), ("i2", [16, 8], U32),
            ("t8a", [16, 8], F32), ("t8b", [16, 8], F32),
            ("idx_bc", [128, H * 10], F32),
            ("scat_0", [128, 128], BF16), ("scat_1", [128, 128], BF16),
            ("stat_0", [128, 16 * 8], F32),
            ("urow", [1, 8], F32),
            ("idxT2", [128, 16], F32), ("valid2", [128, 1], F32),
            ("rv_all2", [128, NPAIR], F32), ("qspT2", [128, NPAIR * 10], BF16),
            ("qspT", [128, NPAIR * 10], BF16),
        ]:
            A(nm, sh, dt)
        A("qs0_1x", [16, L], F32, self.ranges["strmG"][0])
        R1_, R2_ = self.ranges["strmG"][0], self.ranges["wqhi"][0]
        A("xhic_2", [128, KC, 128], BF16, R1_ + 8192)
        A("qevac_2", [128, D], BF16, R1_ + 10240)
        A("q32_2", [128, D], F32, R1_ + 12288)
        A("q2s_2", [128, D], F32, R1_ + 16384)
        A("esc_2", [128, D], F32, R1_ + 20480)
        A("stat_2", [128, 16 * 8], F32, R1_ + 24576)
        A("stat_3", [128, 16 * 8], F32, R1_ + 25088)
        A("vevac_2", [128, D], BF16, R1_ + 10240)
        A("xhic_3", [128, KC, 128], BF16, R2_ + 32768)
        A("qevac_3", [128, D], BF16, R2_ + 34816)
        A("q32_3", [128, D], F32, R2_ + 36864)
        A("q2s_3", [128, D], F32, R2_ + 40960)
        A("esc_3", [128, D], F32, R2_ + 45056)
        A("vevac_3", [128, D], BF16, R2_ + 34816)
        assert self._cur <= self.nc.sbuf_top, (self._cur, self.nc.sbuf_top)

    def _link_overlaps(self):
        names = list(self.ranges)
        for i, a in enumerate(names):
            sa, ea = self.ranges[a]
            ba = self.B(a)
            for b_ in names[i + 1:]:
                sb, eb = self.ranges[b_]
                if sa < eb and sb < ea:
                    bb = self.B(b_)
                    ba.ov.append(bb)
                    bb.ov.append(ba)

    def _mk_psum(self):
        nc = self.nc
        self.PS = [nc.alloc_psum_tensor("PS%d" % i, [128, 512], F32) for i in range(6)]
        self.PT = [nc.alloc_psum_tensor("PT%d" % i, [128, 256], BF16) for i in range(2)]
        self.psb = [self.B("PS%d" % i) for i in range(6)]
        self.ptb = [self.B("PT%d" % i) for i in range(2)]
        self._ring = 0
        self._ringw = 4
        self._tring = 0

    def tap(self, name, ap, buf):
        name = name + getattr(self, "_tapsuf", "")
        if not getattr(self, "_tapon", False) or name not in TAPS:
            return
        t = self.nc.dram_tensor("tap_" + name, list(ap.shape), ap.dtype,
                                kind="ExternalOutput")
        self.p.op("sync", lambda e, t=t, ap=ap: e.dma_start(out=t.ap(), in_=ap),
                  reads=[buf], dma=True)

    def ring(self):
        i = self._ring % self._ringw
        self._ring += 1
        return self.PS[i], self.psb[i]

    def xbank(self):
        return self.PS[5], self.psb[5]

    def xbank2(self):
        return self.PS[4], self.psb[4]

    def tring(self):
        i = self._tring % 2
        self._tring += 1
        return self.PT[i], self.ptb[i]

    def build_consts(self):
        p = self.p
        scr = self.strmG[:].rearrange("q a b -> q (a b)")
        bA = self.B("strmG")
        t1, t2 = scr[:, 0:128], scr[:, 128:256]
        tb8 = scr[0:8, 256:1280]
        tbg = scr[:, 1280:1344]
        iF, iPC = self.B("c_iotaF"), self.B("c_iotaPC")
        for i, v in enumerate(self.CAPV):
            p.op("vector", lambda e, i=i, v=v: e.memset(self.c_cap[:, i:i + 1], v),
                 writes=[self.B("c_cap")])
        p.op("gpsimd", lambda e: e.iota(self.c_iotaF[:], [[1, L]], channel_multiplier=0,
                                        allow_small_or_imprecise_dtypes=True), writes=[iF])
        p.op("gpsimd", lambda e: e.iota(self.c_iotaPC[:], [[128, 16]], channel_multiplier=1,
                                        allow_small_or_imprecise_dtypes=True), writes=[iPC])
        p.op("gpsimd", lambda e: e.iota(t1, [[0, 128]], channel_multiplier=1,
                                        allow_small_or_imprecise_dtypes=True), writes=[bA])
        p.op("gpsimd", lambda e: e.iota(t2, [[1, 128]], channel_multiplier=0,
                                        allow_small_or_imprecise_dtypes=True), writes=[bA])
        p.op("vector", lambda e: e.tensor_tensor(out=self.c_idnF[:], in0=t1, in1=t2,
                                                 op=ALU.is_equal),
             reads=[bA], writes=[self.B("c_idnF")])
        p.op("vector", lambda e: e.tensor_copy(out=self.c_idn16[:], in_=self.c_idnF[:]),
             reads=[self.B("c_idnF")], writes=[self.B("c_idn16")])
        p.op("vector", lambda e: e.memset(self.c_ones_row[:], 1.0), writes=[self.B("c_ones_row")])
        p.op("vector", lambda e: e.memset(self.c_ones_col[:], 1.0), writes=[self.B("c_ones_col")])
        p.op("gpsimd", lambda e: e.iota(tb8.rearrange("q (a b) -> q a b", b=128),
                                        [[1, 8], [0, 128]], channel_multiplier=0,
                                        allow_small_or_imprecise_dtypes=True), writes=[bA])
        p.op("vector", lambda e: e.tensor_scalar(out=self.c_basis[:], in0=tb8,
                                                 scalar1=self.c_iotaPC[0:8, 0:1], scalar2=None,
                                                 op0=ALU.is_equal),
             reads=[bA, iPC], writes=[self.B("c_basis")])
        p.op("vector", lambda e: e.tensor_scalar(out=self.c_jmod[:], in0=self.c_iotaPC[:, 0:1],
                                                 scalar1=64.0, scalar2=None, op0=ALU.is_ge),
             reads=[iPC], writes=[self.B("c_jmod")])
        p.op("vector", lambda e: e.scalar_tensor_tensor(out=self.c_jmod[:], in0=self.c_jmod[:],
                                                        scalar=-64.0, in1=self.c_iotaPC[:, 0:1],
                                                        op0=ALU.mult, op1=ALU.add),
             reads=[iPC, self.B("c_jmod")], writes=[self.B("c_jmod")])
        for (name, src, dst) in [("c_bk", self.bk_pc, self.c_bk), ("c_bo", self.bo_pc, self.c_bo),
                                 ("c_l1g", self.l1g, self.c_l1g), ("c_l1b", self.l1b, self.c_l1b),
                                 ("c_l2g", self.l2g, self.c_l2g), ("c_l2b", self.l2b, self.c_l2b)]:
            p.op("scalar", lambda e, s=src, d=dst: e.dma_start(out=d[:], in_=s.ap()),
                 writes=[self.B(name)], dma=True)
        p.op("scalar", lambda e: e.dma_start(
            out=self.gw32[:].rearrange("q (kc ex) -> q kc ex", ex=8),
            in_=self.gwt.ap()),
            writes=[self.B("gw32")], dma=True)
        p.op("vector", lambda e: e.tensor_copy(out=self.c_gwhi[:], in_=self.gw32[:]),
             reads=[self.B("gw32")], writes=[self.B("c_gwhi")])
        p.op("vector", lambda e: e.tensor_copy(out=tbg, in_=self.c_gwhi[:]),
             reads=[self.B("c_gwhi")], writes=[bA])
        p.op("vector", lambda e: e.tensor_tensor(out=self.c_gwlo[:], in0=self.gw32[:],
                                                 in1=tbg, op=ALU.subtract),
             reads=[bA, self.B("gw32")], writes=[self.B("c_gwlo")])
        p.op("vector", lambda e: e.memset(self.oh2[:], 0.0), writes=[self.B("oh2")])
        p.op("vector", lambda e: e.memset(self.p_sb[:], 0.0), writes=[self.B("p_sb")])
        for i in range(6):
            p.op("vector", lambda e, i=i: e.memset(self.PS[i][:], 0.0), writes=[self.psb[i]])


    def build_prep(self):
        p = self.p
        casts = [("xhi", self.s_xhi.ap(), self.xt.ap()),
                 ("wqhi", self.s_wqhi.ap(), self.wqt.ap()),
                 ("wk", self.s_wk.ap(), self.wkt.ap()),
                 ("wv", self.s_wv.ap(), self.wvt.ap()),
                 ("wo", self.s_wo.ap(), self.wot.ap()),
                 ("awt", self.s_awt.ap(), self.awt.ap())]
        def fence():
            return [(s, p.cnt[s]) for s in DSEM["gpsimd"] if p.cnt[s]]

        for name, dst, src in casts:
            p.op("gpsimd", lambda e, d=dst, s=src: e.dma_start(out=d, in_=s), dma=True)
        ftok = fence()
        for name, _, _ in casts:
            self.prep_tok[name] = ftok

    def bcast_row(self, src_ap, src_buf, width, dst_ap, dst_buf):
        p = self.p
        for t in range((width + 511) // 512):
            w = min(512, width - t * 512)
            ps, pb = self.ring()
            p.op("tensor", lambda e, ps=ps, t=t, w=w, s=src_ap: e.matmul(
                ps[0:128, 0:w], self.c_ones_row[:], s[0:1, t * 512:t * 512 + w],
                start=True, stop=True),
                reads=[src_buf, self.B("c_ones_row")], writes=[pb])
            p.op("vector", lambda e, ps=ps, t=t, w=w, d=dst_ap: e.tensor_copy(
                out=d[:, t * 512:t * 512 + w], in_=ps[0:128, 0:w]),
                reads=[pb], writes=[dst_buf])

    def ln_rows(self, src, src_buf):
        p = self.p
        sb = [self.ring() for _ in range(2)]
        qb = [self.ring() for _ in range(2)]
        for tt in range(2):
            ps, pb = sb[tt]
            for mc in range(KC):
                p.op("tensor", lambda e, ps=ps, mc=mc, tt=tt, s=src: e.matmul(
                    ps[0:1, 0:512], self.c_ones_col[:], s[:, mc, tt * 512:(tt + 1) * 512],
                    start=(mc == 0), stop=(mc == KC - 1)),
                    reads=[src_buf, self.B("c_ones_col")], writes=[pb], inc=(mc == KC - 1))
            psq, pbq = qb[tt]
            for mc in range(KC):
                scr = getattr(self, "sqscr_%d" % (mc % 2))
                scb = self.B("sqscr_%d" % (mc % 2))
                p.op("vector", lambda e, sc=scr, mc=mc, tt=tt, s=src: e.tensor_tensor(
                    out=sc[:], in0=s[:, mc, tt * 512:(tt + 1) * 512],
                    in1=s[:, mc, tt * 512:(tt + 1) * 512], op=ALU.mult),
                    reads=[src_buf], writes=[scb])
                p.op("tensor", lambda e, ps=psq, sc=scr, mc=mc: e.matmul(
                    ps[0:1, 0:512], self.c_ones_col[:], sc[:],
                    start=(mc == 0), stop=(mc == KC - 1)),
                    reads=[scb, self.B("c_ones_col")],
                    writes=[pbq] if mc == KC - 1 else (), inc=True)
        rS, rQ, rR, rW = self.B("s_row"), self.B("qq_row"), self.B("r_row"), self.B("rowscr")
        for tt in range(2):
            ps, pb = sb[tt]
            p.op("vector", lambda e, ps=ps, tt=tt: e.tensor_scalar(
                out=self.s_row[0:1, tt * 512:(tt + 1) * 512], in0=ps[0:1, 0:512],
                scalar1=1.0 / D, scalar2=None, op0=ALU.mult), reads=[pb], writes=[rS])
            ps, pb = qb[tt]
            p.op("vector", lambda e, ps=ps, tt=tt: e.tensor_scalar(
                out=self.qq_row[0:1, tt * 512:(tt + 1) * 512], in0=ps[0:1, 0:512],
                scalar1=1.0 / D, scalar2=None, op0=ALU.mult), reads=[pb], writes=[rQ])
        p.op("vector", lambda e: e.tensor_tensor(out=self.r_row[:], in0=self.s_row[:],
                                                 in1=self.s_row[:], op=ALU.mult),
             reads=[rS], writes=[rR])
        p.op("vector", lambda e: e.tensor_tensor(out=self.r_row[:], in0=self.qq_row[:],
                                                 in1=self.r_row[:], op=ALU.subtract),
             reads=[rQ, rR], writes=[rR])
        p.op("scalar", lambda e: e.activation(out=self.r_row[:], in_=self.r_row[:],
                                              func=AF.Ln, bias=EPS, scale=1.0),
             reads=[rR, self.B("c_cap")], writes=[rR])
        p.op("scalar", lambda e: e.activation(out=self.r_row[:], in_=self.r_row[:],
                                              func=AF.Exp, bias=0.0, scale=-0.5),
             reads=[rR, self.B("c_cap")], writes=[rR])
        p.op("vector", lambda e: e.tensor_tensor(out=self.rowscr[:], in0=self.s_row[:],
                                                 in1=self.r_row[:], op=ALU.mult),
             reads=[rS, rR], writes=[rW])
        self.bcast_row(self.r_row[:], rR, GTOK, self.r1b[:], self.B("r1b"))
        self.bcast_row(self.rowscr[:], rW, GTOK, self.r2b[:], self.B("r2b"))

    def ln_apply(self, src, src_buf, emit_dst):
        p = self.p
        for mc in range(KC):
            for tt in range(2):
                scr = getattr(self, "aplscr_%d" % tt)
                scb = self.B("aplscr_%d" % tt)
                p.op("vector", lambda e, sc=scr, mc=mc, tt=tt, s=src: e.tensor_tensor(
                    out=sc[:], in0=s[:, mc, tt * 512:(tt + 1) * 512],
                    in1=self.r1b[:, tt * 512:(tt + 1) * 512], op=ALU.mult),
                    reads=[src_buf, self.B("r1b")], writes=[scb])
                p.op("vector", lambda e, sc=scr, tt=tt: e.tensor_tensor(
                    out=sc[:], in0=sc[:], in1=self.r2b[:, tt * 512:(tt + 1) * 512],
                    op=ALU.subtract),
                    reads=[scb, self.B("r2b")], writes=[scb])
                emit_dst(mc, tt, scr, scb)


def _attention_merged(bd):
    p = bd.p
    xt_r = bd.xt.ap()
    BT = [0, L]

    wqhiB, wqloB = bd.B("wqhi"), bd.B("wqlo")
    p.op("scalar", lambda e: e.dma_start(out=bd.wqhi[:], in_=bd.s_wqhi.ap()),
         writes=[wqhiB], extra=bd.prep_tok["wqhi"], dma=True)
    for k in range(KC):
        j = k % 2
        c32, c32b = getattr(bd, "xc32_%d" % j), bd.B("xc32_%d" % j)
        h32, h32b = getattr(bd, "xh32_%d" % j), bd.B("xh32_%d" % j)
        c32f = c32[:].rearrange("q a b -> q (a b)")
        h32f = h32[:].rearrange("q a b -> q (a b)")
        p.op("scalar", lambda e, k=k, c32f=c32f: e.dma_start(
            out=c32f, in_=bd.wqt.ap()[:, k, :]),
            writes=[c32b], dma=True)
        p.op("vector", lambda e, k=k, h32f=h32f: e.tensor_copy(
            out=h32f, in_=bd.wqhi[:, k, :]),
            reads=[wqhiB], writes=[h32b])
        p.op("vector", lambda e, k=k, c32f=c32f, h32f=h32f: e.tensor_tensor(
            out=bd.wqlo[:, k, :], in0=c32f, in1=h32f, op=ALU.subtract),
            reads=[c32b, h32b], writes=[wqloB])

    brB = bd.B("browscr")
    p.op("scalar", lambda e: e.dma_start(out=bd.browscr[:], in_=bd.bq_r.ap()),
         writes=[brB], dma=True)
    bd.bcast_row(bd.browscr[:], brB, D, bd.bqv_b[:], bd.B("bqv_b"))

    QS = [bd.qs0, bd.qs0_1x]
    QSB = [bd.B("qs0"), bd.B("qs0_1x")]

    bd._ringw = 6
    pend_st = []
    for c in range(NCH):
        for b in range(2):
            j = (2 * c + b) % 4
            st = getattr(bd, "stat_%d" % j)
            stB = bd.B("stat_%d" % j)
            q2t, q2B = getattr(bd, "q2s_%d" % j), bd.B("q2s_%d" % j)
            est, escB = getattr(bd, "esc_%d" % j), bd.B("esc_%d" % j)
            xhi, xhib = getattr(bd, "xhic_%d" % j), bd.B("xhic_%d" % j)
            q32, q32b = getattr(bd, "q32_%d" % j), bd.B("q32_%d" % j)
            qev, qevb = getattr(bd, "qevac_%d" % j), bd.B("qevac_%d" % j)
            bt = BT[b]
            p.op("scalar", lambda e, c=c, bt=bt, xhi=xhi: e.dma_start(
                out=xhi[:], in_=bd.s_xhi.ap()[:, :, bt + c * 128: bt + (c + 1) * 128]),
                writes=[xhib], extra=bd.prep_tok["xhi"], dma=True)
            halves = [bd.ring(), bd.ring()]
            for hf in range(2):
                ps, pb = halves[hf]
                passes = [(bd.wqhi, wqhiB), (bd.wqlo, wqloB)]
                for pi, (wb, wbb) in enumerate(passes):
                    for k in range(KC):
                        fst = (pi == 0 and k == 0)
                        lst = (pi == 1 and k == KC - 1)
                        p.op("tensor", lambda e, ps=ps, xhi=xhi, wb=wb, k=k, hf=hf,
                             fst=fst, lst=lst: e.matmul(
                                 ps[0:128, 0:512], xhi[:, k, :],
                                 wb[:, k, hf * 512:(hf + 1) * 512], start=fst, stop=lst),
                             reads=[xhib, wbb], writes=[pb], inc=lst)
            for hf in range(2):
                ps, pb = halves[hf]
                p.op("vector", lambda e, ps=ps, hf=hf, q32=q32: e.tensor_tensor(
                    out=q32[:, hf * 512:(hf + 1) * 512], in0=ps[0:128, 0:512],
                    in1=bd.bqv_b[:, hf * 512:(hf + 1) * 512], op=ALU.add),
                    reads=[pb, bd.B("bqv_b")], writes=[q32b])
            p.op("gpsimd", lambda e, qev=qev, q32=q32: e.tensor_copy(
                out=qev[:], in_=q32[:]),
                reads=[q32b], writes=[qevb])
            p.op("scalar", lambda e, c=c, b=b, qev=qev: e.dma_start(
                out=bd.s_q[b].ap()[c], in_=qev[:]),
                reads=[qevb], writes=[bd.B("s_q%d" % b)], dma=True)

            def mk_stats(c=c, b=b, st=st, stB=stB, q2t=q2t, q2B=q2B, est=est,
                         escB=escB, q32=q32, q32b=q32b):
                def emit():
                    mq, sq, sqq, ent, l2t, var, qsc, tmp = (
                        st[:, i * 16:(i + 1) * 16] for i in range(8))
                    q3 = q32[:].rearrange("q (h dk) -> q h dk", dk=DK)
                    q23 = q2t[:].rearrange("q (h dk) -> q h dk", dk=DK)
                    e3 = est[:].rearrange("q (h dk) -> q h dk", dk=DK)
                    p.op("gpsimd", lambda e: e.tensor_tensor(
                        out=q2t[:], in0=q32[:], in1=q32[:], op=ALU.mult),
                        reads=[q32b], writes=[q2B])
                    p.op("vector", lambda e: e.reduce_sum(out=sq, in_=q3, axis=AX.X),
                         reads=[q32b], writes=[stB])
                    p.op("vector", lambda e: e.reduce_sum(out=sqq, in_=q23, axis=AX.X),
                         reads=[q2B], writes=[stB])
                    p.op("vector", lambda e: e.reduce_max(out=mq, in_=q3, axis=AX.X),
                         reads=[q32b], writes=[stB])
                    p.op("vector", lambda e: e.tensor_tensor(
                        out=e3, in0=q3, in1=mq.unsqueeze(2).to_broadcast([128, H, DK]),
                        op=ALU.subtract),
                        reads=[q32b, stB], writes=[escB])
                    p.op("scalar", lambda e: e.activation(
                        out=q2t[:], in_=est[:], func=AF.Exp, bias=0.0, scale=1.0),
                        reads=[escB], writes=[q2B])
                    p.op("vector", lambda e: e.reduce_sum(out=tmp, in_=q23, axis=AX.X),
                         reads=[q2B], writes=[stB])
                    p.op("vector", lambda e: e.reciprocal(out=tmp, in_=tmp),
                         reads=[stB], writes=[stB])
                    p.op("vector", lambda e: e.tensor_tensor(
                        out=q23, in0=q23, in1=tmp.unsqueeze(2).to_broadcast([128, H, DK]),
                        op=ALU.mult),
                        reads=[q2B, stB], writes=[q2B])
                    p.op("scalar", lambda e: e.activation(
                        out=est[:], in_=q2t[:], func=AF.Ln, bias=1e-9, scale=1.0),
                        reads=[q2B, bd.B("c_cap")], writes=[escB])
                    p.op("gpsimd", lambda e: e.tensor_tensor(
                        out=est[:], in0=q2t[:], in1=est[:], op=ALU.mult),
                        reads=[q2B, escB], writes=[escB])
                    p.op("vector", lambda e: e.reduce_sum(out=ent, in_=e3, axis=AX.X,
                                                          negate=True),
                         reads=[escB], writes=[stB])
                    p.op("scalar", lambda e: e.activation(
                        out=l2t, in_=sqq, func=AF.Ln, bias=0.0, scale=1.0),
                        reads=[stB], writes=[stB])
                    p.op("scalar", lambda e: e.activation(
                        out=l2t, in_=l2t, func=AF.Exp, bias=0.0, scale=0.5),
                        reads=[stB, bd.B("c_cap")], writes=[stB])
                    p.op("vector", lambda e: e.tensor_tensor(
                        out=var, in0=sq, in1=sq, op=ALU.mult), reads=[stB], writes=[stB])
                    p.op("vector", lambda e: e.tensor_scalar(
                        out=var, in0=var, scalar1=1.0 / DK, scalar2=None, op0=ALU.mult),
                        reads=[stB], writes=[stB])
                    p.op("vector", lambda e: e.tensor_tensor(
                        out=var, in0=sqq, in1=var, op=ALU.subtract),
                        reads=[stB], writes=[stB])
                    p.op("vector", lambda e: e.tensor_scalar(
                        out=var, in0=var, scalar1=1.0 / (DK - 1), scalar2=None,
                        op0=ALU.mult),
                        reads=[stB], writes=[stB])
                    p.op("vector", lambda e: e.tensor_scalar(
                        out=qsc, in0=l2t, scalar1=0.5, scalar2=None, op0=ALU.mult),
                        reads=[stB], writes=[stB])
                    p.op("vector", lambda e: e.scalar_tensor_tensor(
                        out=qsc, in0=ent, scalar=0.3, in1=qsc, op0=ALU.mult, op1=ALU.add),
                        reads=[stB], writes=[stB])
                    p.op("vector", lambda e: e.scalar_tensor_tensor(
                        out=qsc, in0=var, scalar=0.2, in1=qsc, op0=ALU.mult, op1=ALU.add),
                        reads=[stB], writes=[stB])
                    ps, pb = bd.ring()
                    p.op("tensor", lambda e, ps=ps: e.transpose(
                        ps[0:16, 0:128], qsc, bd.c_idnF[:]),
                        reads=[stB, bd.B("c_idnF")], writes=[pb])
                    p.op("vector", lambda e, ps=ps: e.tensor_copy(
                        out=QS[b][0:16, c * 128:(c + 1) * 128], in_=ps[0:16, 0:128]),
                        reads=[pb], writes=[QSB[b]])
                return emit
            pend_st.append(mk_stats())
            if len(pend_st) > 1:
                pend_st.pop(0)()
    for f_ in pend_st:
        f_()
    bd._ringw = 4

    IDXT = [bd.idxT, bd.idxT2]
    VALID = [bd.valid, bd.valid2]
    QSP = [bd.qspT, bd.qspT2]

    def a4(b):
        qs, qsB = QS[b], QSB[b]
        idxTb = bd.B("idxT%s" % ("" if b == 0 else "2"))
        validb = bd.B("valid%s" % ("" if b == 0 else "2"))
        iB = bd.B("idxf")
        p.op("vector", lambda e, qs=qs: e.max(out=bd.t8a[:], in_=qs[0:16, :]),
             reads=[qsB], writes=[bd.B("t8a")])
        p.op("vector", lambda e, qs=qs: e.max_index(out=bd.i8[:], in_max=bd.t8a[:],
                                                    in_values=qs[0:16, :]),
             reads=[qsB, bd.B("t8a")], writes=[bd.B("i8")])
        uB = bd.B("urow")
        xr0, xr0b = bd.xres_0, bd.B("xres_0")
        p.op("vector", lambda e, qs=qs: e.reduce_sum(out=bd.urow[0:1, 0:1],
                                                     in_=qs[0:1, :], axis=AX.X),
             reads=[qsB], writes=[uB])
        p.op("vector", lambda e, qs=qs, xr0=xr0: e.tensor_tensor(
            out=xr0[0:1, :], in0=qs[0:1, :], in1=qs[0:1, :], op=ALU.mult),
            reads=[qsB], writes=[xr0b])
        p.op("vector", lambda e, xr0=xr0: e.reduce_sum(out=bd.urow[0:1, 1:2],
                                                       in_=xr0[0:1, :], axis=AX.X),
             reads=[xr0b], writes=[uB])
        p.op("vector", lambda e: e.tensor_scalar(out=bd.urow[0:1, 2:3], in0=bd.urow[0:1, 0:1],
                                                 scalar1=1.0 / L, scalar2=None, op0=ALU.mult),
             reads=[uB], writes=[uB])
        p.op("vector", lambda e: e.tensor_tensor(out=bd.urow[0:1, 3:4], in0=bd.urow[0:1, 0:1],
                                                 in1=bd.urow[0:1, 2:3], op=ALU.mult),
             reads=[uB], writes=[uB])
        p.op("vector", lambda e: e.tensor_tensor(out=bd.urow[0:1, 3:4], in0=bd.urow[0:1, 1:2],
                                                 in1=bd.urow[0:1, 3:4], op=ALU.subtract),
             reads=[uB], writes=[uB])
        p.op("vector", lambda e: e.tensor_scalar(out=bd.urow[0:1, 3:4], in0=bd.urow[0:1, 3:4],
                                                 scalar1=1.0 / (L - 1), scalar2=None,
                                                 op0=ALU.mult),
             reads=[uB], writes=[uB])
        p.op("scalar", lambda e: e.activation(out=bd.urow[0:1, 4:5], in_=bd.urow[0:1, 3:4],
                                              func=AF.Ln, bias=0.0, scale=1.0),
             reads=[uB], writes=[uB])
        p.op("scalar", lambda e: e.activation(out=bd.urow[0:1, 4:5], in_=bd.urow[0:1, 4:5],
                                              func=AF.Exp, bias=0.0, scale=0.5),
             reads=[uB, bd.B("c_cap")], writes=[uB])
        p.op("vector", lambda e: e.tensor_scalar(out=bd.urow[0:1, 5:6], in0=bd.urow[0:1, 2:3],
                                                 scalar1=1e-6, scalar2=None, op0=ALU.add),
             reads=[uB], writes=[uB])
        p.op("vector", lambda e: e.reciprocal(out=bd.urow[0:1, 5:6], in_=bd.urow[0:1, 5:6]),
             reads=[uB], writes=[uB])
        p.op("vector", lambda e: e.tensor_tensor(out=bd.urow[0:1, 6:7], in0=bd.urow[0:1, 4:5],
                                                 in1=bd.urow[0:1, 5:6], op=ALU.mult),
             reads=[uB], writes=[uB])
        p.op("vector", lambda e: e.tensor_scalar(out=bd.thr[0:1, 0:1], in0=bd.urow[0:1, 6:7],
                                                 scalar1=10.0, scalar2=-0.5, op0=ALU.mult,
                                                 op1=ALU.add),
             reads=[uB], writes=[bd.B("thr")])
        ps, pb = bd.ring()
        p.op("tensor", lambda e, ps=ps: e.matmul(ps[0:128, 0:1], bd.c_ones_row[:],
                                                 bd.thr[0:1, 0:1], start=True, stop=True),
             reads=[bd.B("thr"), bd.B("c_ones_row")], writes=[pb])
        p.op("vector", lambda e, ps=ps: e.tensor_tensor(out=bd.vtmp[:], in0=ps[0:128, 0:1],
                                                        in1=bd.c_jmod[:], op=ALU.is_gt),
             reads=[pb, bd.B("c_jmod")], writes=[bd.B("vtmp")])
        p.op("vector", lambda e, b=b: e.tensor_scalar(out=VALID[b][:], in0=bd.c_jmod[:],
                                                      scalar1=2.5, scalar2=None,
                                                      op0=ALU.is_lt),
             reads=[bd.B("c_jmod")], writes=[validb])
        p.op("vector", lambda e, b=b: e.tensor_tensor(out=VALID[b][:], in0=VALID[b][:],
                                                      in1=bd.vtmp[:], op=ALU.max),
             reads=[bd.B("vtmp"), validb], writes=[validb])
        p.op("vector", lambda e, qs=qs: e.match_replace(out=qs[0:16, :], in_to_replace=bd.t8a[:],
                                                        in_values=qs[0:16, :],
                                                        imm_value=-1e30),
             reads=[bd.B("t8a")], writes=[qsB])
        p.op("vector", lambda e, qs=qs: e.max(out=bd.t8b[:], in_=qs[0:16, :]),
             reads=[qsB], writes=[bd.B("t8b")])
        p.op("vector", lambda e, qs=qs: e.max_index(out=bd.i2[:], in_max=bd.t8b[:],
                                                    in_values=qs[0:16, :]),
             reads=[qsB, bd.B("t8b")], writes=[bd.B("i2")])
        p.op("vector", lambda e: e.tensor_copy(out=bd.idxf[:, 0:8], in_=bd.i8[:]),
             reads=[bd.B("i8")], writes=[iB])
        p.op("vector", lambda e: e.tensor_copy(out=bd.idxf[:, 8:10], in_=bd.i2[:, 0:2]),
             reads=[bd.B("i2")], writes=[iB])
        ps, pb = bd.ring()
        p.op("tensor", lambda e, ps=ps: e.transpose(
            ps[0:10, 0:16], bd.idxf[:], bd.c_idnF[0:16, 0:16]),
            reads=[iB, bd.B("c_idnF")], writes=[pb])
        p.op("vector", lambda e, ps=ps, b=b: e.tensor_copy(
            out=IDXT[b][0:10, :], in_=ps[0:10, 0:16]),
            reads=[pb], writes=[idxTb])
        p.op("scalar", lambda e, b=b: e.dma_start(out=IDXT[b][64:74, :], in_=IDXT[b][0:10, :]),
             reads=[idxTb], writes=[idxTb], dma=True)
        p.op("scalar", lambda e: e.dma_start(out=bd.idxflat[:], in_=bd.idxf[:]),
             reads=[iB], writes=[bd.B("idxflat")], dma=True)
        ps, pb = bd.ring()
        p.op("tensor", lambda e, ps=ps: e.matmul(ps[0:128, 0:160], bd.c_ones_row[:],
                                                 bd.idxflat[:], start=True, stop=True),
             reads=[bd.B("idxflat"), bd.B("c_ones_row")], writes=[pb])
        p.op("vector", lambda e, ps=ps: e.tensor_copy(out=bd.idx_bc[:], in_=ps[0:128, 0:160]),
             reads=[pb], writes=[bd.B("idx_bc")])
        for c in range(NCH):
            p.op("vector", lambda e, c=c: e.tensor_scalar(
                out=bd.ohT[:, c, :], in0=bd.idx_bc[:], scalar1=bd.c_iotaPC[:, c:c + 1],
                scalar2=None, op0=ALU.is_equal),
                reads=[bd.B("idx_bc"), bd.B("c_iotaPC")], writes=[bd.B("ohT")])

    def a5(b):
        psGA, pbGA = bd.xbank()
        psGB, pbGB = bd.xbank2()
        qspb = bd.B("qspT%s" % ("" if b == 0 else "2"))
        for c in range(NCH):
            j = c % 2
            qcv, qcb = getattr(bd, "qc_%d" % j), bd.B("qc_%d" % j)
            p.op("sync", lambda e, c=c, b=b, qcv=qcv: e.dma_start(
                out=qcv[:], in_=bd.s_q[b].ap()[c]),
                reads=[bd.B("s_q%d" % b)], writes=[qcb], dma=True)
            for pr in range(NPAIR):
                for hh in range(2):
                    h = 2 * pr + hh
                    fst = (c == 0 and pr == 0)
                    lst = (c == NCH - 1 and pr == NPAIR - 1)
                    ctok = (pr == NPAIR - 1)
                    ps = psGA if hh == 0 else psGB
                    pb = pbGA if hh == 0 else pbGB
                    kw = {} if hh == 0 else {"tile_position": (0, 64)}
                    base = 0 if hh == 0 else 64
                    p.op("tensor", lambda e, ps=ps, c=c, qcv=qcv, pr=pr, h=h, base=base,
                         kw=kw, fst=fst, lst=lst: e.matmul(
                             ps[base:base + 64, pr * 10:(pr + 1) * 10],
                             qcv[:, h * 64:(h + 1) * 64],
                             bd.ohT[:, c, h * 10:(h + 1) * 10],
                             start=fst, stop=lst, **kw),
                         reads=[qcb, bd.B("ohT")], writes=[pb] if lst else (),
                         inc=(lst or ctok))
        p.op("vector", lambda e, ps=psGA, b=b: e.tensor_copy(
            out=QSP[b][0:64, :], in_=ps[0:64, 0:80]),
            reads=[pbGA], writes=[qspb])
        p.op("vector", lambda e, ps=psGB, b=b: e.tensor_copy(
            out=QSP[b][64:128, :], in_=ps[64:128, 0:80]),
            reads=[pbGB], writes=[qspb])

    a4(0)
    bd._tapon = True
    bd._tapsuf = ""
    bd.tap("idxf", bd.idxf[:], bd.B("idxf"))
    bd.tap("qs0", bd.qs0[0:16, :], QSB[0])
    bd.tap("idxT", bd.idxT[:], bd.B("idxT"))
    bd.tap("valid", bd.valid[:], bd.B("valid"))
    bd._tapon = False
    a5(0)
    a4(1)
    a5(1)

    bd._ringw = 6
    for tt in range(4):
        for b in range(2):
            xht, xhtb = getattr(bd, "xhit_%d" % b), bd.B("xhit_%d" % b)
            p.op("scalar", lambda e, tt=tt, b=b, xht=xht: e.dma_start(
                out=xht[:], in_=bd.s_xhi.ap()[:, :, BT[b] + tt * 512: BT[b] + (tt + 1) * 512]),
                writes=[xhtb], extra=bd.prep_tok["xhi"], dma=True)
        for m in range(KC):
            wj = (tt * KC + m) % 2
            ws, wsb = getattr(bd, "wstage_%d" % wj), bd.B("wstage_%d" % wj)
            p.op("sync", lambda e, m=m, ws=ws: e.dma_start(
                out=ws[:], in_=bd.s_wk.ap()[:, :, m * 128:(m + 1) * 128]),
                writes=[wsb], extra=bd.prep_tok["wk"], dma=True)
            for b in range(2):
                xht, xhtb = getattr(bd, "xhit_%d" % b), bd.B("xhit_%d" % b)
                ps, pb = bd.ring()
                for k in range(KC):
                    p.op("tensor", lambda e, ps=ps, k=k, ws=ws, xht=xht: e.matmul(
                        ps[0:128, 0:512], ws[:, k, :], xht[:, k, :],
                        start=(k == 0), stop=(k == KC - 1)),
                        reads=[wsb, xhtb], writes=[pb], inc=(k == KC - 1))
                kev, kevb = getattr(bd, "kevac_%d" % b), bd.B("kevac_%d" % b)
                p.op("vector", lambda e, ps=ps, m=m, kev=kev: e.tensor_scalar(
                    out=kev[:], in0=ps[0:128, 0:512], scalar1=bd.c_bk[:, m:m + 1],
                    scalar2=None, op0=ALU.add),
                    reads=[pb, bd.B("c_bk")], writes=[kevb])
                p.op("scalar", lambda e, m=m, tt=tt, b=b, kev=kev: e.dma_start(
                    out=bd.s_k[b].ap()[m][:, tt * 512:(tt + 1) * 512], in_=kev[:]),
                    reads=[kevb], writes=[bd.B("s_k%d" % b)], dma=True)
    p.op("scalar", lambda e: e.dma_start(out=bd.browscr[:], in_=bd.bv_r.ap()),
         writes=[brB], dma=True)
    bd.bcast_row(bd.browscr[:], brB, D, bd.bqv_b[:], bd.B("bqv_b"))
    p.op("scalar", lambda e: e.dma_start(out=bd.wvb[:], in_=bd.s_wv.ap()),
         writes=[bd.B("wvb")], extra=bd.prep_tok["wv"], dma=True)
    for c in range(NCH):
        for b in range(2):
            j = (2 * c + b) % 3
            xhi, xhib = getattr(bd, "xhic_%d" % j), bd.B("xhic_%d" % j)
            p.op("scalar", lambda e, c=c, b=b, xhi=xhi: e.dma_start(
                out=xhi[:], in_=bd.s_xhi.ap()[:, :, BT[b] + c * 128: BT[b] + (c + 1) * 128]),
                writes=[xhib], extra=bd.prep_tok["xhi"], dma=True)
            halves = [bd.ring(), bd.ring()]
            for hf in range(2):
                ps, pb = halves[hf]
                for k in range(KC):
                    p.op("tensor", lambda e, ps=ps, k=k, hf=hf, xhi=xhi: e.matmul(
                        ps[0:128, 0:512], xhi[:, k, :],
                        bd.wvb[:, k, hf * 512:(hf + 1) * 512],
                        start=(k == 0), stop=(k == KC - 1)),
                        reads=[xhib, bd.B("wvb")], writes=[pb], inc=(k == KC - 1))
            vev, vevb = getattr(bd, "vevac_%d" % j), bd.B("vevac_%d" % j)
            for hf in range(2):
                ps, pb = halves[hf]
                p.op("vector", lambda e, ps=ps, hf=hf, vev=vev: e.tensor_tensor(
                    out=vev[:, hf * 512:(hf + 1) * 512], in0=ps[0:128, 0:512],
                    in1=bd.bqv_b[:, hf * 512:(hf + 1) * 512], op=ALU.add),
                    reads=[pb, bd.B("bqv_b")], writes=[vevb])
            p.op("scalar", lambda e, c=c, b=b, vev=vev: e.dma_start(
                out=bd.s_v[b].ap()[c], in_=vev[:]),
                reads=[vevb], writes=[bd.B("s_v%d" % b)], dma=True)

    bd._ringw = 4
    PSB = [bd.p_sb, bd.p_sb2]
    PSBB = [bd.B("p_sb"), bd.B("p_sb2")]
    PRT = [bd.prT, bd.prT2]
    PRTB = [bd.B("prT"), bd.B("prT2")]
    KP = [bd.kpair, bd.kpair2]
    KPB = [bd.B("kpair"), bd.B("kpair2")]
    RV = [bd.rv_all, bd.rv_all2]
    RVB = [bd.B("rv_all"), bd.B("rv_all2")]
    for pr in range(NPAIR):
        for b in range(2):
            st = getattr(bd, "stat_%d" % b)
            stB = bd.B("stat_%d" % b)
            m4, mcol, mbias = st[:, 0:4], st[:, 4:5], st[:, 5:6]
            kp, kpb = KP[b], KPB[b]
            p.op("sync", lambda e, pr=pr, b=b, kp=kp: e.dma_start(
                out=kp[:], in_=bd.s_k[b].ap()[pr]),
                reads=[bd.B("s_k%d" % b)], writes=[kpb], dma=True)
            banks = [bd.ring() for _ in range(4)]
            for jj in range(4):
                ps, pb = banks[jj]
                for hh in range(2):
                    base = 0 if hh == 0 else 64
                    tp = (0, 0) if hh == 0 else (64, 64)
                    p.op("tensor", lambda e, ps=ps, jj=jj, pr=pr, b=b, base=base, tp=tp,
                         kp=kp: e.matmul(
                             ps[base:base + 10, 0:512],
                             QSP[b][base:base + 64, pr * 10:(pr + 1) * 10],
                             kp[base:base + 64, jj * 512:(jj + 1) * 512],
                             start=True, stop=True, tile_position=tp),
                         reads=[bd.B("qspT%s" % ("" if b == 0 else "2")), kpb],
                         writes=[pb])
            for jj in range(4):
                ps, pb = banks[jj]
                p.op("vector", lambda e, ps=ps, jj=jj, m4=m4: e.reduce_max(
                    out=m4[:, jj:jj + 1], in_=ps[0:128, 0:512], axis=AX.X),
                    reads=[pb], writes=[stB])
            p.op("vector", lambda e, m4=m4, mcol=mcol: e.reduce_max(
                out=mcol, in_=m4, axis=AX.X), reads=[stB], writes=[stB])
            p.op("vector", lambda e, mcol=mcol, mbias=mbias: e.tensor_scalar(
                out=mbias, in0=mcol, scalar1=-0.125, scalar2=None, op0=ALU.mult),
                reads=[stB], writes=[stB])
            for jj in range(4):
                ps, pb = banks[jj]
                p.op("scalar", lambda e, ps=ps, jj=jj, b=b, mbias=mbias, m4=m4: e.activation(
                    out=PSB[b][:, jj * 512:(jj + 1) * 512], in_=ps[0:128, 0:512],
                    func=AF.Exp, bias=mbias, scale=0.125, accum_out=m4[:, jj:jj + 1]),
                    reads=[pb, stB, bd.B("c_cap")], writes=[PSBB[b], stB])
            p.op("vector", lambda e, m4=m4, mcol=mcol: e.reduce_sum(
                out=mcol, in_=m4, axis=AX.X), reads=[stB], writes=[stB])
            p.op("vector", lambda e, mcol=mcol: e.reciprocal(out=mcol, in_=mcol),
                 reads=[stB], writes=[stB])
            p.op("vector", lambda e, pr=pr, b=b, mcol=mcol: e.tensor_tensor(
                out=RV[b][:, pr:pr + 1], in0=mcol, in1=VALID[b][:], op=ALU.mult),
                reads=[stB, bd.B("valid%s" % ("" if b == 0 else "2"))], writes=[RVB[b]])
            for c in range(NCH):
                pt, ptb = bd.tring()
                p.op("tensor", lambda e, pt=pt, c=c, b=b: e.transpose(
                    pt[0:128, 0:128], PSB[b][:, c * 128:(c + 1) * 128], bd.c_idn16[:]),
                    reads=[PSBB[b], bd.B("c_idn16")], writes=[ptb])
                p.op("vector", lambda e, pt=pt, c=c, pr=pr, b=b: e.tensor_copy(
                    out=PRT[b][:, c, pr * 20:pr * 20 + 10], in_=pt[0:128, 0:10]),
                    reads=[ptb], writes=[PRTB[b]])
                p.op("vector", lambda e, pt=pt, c=c, pr=pr, b=b: e.tensor_copy(
                    out=PRT[b][:, c, pr * 20 + 10:pr * 20 + 20], in_=pt[0:128, 64:74]),
                    reads=[ptb], writes=[PRTB[b]])

    def a78(b):
        psOA, pbOA = bd.xbank()
        psOB, pbOB = bd.xbank2()
        for c in range(NCH):
            j = c % 2
            vcv, vcb = getattr(bd, "vc_%d" % j), bd.B("vc_%d" % j)
            p.op("sync", lambda e, c=c, b=b, vcv=vcv: e.dma_start(
                out=vcv[:], in_=bd.s_v[b].ap()[c]),
                reads=[bd.B("s_v%d" % b)], writes=[vcb], dma=True)
            for pr in range(NPAIR):
                for hh in range(2):
                    fst = (c == 0 and pr == 0)
                    lst = (c == NCH - 1 and pr == NPAIR - 1)
                    ctok = (pr == NPAIR - 1)
                    ps = psOA if hh == 0 else psOB
                    pb = pbOA if hh == 0 else pbOB
                    base = 0 if hh == 0 else 64
                    kw = {} if hh == 0 else {"tile_position": (0, 64)}
                    p.op("tensor", lambda e, ps=ps, c=c, vcv=vcv, pr=pr, hh=hh, b=b,
                         base=base, kw=kw, fst=fst, lst=lst: e.matmul(
                             ps[base:base + 10, pr * 64:(pr + 1) * 64],
                             PRT[b][:, c, pr * 20 + hh * 10: pr * 20 + (hh + 1) * 10],
                             vcv[:, pr * 128 + hh * 64: pr * 128 + (hh + 1) * 64],
                             start=fst, stop=lst, **kw),
                         reads=[PRTB[b], vcb], writes=[pb] if lst else (),
                         inc=(lst or ctok))
        fB = bd.B("full")
        for pr in range(NPAIR):
            sj = pr % 2
            sc, sb = getattr(bd, "scat_%d" % sj), bd.B("scat_%d" % sj)
            p.op("vector", lambda e, sc=sc: e.memset(sc[:], 0.0), writes=[sb])
            p.op("vector", lambda e, ps=psOA, pr=pr, sc=sc, b=b: e.tensor_scalar(
                out=sc[0:10, 0:64], in0=ps[0:10, pr * 64:(pr + 1) * 64],
                scalar1=RV[b][0:10, pr:pr + 1], scalar2=None, op0=ALU.mult),
                reads=[pbOA, RVB[b]], writes=[sb])
            p.op("vector", lambda e, ps=psOB, pr=pr, sc=sc, b=b: e.tensor_scalar(
                out=sc[64:74, 64:128], in0=ps[64:74, pr * 64:(pr + 1) * 64],
                scalar1=RV[b][64:74, pr:pr + 1], scalar2=None, op0=ALU.mult),
                reads=[pbOB, RVB[b]], writes=[sb])
            ohB = bd.B("oh2")
            p.op("vector", lambda e, pr=pr, b=b: e.tensor_scalar(
                out=bd.oh2[0:10, :], in0=bd.c_iotaF[0:10, :],
                scalar1=IDXT[b][0:10, 2 * pr:2 * pr + 1], scalar2=None, op0=ALU.is_equal),
                reads=[bd.B("c_iotaF"), bd.B("idxT%s" % ("" if b == 0 else "2"))],
                writes=[ohB])
            p.op("vector", lambda e, pr=pr, b=b: e.tensor_scalar(
                out=bd.oh2[64:74, :], in0=bd.c_iotaF[64:74, :],
                scalar1=IDXT[b][64:74, 2 * pr + 1:2 * pr + 2], scalar2=None,
                op0=ALU.is_equal),
                reads=[bd.B("c_iotaF"), bd.B("idxT%s" % ("" if b == 0 else "2"))],
                writes=[ohB])
            for jj in range(4):
                ps, pb = bd.ring()
                p.op("tensor", lambda e, ps=ps, jj=jj, sc=sc: e.matmul(
                    ps[0:128, 0:512], sc[:], bd.oh2[:, jj * 512:(jj + 1) * 512],
                    start=True, stop=True),
                    reads=[sb, ohB], writes=[pb])
                p.op("vector", lambda e, ps=ps, jj=jj, pr=pr: e.tensor_copy(
                    out=bd.full[:, pr, jj * 512:(jj + 1) * 512], in_=ps[0:128, 0:512]),
                    reads=[pb], writes=[fB])
        if b == 0:
            bd._tapon = True
            bd.tap("prT", PRT[0][:], PRTB[0])
            bd._tapon = False
        if b == 0 and "psO" in TAPS:
            bd._tapon = True
            p.op("vector", lambda e, ps=psOA: e.tensor_copy(out=bd.q2s_0[0:10, 0:512],
                                                            in_=ps[0:10, 0:512]),
                 reads=[pbOA], writes=[bd.B("q2s_0")])
            p.op("vector", lambda e, ps=psOB: e.tensor_copy(out=bd.q2s_0[64:74, 0:512],
                                                            in_=ps[64:74, 0:512]),
                 reads=[pbOB], writes=[bd.B("q2s_0")])
            bd.tap("psO", bd.q2s_0[:, 0:512], bd.B("q2s_0"))
            bd._tapon = False
        if b == 0:
            bd._tapon = True
            bd.tap("full", bd.full[:], fB)
            bd.tap("rv0", bd.rv_all[:, 0:1], bd.B("rv_all"))
            bd._tapon = False
        spB = bd.B("s_sp")
        for mc in range(KC):
            j = mc % 2
            xrv, xrb = getattr(bd, "xres_%d" % j), bd.B("xres_%d" % j)
            p.op("scalar", lambda e, mc=mc, b=b, xrv=xrv: e.dma_start(
                out=xrv[:], in_=xt_r[:, mc, BT[b]:BT[b] + L]),
                writes=[xrb], dma=True)
            ws, wsb = getattr(bd, "wstage_%d" % j), bd.B("wstage_%d" % j)
            p.op("sync", lambda e, mc=mc, ws=ws: e.dma_start(
                out=ws[:], in_=bd.s_wo.ap()[:, :, mc * 128:(mc + 1) * 128]),
                writes=[wsb], extra=bd.prep_tok["wo"], dma=True)
            for tt in range(4):
                ps, pb = bd.ring()
                for k in range(KC):
                    p.op("tensor", lambda e, ps=ps, k=k, tt=tt, ws=ws: e.matmul(
                        ps[0:128, 0:512], ws[:, k, :],
                        bd.full[:, k, tt * 512:(tt + 1) * 512],
                        start=(k == 0), stop=(k == KC - 1)),
                        reads=[wsb, fB], writes=[pb], inc=(k == KC - 1))
                si = 2 * b + tt // 2
                if si == 0:
                    p.op("vector", lambda e, ps=ps, tt=tt, mc=mc, xrv=xrv:
                         e.scalar_tensor_tensor(
                             out=bd.strmG[:, mc, tt * 512:(tt + 1) * 512],
                             in0=ps[0:128, 0:512], scalar=bd.c_bo[:, mc:mc + 1],
                             in1=xrv[:, tt * 512:(tt + 1) * 512], op0=ALU.add, op1=ALU.add),
                         reads=[pb, xrb, bd.B("c_bo")], writes=[bd.B("strmG")])
                    continue
                sj = tt % 2
                ssv, ssb = getattr(bd, "spillscr_%d" % sj), bd.B("spillscr_%d" % sj)
                p.op("vector", lambda e, ps=ps, tt=tt, mc=mc, xrv=xrv, ssv=ssv:
                     e.scalar_tensor_tensor(
                         out=ssv[:], in0=ps[0:128, 0:512], scalar=bd.c_bo[:, mc:mc + 1],
                         in1=xrv[:, tt * 512:(tt + 1) * 512], op0=ALU.add, op1=ALU.add),
                     reads=[pb, xrb, bd.B("c_bo")], writes=[ssb])
                p.op("scalar", lambda e, mc=mc, si=si, tt=tt, ssv=ssv: e.dma_start(
                    out=bd.s_sp.ap()[si][:, mc, (tt % 2) * 512:(tt % 2 + 1) * 512],
                    in_=ssv[:]),
                    reads=[ssb], writes=[spB], dma=True)

    a78(0)
    a78(1)


def _ffn_group(bd, si):
    p = bd.p
    par = si % 2
    strm = bd.strmG if par == 0 else bd.strmB
    sgB = bd.B("strmG" if par == 0 else "strmB")
    hbf = bd.hbuf if par == 0 else bd.hbufB
    hbB = bd.B("hbuf" if par == 0 else "hbufB")
    abf = bd.abuf if par == 0 else bd.abufB
    abB = bd.B("abuf" if par == 0 else "abufB")
    bt = si * GTOK
    if si > 0:
        p.op("scalar", lambda e, si=si, strm=strm: e.dma_start(
            out=strm[:], in_=bd.s_sp.ap()[si]),
            reads=[bd.B("s_sp")], writes=[sgB], dma=True)

    bd.ln_rows(strm[:], sgB)

    bd.tap("strm_a8", strm[:], sgB)

    def to_strm(mc, tt, scr, scb):
        p.op("vector", lambda e, sc=scr, mc=mc, tt=tt: e.tensor_copy(
            out=strm[:, mc, tt * 512:(tt + 1) * 512], in_=sc[:]),
            reads=[scb], writes=[sgB])
    bd.ln_apply(strm[:], sgB, to_strm)
    bd.tap("y1", strm[:], sgB)

    rR, rW = bd.B("r_row"), bd.B("rowscr")
    p.op("vector", lambda e: e.tensor_tensor(out=bd.rowscr[:], in0=bd.r_row[:],
                                             in1=bd.r_row[:], op=ALU.mult),
         reads=[rR], writes=[rW])
    p.op("vector", lambda e: e.tensor_scalar(out=bd.rowscr[:], in0=bd.rowscr[:],
                                             scalar1=-EPS, scalar2=1.0 + EPS,
                                             op0=ALU.mult, op1=ALU.add),
         reads=[rW], writes=[rW])
    p.op("scalar", lambda e: e.activation(out=bd.rowscr[:], in_=bd.rowscr[:],
                                          func=AF.Ln, bias=0.0, scale=1.0),
         reads=[rW], writes=[rW])
    p.op("scalar", lambda e: e.activation(out=bd.rowscr[:], in_=bd.rowscr[:],
                                          func=AF.Exp, bias=0.0, scale=-0.5),
         reads=[rW, bd.B("c_cap")], writes=[rW])
    bd.bcast_row(bd.rowscr[:], rW, GTOK, bd.r1b[:], bd.B("r1b"))
    p.op("vector", lambda e: e.memset(bd.r2b[:], 0.0), writes=[bd.B("r2b")])
    zB = bd.B("zbuf")

    def to_z(mc, tt, scr, scb):
        p.op("gpsimd", lambda e, sc=scr, mc=mc, tt=tt: e.tensor_scalar(
            out=bd.zbuf[:, mc, tt * 512:(tt + 1) * 512], in0=sc[:],
            scalar1=bd.c_l1g[:, mc:mc + 1], scalar2=bd.c_l1b[:, mc:mc + 1],
            op0=ALU.mult, op1=ALU.add),
            reads=[scb, bd.B("c_l1g"), bd.B("c_l1b")], writes=[zB])
    bd.ln_apply(strm[:], sgB, to_z)

    for mc in range(KC):
        wj = mc % 2
        ws, wsb = getattr(bd, "wstage_%d" % wj), bd.B("wstage_%d" % wj)
        p.op("sync", lambda e, mc=mc, ws=ws: e.dma_start(
            out=ws[:], in_=bd.s_awt.ap()[:, :, mc * 128:(mc + 1) * 128]),
            writes=[wsb], extra=bd.prep_tok["awt"], dma=True)
        for tt in range(2):
            ps, pb = bd.ring()
            for k in range(KC):
                p.op("tensor", lambda e, ps=ps, k=k, tt=tt, ws=ws: e.matmul(
                    ps[0:128, 0:512], ws[:, k, :], bd.zbuf[:, k, tt * 512:(tt + 1) * 512],
                    start=(k == 0), stop=(k == KC - 1)),
                    reads=[wsb, zB], writes=[pb], inc=(k == KC - 1))
            p.op("vector", lambda e, ps=ps, mc=mc, tt=tt: e.tensor_tensor(
                out=hbf[:, mc, tt * 512:(tt + 1) * 512],
                in0=strm[:, mc, tt * 512:(tt + 1) * 512], in1=ps[0:128, 0:512],
                op=ALU.add),
                reads=[pb, sgB], writes=[hbB])

    bd.ln_rows(hbf[:], hbB)
    tB = bd.B("tbuf")
    lgp = [bd.ring(), bd.ring()]
    lgB = bd.B("lgsb")

    def to_t(mc, tt, scr, scb):
        t32, t32b = getattr(bd, "t32scr_%d" % tt), bd.B("t32scr_%d" % tt)
        tlo, tlob = getattr(bd, "tloscr_%d" % tt), bd.B("tloscr_%d" % tt)
        sgs, sgsb = getattr(bd, "sgscr_%d" % tt), bd.B("sgscr_%d" % tt)
        p.op("vector", lambda e, sc=scr, mc=mc, t32=t32: e.tensor_scalar(
            out=t32[:], in0=sc[:], scalar1=bd.c_l2g[:, mc:mc + 1],
            scalar2=bd.c_l2b[:, mc:mc + 1], op0=ALU.mult, op1=ALU.add),
            reads=[scb, bd.B("c_l2g"), bd.B("c_l2b")], writes=[t32b])
        p.op("vector", lambda e, mc=mc, tt=tt, t32=t32: e.tensor_copy(
            out=bd.tbuf[:, mc, tt * 512:(tt + 1) * 512], in_=t32[:]),
            reads=[t32b], writes=[tB])
        p.op("gpsimd", lambda e, mc=mc, tt=tt, sgs=sgs: e.tensor_copy(
            out=sgs[:], in_=bd.tbuf[:, mc, tt * 512:(tt + 1) * 512]),
            reads=[tB], writes=[sgsb])
        p.op("vector", lambda e, t32=t32, tlo=tlo, sgs=sgs: e.tensor_tensor(
            out=tlo[:], in0=t32[:], in1=sgs[:], op=ALU.subtract),
            reads=[t32b, sgsb], writes=[tlob])
        ps, pb = lgp[tt]
        passes = [(bd.c_gwhi, bd.B("c_gwhi"), None, tB),
                  (bd.c_gwhi, bd.B("c_gwhi"), tlo, tlob),
                  (bd.c_gwlo, bd.B("c_gwlo"), None, tB)]
        for pi, (wv, wvB, rhs, rhsB) in enumerate(passes):
            fst = (mc == 0 and pi == 0)
            lst = (mc == KC - 1 and pi == 2)
            p.op("tensor", lambda e, ps=ps, wv=wv, mc=mc, tt=tt, rhs=rhs,
                 fst=fst, lst=lst: e.matmul(
                     ps[0:8, 0:512], wv[:, mc * 8:(mc + 1) * 8],
                     bd.tbuf[:, mc, tt * 512:(tt + 1) * 512] if rhs is None else rhs[:],
                     start=fst, stop=lst),
                 reads=[wvB, rhsB], writes=[pb] if lst else (), inc=True)
    bd.ln_apply(hbf[:], hbB, to_t)
    for tt in range(2):
        ps, pb = lgp[tt]
        p.op("vector", lambda e, ps=ps, tt=tt: e.tensor_copy(
            out=bd.lgsb[0:8, tt * 512:(tt + 1) * 512], in_=ps[0:8, 0:512]),
            reads=[pb], writes=[lgB])

    bd.tap("hbuf", hbf[:], hbB)
    bd.tap("tbuf", bd.tbuf[:], tB)
    bd.tap("lgsb", bd.lgsb[:], lgB)
    for mc in range(KC):
        p.op("vector", lambda e, mc=mc: e.tensor_tensor(
            out=strm[:, mc, :], in0=strm[:, mc, :], in1=hbf[:, mc, :],
            op=ALU.add),
            reads=[hbB, sgB], writes=[sgB])

    cfB, ctB, ltB, gcB = bd.B("combfm"), bd.B("combtm"), bd.B("lgtm"), bd.B("gcols")
    for c in range(KC):
        ps, pb = bd.ring()
        p.op("tensor", lambda e, ps=ps, c=c: e.transpose(
            ps[0:128, 0:8], bd.lgsb[0:8, c * 128:(c + 1) * 128], bd.c_idnF[0:8, 0:8]),
            reads=[lgB, bd.B("c_idnF")], writes=[pb])
        p.op("vector", lambda e, ps=ps, c=c: e.tensor_copy(
            out=bd.lgtm[:, c * 8:(c + 1) * 8], in_=ps[0:128, 0:8]),
            reads=[pb], writes=[ltB])
    mx = bd.gcols[:, 0:8]
    iu = bd.gcols[:, 8:16].bitcast(U32)
    e12 = bd.gcols[:, 16:18]
    w1c = bd.gcols[:, 18:19]
    w2c = bd.gcols[:, 19:20]
    dc_ = bd.gcols[:, 20:21]
    for c in range(KC):
        p.op("vector", lambda e, c=c: e.max(out=mx, in_=bd.lgtm[:, c * 8:(c + 1) * 8]),
             reads=[ltB], writes=[gcB])
        p.op("vector", lambda e, c=c: e.max_index(out=iu, in_max=mx,
                                                  in_values=bd.lgtm[:, c * 8:(c + 1) * 8]),
             reads=[ltB, gcB], writes=[gcB])
        p.op("vector", lambda e: e.tensor_copy(out=e12, in_=iu[:, 0:2]),
             reads=[gcB], writes=[gcB])
        p.op("vector", lambda e: e.tensor_tensor(out=dc_, in0=mx[:, 1:2], in1=mx[:, 0:1],
                                                 op=ALU.subtract),
             reads=[gcB], writes=[gcB])
        p.op("scalar", lambda e: e.activation(out=dc_, in_=dc_, func=AF.Exp, bias=0.0,
                                              scale=1.0),
             reads=[gcB], writes=[gcB])
        p.op("vector", lambda e: e.tensor_scalar(out=dc_, in0=dc_, scalar1=1.0, scalar2=None,
                                                 op0=ALU.add),
             reads=[gcB], writes=[gcB])
        p.op("vector", lambda e: e.reciprocal(out=w1c, in_=dc_),
             reads=[gcB], writes=[gcB])
        p.op("vector", lambda e: e.tensor_scalar(out=w2c, in0=w1c, scalar1=-1.0, scalar2=1.0,
                                                 op0=ALU.mult, op1=ALU.add),
             reads=[gcB], writes=[gcB])
        p.op("vector", lambda e, c=c: e.tensor_scalar(
            out=bd.lgtm[:, c * 8:(c + 1) * 8], in0=bd.c_iotaF[:, 0:8],
            scalar1=e12[:, 0:1], scalar2=w1c, op0=ALU.is_equal, op1=ALU.mult),
            reads=[bd.B("c_iotaF"), gcB], writes=[ltB])
        p.op("vector", lambda e, c=c: e.tensor_scalar(
            out=bd.combtm[:, c * 8:(c + 1) * 8], in0=bd.c_iotaF[:, 0:8],
            scalar1=e12[:, 1:2], scalar2=w2c, op0=ALU.is_equal, op1=ALU.mult),
            reads=[bd.B("c_iotaF"), gcB], writes=[ctB])
        p.op("vector", lambda e, c=c: e.tensor_tensor(
            out=bd.combtm[:, c * 8:(c + 1) * 8], in0=bd.combtm[:, c * 8:(c + 1) * 8],
            in1=bd.lgtm[:, c * 8:(c + 1) * 8], op=ALU.add),
            reads=[ctB, ltB], writes=[ctB])
    for c in range(KC):
        ps, pb = bd.ring()
        p.op("tensor", lambda e, ps=ps, c=c: e.transpose(
            ps[0:8, 0:128], bd.combtm[:, c * 8:(c + 1) * 8], bd.c_idnF[:]),
            reads=[ctB, bd.B("c_idnF")], writes=[pb])
        p.op("vector", lambda e, ps=ps, c=c: e.tensor_copy(
            out=bd.combfm[0:8, c * 128:(c + 1) * 128], in_=ps[0:8, 0:128]),
            reads=[pb], writes=[cfB])

    bd.tap("combfm", bd.combfm[:], cfB)
    cbB = bd.B("combbc")
    for e_ in range(NEXP):
        for tt in range(2):
            ps, pb = bd.ring()
            p.op("tensor", lambda e, ps=ps, e_=e_, tt=tt: e.matmul(
                ps[0:128, 0:512], bd.c_basis[:, e_ * 128:(e_ + 1) * 128],
                bd.combfm[0:8, tt * 512:(tt + 1) * 512], start=True, stop=True),
                reads=[cfB, bd.B("c_basis")], writes=[pb])
            p.op("vector", lambda e, ps=ps, tt=tt: e.tensor_copy(
                out=bd.combbc[:, tt * 512:(tt + 1) * 512], in_=ps[0:128, 0:512]),
                reads=[pb], writes=[cbB])
        NH = ICH // 2
        for half in range(2):
            for il in range(NH):
                ic = half * NH + il
                sj = il % 3
                gst, gsb = getattr(bd, "gstage_%d" % sj), bd.B("gstage_%d" % sj)
                ust, usb = getattr(bd, "ustage_%d" % sj), bd.B("ustage_%d" % sj)
                p.op("gpsimd", lambda e, e_=e_, ic=ic, gst=gst: e.dma_start(
                    out=gst[:], in_=bd.gpt.ap()[e_, ic]),
                    writes=[gsb], dma=True)
                p.op("gpsimd", lambda e, e_=e_, ic=ic, ust=ust: e.dma_start(
                    out=ust[:], in_=bd.upt.ap()[e_, ic]),
                    writes=[usb], dma=True)
                gb = [bd.ring(), bd.ring()]
                ub = [bd.ring(), bd.ring()]
                for (stage, stb, banks) in ((gst, gsb, gb), (ust, usb, ub)):
                    for k in range(KC):
                        for tt in range(2):
                            ps, pb = banks[tt]
                            p.op("tensor", lambda e, ps=ps, stg=stage, k=k, tt=tt: e.matmul(
                                ps[0:128, 0:512], stg[:, k, :],
                                bd.tbuf[:, k, tt * 512:(tt + 1) * 512],
                                start=(k == 0), stop=(k == KC - 1)),
                                reads=[stb, tB], writes=[pb], inc=(k == KC - 1))
                for tt in range(2):
                    psg, pbg = gb[tt]
                    psu, pbu = ub[tt]
                    sgs, sgsb = getattr(bd, "sgscr_%d" % tt), bd.B("sgscr_%d" % tt)
                    p.op("scalar", lambda e, ps=psg, sgs=sgs: e.activation(
                        out=sgs[:], in_=ps[0:128, 0:512], func=AF.Silu, bias=0.0, scale=1.0),
                        reads=[pbg], writes=[sgsb])
                    p.op("vector", lambda e, ps=psu, sgs=sgs, il=il, tt=tt: e.tensor_tensor(
                        out=abf[:, il, tt * 512:(tt + 1) * 512], in0=sgs[:],
                        in1=ps[0:128, 0:512], op=ALU.mult),
                        reads=[sgsb, pbu], writes=[abB])
            for dcc in range(KC):
                sj = dcc % 3
                dst, dsb = getattr(bd, "dstage_%d" % sj), bd.B("dstage_%d" % sj)
                p.op("gpsimd", lambda e, e_=e_, dcc=dcc, dst=dst, half=half: e.dma_start(
                    out=dst[:], in_=bd.dpt.ap()[e_, dcc][:, half * NH:(half + 1) * NH, :]),
                    writes=[dsb], dma=True)
                banks = [bd.ring(), bd.ring()]
                for il in range(NH):
                    for tt in range(2):
                        ps, pb = banks[tt]
                        p.op("tensor", lambda e, ps=ps, dst=dst, il=il, tt=tt: e.matmul(
                            ps[0:128, 0:512], dst[:, il, :],
                            abf[:, il, tt * 512:(tt + 1) * 512],
                            start=(il == 0), stop=(il == NH - 1)),
                            reads=[dsb, abB], writes=[pb], inc=(il == NH - 1))
                for tt in range(2):
                    ps, pb = banks[tt]
                    hs, hsb = getattr(bd, "haccscr_%d" % tt), bd.B("haccscr_%d" % tt)
                    p.op("vector", lambda e, ps=ps, hs=hs, tt=tt: e.tensor_tensor(
                        out=hs[:], in0=ps[0:128, 0:512],
                        in1=bd.combbc[:, tt * 512:(tt + 1) * 512], op=ALU.mult),
                        reads=[pb, cbB], writes=[hsb])
                    p.op("vector", lambda e, hs=hs, dcc=dcc, tt=tt: e.tensor_tensor(
                        out=strm[:, dcc, tt * 512:(tt + 1) * 512],
                        in0=strm[:, dcc, tt * 512:(tt + 1) * 512], in1=hs[:], op=ALU.add),
                        reads=[hsb], writes=[sgB])

    bd.tap("strm_pre", strm[:], sgB)
    bd.ln_rows(strm[:], sgB)

    out_r = bd.out.ap().rearrange("(kc q) t -> q kc t", q=128)

    def to_out(mc, tt, scr, scb):
        p.op("vector", lambda e, sc=scr, mc=mc, tt=tt: e.tensor_copy(
            out=strm[:, mc, tt * 512:(tt + 1) * 512], in_=sc[:]),
            reads=[scb], writes=[sgB])
        if tt == 1:
            p.op("sync", lambda e, mc=mc: e.dma_start(
                out=out_r[:, mc, bt:bt + GTOK], in_=strm[:, mc, :]),
                reads=[sgB], writes=[bd.B("out")], dma=True)
    bd.ln_apply(strm[:], sgB, to_out)


def build_program():
    bd = Build()
    p = bd.p
    bd.build_consts()
    bd.build_prep()
    bd._tapon = False
    _attention_merged(bd)
    for si in range(4):
        bd._tapon = (si < 2)
        bd._tapsuf = "" if si == 0 else "_g1"
        _ffn_group(bd, si)
    alldma = [(s, p.cnt[s]) for q in DSEM for s in DSEM[q] if p.cnt[s]]
    p.items["sync"].append((lambda e: e.nop(), alldma, None, 0))
    for s in SEMS:
        assert p.cnt[s] < 60000, (s, p.cnt[s])
    nc = bd.nc
    import contextlib
    with contextlib.ExitStack() as stk:
        sems = {s: stk.enter_context(nc.semaphore(s)) for s in SEMS}
        block = stk.enter_context(nc.Block())
        p.install(block, sems)
    return bd


def kernel(**inputs):
    f = lambda n: np.asarray(inputs[n], np.float32)
    x = f("L0_attn_x")
    pc = lambda v: np.ascontiguousarray(v.reshape(KC, 128).T)
    row = lambda v: np.ascontiguousarray(v.reshape(1, D))
    def wfm(w):
        return np.ascontiguousarray(w.T.reshape(KC, 128, w.shape[0]).transpose(1, 0, 2))

    gp, up, dp = f("L0_ffn_gate_proj"), f("L0_ffn_up_proj"), f("L0_ffn_down_proj")
    com = {
        "wqt": wfm(f("L0_attn_Wq")), "wkt": wfm(f("L0_attn_Wk")),
        "wvt": wfm(f("L0_attn_Wv")), "wot": wfm(f("L0_attn_Wo")),
        "awt": wfm(f("L0_ffn_attn_w")),
        "gwt": np.ascontiguousarray(
            f("L0_ffn_gate_w").T.reshape(KC, 128, NEXP).transpose(1, 0, 2)),
        "gpt": np.ascontiguousarray(
            gp.reshape(NEXP, ICH, 128, KC, 128).transpose(0, 1, 4, 3, 2)),
        "upt": np.ascontiguousarray(
            up.reshape(NEXP, ICH, 128, KC, 128).transpose(0, 1, 4, 3, 2)),
        "dpt": np.ascontiguousarray(
            dp.reshape(NEXP, KC, 128, ICH, 128).transpose(0, 1, 4, 3, 2)),
        "bq_r": row(f("L0_attn_bq")), "bv_r": row(f("L0_attn_bv")),
        "bk_pc": pc(f("L0_attn_bk")), "bo_pc": pc(f("L0_attn_bo")),
        "l1g": pc(f("L0_ffn_ln1_g")), "l1b": pc(f("L0_ffn_ln1_b")),
        "l2g": pc(f("L0_ffn_ln2_g")), "l2b": pc(f("L0_ffn_ln2_b")),
    }
    in_maps = []
    for c in range(N_CORES):
        m = dict(com)
        xc = np.concatenate([x[2 * c].T, x[2 * c + 1].T], axis=1)
        m["xt"] = np.ascontiguousarray(xc.reshape(KC, 128, TOKS).transpose(1, 0, 2))
        in_maps.append(m)
    bd = build_program()
    res = run_bass_kernel_spmd(bd.nc, in_maps, core_ids=list(range(N_CORES)))
    outs = []
    for c in range(N_CORES):
        o = np.asarray(res.results[c]["out"])
        outs.append(o[:, :L].T)
        outs.append(o[:, L:].T)
    return np.ascontiguousarray(np.stack(outs)).astype(np.float32)


# version 19
# speedup vs baseline: 1.0367x; 1.0367x over previous
import numpy as np
import concourse.bass as bass
import concourse.mybir as mybir
from concourse.bass_utils import run_bass_kernel_spmd

F32 = mybir.dt.float32
F16 = mybir.dt.float16
BF16 = mybir.dt.bfloat16
U32 = mybir.dt.uint32
AF = mybir.ActivationFunctionType
ALU = mybir.AluOpType
AX = mybir.AxisListType

N_CORES = 8
B_GLOB, L, D = 16, 2048, 1024
B_LOC = 2
H, DK = 16, 64
NPAIR = 8
KC = 8
NCH = 16
NEXP, INTER = 8, 4096
ICH = 32
GTOK = 1024
TOKS = B_LOC * L
EPS = 1e-5

ENGS = ("sync", "gpsimd", "vector", "scalar", "tensor")
CSEM = {"tensor": "sT", "vector": "sV", "scalar": "sS", "gpsimd": "sP", "sync": "sY"}
NDQ = 8
DSEM = {"sync": ["dS%d" % i for i in range(NDQ)],
        "gpsimd": ["dP%d" % i for i in range(NDQ)],
        "scalar": ["dA%d" % i for i in range(NDQ)]}
SEMS = tuple(["sT", "sV", "sS", "sP", "sY"] + DSEM["sync"] + DSEM["gpsimd"]
             + DSEM["scalar"])

TAPS = {}


class Buf(object):
    __slots__ = ("name", "w", "r", "ov")

    def __init__(self, name):
        self.name = name
        self.w = None
        self.r = []
        self.ov = []


class Prog(object):
    def __init__(self):
        self.items = {e: [] for e in ENGS}
        self.cnt = {s: 0 for s in SEMS}
        self.strict_self = True
        self.drot = {e: 0 for e in DSEM}

    def op(self, eng, fn, reads=(), writes=(), extra=(), inc=True, dma=False):
        deps = [d for d in extra if d is not None]
        for b in reads:
            if b.w is not None:
                deps.append(b.w)
        for b in writes:
            for v in (b,) + tuple(b.ov):
                if v.w is not None:
                    deps.append(v.w)
                deps.extend(v.r)
        if dma:
            sem, amt = DSEM[eng][self.drot[eng] % NDQ], 16
            self.drot[eng] += 1
        elif inc:
            sem, amt = CSEM[eng], 1
        else:
            sem, amt = None, 0
        self.items[eng].append((fn, deps, sem, amt))
        tok = None
        if sem is not None:
            self.cnt[sem] += amt
            tok = (sem, self.cnt[sem])
            for b in reads:
                b.r.append(tok)
            for b in writes:
                for v in (b,) + tuple(b.ov):
                    v.w = tok
                    v.r = []
        return tok

    def install(self, block, sems):
        for name in ENGS:
            items = self.items[name]
            own = CSEM[name]

            def body(eng, items=items, own=own, name=name):
                wm = {s: 0 for s in SEMS}
                for fn, deps, sem, amt in items:
                    best = {}
                    for (s, v) in deps:
                        if s == own and (name == "tensor" or not self.strict_self):
                            continue
                        if v > best.get(s, 0):
                            best[s] = v
                    for s in sorted(best):
                        if best[s] > wm[s]:
                            eng.wait_ge(sems[s], best[s])
                            wm[s] = best[s]
                    inst = fn(eng)
                    if sem is not None:
                        inst.then_inc(sems[sem], amt)

            getattr(block, name)(body)


class Build(object):
    def __init__(self):
        self.nc = bass.Bass("TRN2", target_bir_lowering=False)
        self.p = Prog()
        self.bufs = {}
        self.ranges = {}
        self.prep_tok = {}
        self._mk_io()
        self._mk_sbuf()
        self.CAPV = [1e-9, 1e-5, 0.5, -0.5, 0.125]
        for i, v in enumerate(self.CAPV):
            self.nc.const_aps.register(F32, v, self.c_cap[:, i:i + 1])
        self._link_overlaps()
        self._mk_psum()

    def B(self, name):
        if name not in self.bufs:
            self.bufs[name] = Buf(name)
        return self.bufs[name]

    def _mk_io(self):
        nc = self.nc
        di = lambda n, s: nc.dram_tensor(n, s, F32, kind="ExternalInput")
        self.xt = di("xt", [128, KC, TOKS])
        self.wqt, self.wkt = di("wqt", [128, KC, D]), di("wkt", [128, KC, D])
        self.wvt, self.wot = di("wvt", [128, KC, D]), di("wot", [128, KC, D])
        self.awt = di("awt", [128, KC, D])
        self.bq_r, self.bv_r = di("bq_r", [1, D]), di("bv_r", [1, D])
        self.bk_pc, self.bo_pc = di("bk_pc", [128, KC]), di("bo_pc", [128, KC])
        self.l1g, self.l1b = di("l1g", [128, KC]), di("l1b", [128, KC])
        self.l2g, self.l2b = di("l2g", [128, KC]), di("l2b", [128, KC])
        self.gwt = di("gwt", [128, KC, NEXP])
        self.gpt = di("gpt", [NEXP, ICH, 128, KC, 128])
        self.upt = di("upt", [NEXP, ICH, 128, KC, 128])
        self.dpt = di("dpt", [NEXP, KC, 128, ICH, 128])
        self.out = nc.dram_tensor("out", [D, TOKS], F32, kind="ExternalOutput")
        ds = lambda n, s, t: nc.dram_tensor(n, s, t, kind="Internal")
        self.s_xhi = ds("s_xhi", [128, KC, TOKS], BF16)
        self.s_wqhi = ds("s_wqhi", [128, KC, D], BF16)
        self.s_wk = ds("s_wk", [128, KC, D], BF16)
        self.s_wv = ds("s_wv", [128, KC, D], BF16)
        self.s_wo = ds("s_wo", [128, KC, D], BF16)
        self.s_awt = ds("s_awt", [128, KC, D], BF16)
        self.s_q = [ds("s_q0", [NCH, 128, D], BF16), ds("s_q1", [NCH, 128, D], BF16)]
        self.s_k = [ds("s_k0", [KC, 128, L], BF16), ds("s_k1", [KC, 128, L], BF16)]
        self.s_v = [ds("s_v0", [NCH, 128, D], BF16), ds("s_v1", [NCH, 128, D], BF16)]
        self.s_sp = ds("s_sp", [4, 128, KC, GTOK], F32)

    def _mk_sbuf(self):
        nc = self.nc
        self._cur = 16512
        top = nc.sbuf_top

        def A(name, shape, dtype, off=None):
            nbytes = int(np.prod(shape[1:])) * mybir.dt.size(dtype)
            if off is None:
                off = (self._cur + 63) // 64 * 64
                self._cur = off + nbytes
            assert off + nbytes <= top, (name, off, nbytes, top)
            self.ranges[name] = (off, off + nbytes)
            t = nc.alloc_sbuf_tensor_at(name, list(shape), dtype, offset=off)
            setattr(self, name, t)
            return off

        for nm, sh, dt in [
            ("c_iotaF", [128, L], F16), ("c_iotaPC", [128, 16], F32),
            ("c_idn16", [128, 128], BF16), ("c_idnF", [128, 128], F32),
            ("c_ones_row", [1, 128], F32), ("c_ones_col", [128, 1], F32),
            ("c_basis", [8, 1024], F32), ("c_gwhi", [128, 64], BF16),
            ("c_gwlo", [128, 64], BF16), ("gw32", [128, 64], F32),
            ("c_bk", [128, KC], F32), ("c_bo", [128, KC], F32),
            ("c_l1g", [128, KC], F32), ("c_l1b", [128, KC], F32),
            ("c_l2g", [128, KC], F32), ("c_l2b", [128, KC], F32),
            ("c_jmod", [128, 1], F32),
            ("c_cap", [128, 8], F32),
        ]:
            A(nm, sh, dt)

        R1 = (self._cur + 63) // 64 * 64
        A("strmG", [128, KC, GTOK], F32, R1)
        self._cur = R1 + 32768

        R2 = self._cur
        A("wqhi", [128, KC, D], BF16, R2)
        A("wqlo", [128, KC, D], BF16, R2 + 16384)
        A("wvb", [128, KC, D], BF16, R2 + 32768)
        A("full", [128, KC, L], BF16, R2)
        A("hbuf", [128, KC, GTOK], F32, R2)
        A("zbuf", [128, KC, GTOK], BF16, R2 + 32768)
        A("abuf", [128, ICH // 2, GTOK], BF16, R2)
        A("strmB", [128, KC, GTOK], F32, R2)
        A("hbufB", [128, KC, GTOK], F32, R1)
        A("abufB", [128, ICH // 2, GTOK], BF16, R1)
        self._cur = R2 + 49152

        R3 = self._cur
        A("p_sb", [128, L], BF16, R3)
        A("prT", [128, NCH, NPAIR * 20], BF16, R3 + 4096)
        A("ohT", [128, NCH, H * 10], BF16, R3 + 14336)
        A("oh2", [128, L], BF16, R3 + 24576)
        A("kpair", [128, L], BF16, R3 + 28672)
        A("tbuf", [128, KC, GTOK], BF16, R3)
        A("dstage_0", [128, ICH // 2, 128], BF16, R3 + 16384)
        A("dstage_1", [128, ICH // 2, 128], BF16, R3 + 20480)
        A("gstage_0", [128, KC, 128], BF16, R3 + 32768)
        A("gstage_1", [128, KC, 128], BF16, R3 + 34816)
        A("ustage_0", [128, KC, 128], BF16, R3 + 36864)
        A("ustage_1", [128, KC, 128], BF16, R3 + 38912)
        A("gstage_2", [128, KC, 128], BF16, R3 + 24576)
        A("ustage_2", [128, KC, 128], BF16, R3 + 26624)
        A("dstage_2", [128, ICH // 2, 128], BF16, R3 + 28672)
        A("lgsb", [8, GTOK], F32, R3 + 40960)
        A("combfm", [8, GTOK], F32, R3 + 40960)
        A("combbc", [128, GTOK], F32, R3 + 45056)
        A("q2s_1", [128, D], F32, R3 + 40960)
        A("esc_1", [128, D], F32, R3 + 45056)
        A("stat_1", [128, 16 * 8], F32, R3 + 32768)
        A("qevac_1", [128, D], BF16, R3 + 36864)
        self._cur = R3 + 49152

        R4 = self._cur
        for j in range(2):
            A("xc32_%d" % j, [128, KC, 128], F32, R4 + 4096 * j)
            A("xh32_%d" % j, [128, KC, 128], F32, R4 + 8192 + 4096 * j)
            A("xhic_%d" % j, [128, KC, 128], BF16, R4 + 16384 + 2048 * j)
            A("q32_%d" % j, [128, D], F32, R4 + 24576 + 4096 * j)
            A("xhit_%d" % j, [128, KC, 512], BF16, R4 + 8192 * j)
            A("kevac_%d" % j, [128, 512], BF16, R4 + 16384 + 1024 * j)
            A("vevac_%d" % j, [128, D], BF16, R4 + 18432 + 2048 * j)
            A("qc_%d" % j, [128, D], BF16, R4 + 2048 * j)
            A("vc_%d" % j, [128, D], BF16, R4 + 4096 + 2048 * j)
            A("xres_%d" % j, [128, L], F32, R4 + 8192 + 8192 * j)
            A("sqscr_%d" % j, [128, 512], F32, R4 + 2048 * j)
            A("sgscr_%d" % j, [128, 512], F32, R4 + 4096 + 2048 * j)
            A("t32scr_%d" % j, [128, 512], F32, R4 + 8192 + 2048 * j)
            A("tloscr_%d" % j, [128, 512], BF16, R4 + 12288 + 1024 * j)
            A("haccscr_%d" % j, [128, 512], F32, R4 + 14336 + 2048 * j)
            A("aplscr_%d" % j, [128, 512], F32, R4 + 18432 + 2048 * j)
            A("spillscr_%d" % j, [128, 512], F32, R4 + 0 + 2048 * j)
        A("qevac_0", [128, D], BF16, R4 + 40960)
        A("q2s_0", [128, D], F32, R4 + 32768)
        A("esc_0", [128, D], F32, R4 + 36864)
        A("lgtm", [128, 64], F32, R4 + 28672)
        A("combtm", [128, 64], F32, R4 + 28928)
        A("gcols", [128, 24], F32, R4 + 29184)
        A("p_sb2", [128, L], BF16, R4 + 20480)
        A("prT2", [128, NCH, NPAIR * 20], BF16, 201216)
        A("kpair2", [128, L], BF16, R4 + 0)
        A("r1b", [128, GTOK], F32, R4 + 32768)
        A("r2b", [128, GTOK], F32, R4 + 36864)
        self._cur = R4 + 43008

        R5 = self._cur
        A("qs0", [16, L], F32, R5)
        A("bqv_b", [128, D], F32, R5 + 8192)
        A("browscr", [1, D], F32, R5 + 12288)
        A("idxflat", [1, H * 10], F32, R5 + 12288)
        A("thr", [1, 8], F32, R5 + 12928)
        A("s_row", [1, GTOK], F32, R5)
        A("qq_row", [1, GTOK], F32, R5 + 4096)
        A("r_row", [1, GTOK], F32, R5 + 8192)
        A("rowscr", [1, GTOK], F32, R5 + 12992)
        self._cur = R5 + 12992 + 4096

        for nm, sh, dt in [
            ("wstage_0", [128, KC, 128], BF16), ("wstage_1", [128, KC, 128], BF16),
            ("m4", [128, 4], F32), ("mcol", [128, 1], F32), ("mbias", [128, 1], F32),
            ("rscol", [128, 1], F32), ("rv_all", [128, NPAIR], F32),
            ("valid", [128, 1], F32), ("vtmp", [128, 1], F32),
            ("idxT", [128, 16], F32), ("idxf", [16, 10], F32),
            ("i8", [16, 8], U32), ("i2", [16, 8], U32),
            ("t8a", [16, 8], F32), ("t8b", [16, 8], F32),
            ("idx_bc", [128, H * 10], F32),
            ("scat_0", [128, 128], BF16), ("scat_1", [128, 128], BF16),
            ("stat_0", [128, 16 * 8], F32),
            ("urow", [1, 8], F32),
            ("idxT2", [128, 16], F32), ("valid2", [128, 1], F32),
            ("rv_all2", [128, NPAIR], F32), ("qspT2", [128, NPAIR * 10], BF16),
            ("qspT", [128, NPAIR * 10], BF16),
        ]:
            A(nm, sh, dt)
        A("qs0_1x", [16, L], F32, self.ranges["strmG"][0])
        R1_, R2_ = self.ranges["strmG"][0], self.ranges["wqhi"][0]
        A("xhic_2", [128, KC, 128], BF16, R1_ + 8192)
        A("qevac_2", [128, D], BF16, R1_ + 10240)
        A("q32_2", [128, D], F32, R1_ + 12288)
        A("q2s_2", [128, D], F32, R1_ + 16384)
        A("esc_2", [128, D], F32, R1_ + 20480)
        A("stat_2", [128, 16 * 8], F32, R1_ + 24576)
        A("stat_3", [128, 16 * 8], F32, R1_ + 25088)
        A("vevac_2", [128, D], BF16, R1_ + 10240)
        A("xhic_3", [128, KC, 128], BF16, R2_ + 32768)
        A("qevac_3", [128, D], BF16, R2_ + 34816)
        A("q32_3", [128, D], F32, R2_ + 36864)
        A("q2s_3", [128, D], F32, R2_ + 40960)
        A("esc_3", [128, D], F32, R2_ + 45056)
        A("vevac_3", [128, D], BF16, R2_ + 34816)
        assert self._cur <= self.nc.sbuf_top, (self._cur, self.nc.sbuf_top)

    def _link_overlaps(self):
        names = list(self.ranges)
        for i, a in enumerate(names):
            sa, ea = self.ranges[a]
            ba = self.B(a)
            for b_ in names[i + 1:]:
                sb, eb = self.ranges[b_]
                if sa < eb and sb < ea:
                    bb = self.B(b_)
                    ba.ov.append(bb)
                    bb.ov.append(ba)

    def _mk_psum(self):
        nc = self.nc
        self.PS = [nc.alloc_psum_tensor("PS%d" % i, [128, 512], F32) for i in range(6)]
        self.PT = [nc.alloc_psum_tensor("PT%d" % i, [128, 256], BF16) for i in range(2)]
        self.psb = [self.B("PS%d" % i) for i in range(6)]
        self.ptb = [self.B("PT%d" % i) for i in range(2)]
        self._ring = 0
        self._ringw = 4
        self._tring = 0

    def tap(self, name, ap, buf):
        name = name + getattr(self, "_tapsuf", "")
        if not getattr(self, "_tapon", False) or name not in TAPS:
            return
        t = self.nc.dram_tensor("tap_" + name, list(ap.shape), ap.dtype,
                                kind="ExternalOutput")
        self.p.op("sync", lambda e, t=t, ap=ap: e.dma_start(out=t.ap(), in_=ap),
                  reads=[buf], dma=True)

    def ring(self):
        i = self._ring % self._ringw
        self._ring += 1
        return self.PS[i], self.psb[i]

    def xbank(self):
        return self.PS[5], self.psb[5]

    def xbank2(self):
        return self.PS[4], self.psb[4]

    def tring(self):
        i = self._tring % 2
        self._tring += 1
        return self.PT[i], self.ptb[i]

    def build_consts(self):
        p = self.p
        scr = self.strmG[:].rearrange("q a b -> q (a b)")
        bA = self.B("strmG")
        t1, t2 = scr[:, 0:128], scr[:, 128:256]
        tb8 = scr[0:8, 256:1280]
        tbg = scr[:, 1280:1344]
        iF, iPC = self.B("c_iotaF"), self.B("c_iotaPC")
        for i, v in enumerate(self.CAPV):
            p.op("vector", lambda e, i=i, v=v: e.memset(self.c_cap[:, i:i + 1], v),
                 writes=[self.B("c_cap")])
        p.op("gpsimd", lambda e: e.iota(self.c_iotaF[:], [[1, L]], channel_multiplier=0,
                                        allow_small_or_imprecise_dtypes=True), writes=[iF])
        p.op("gpsimd", lambda e: e.iota(self.c_iotaPC[:], [[128, 16]], channel_multiplier=1,
                                        allow_small_or_imprecise_dtypes=True), writes=[iPC])
        p.op("gpsimd", lambda e: e.iota(t1, [[0, 128]], channel_multiplier=1,
                                        allow_small_or_imprecise_dtypes=True), writes=[bA])
        p.op("gpsimd", lambda e: e.iota(t2, [[1, 128]], channel_multiplier=0,
                                        allow_small_or_imprecise_dtypes=True), writes=[bA])
        p.op("vector", lambda e: e.tensor_tensor(out=self.c_idnF[:], in0=t1, in1=t2,
                                                 op=ALU.is_equal),
             reads=[bA], writes=[self.B("c_idnF")])
        p.op("vector", lambda e: e.tensor_copy(out=self.c_idn16[:], in_=self.c_idnF[:]),
             reads=[self.B("c_idnF")], writes=[self.B("c_idn16")])
        p.op("vector", lambda e: e.memset(self.c_ones_row[:], 1.0), writes=[self.B("c_ones_row")])
        p.op("vector", lambda e: e.memset(self.c_ones_col[:], 1.0), writes=[self.B("c_ones_col")])
        p.op("gpsimd", lambda e: e.iota(tb8.rearrange("q (a b) -> q a b", b=128),
                                        [[1, 8], [0, 128]], channel_multiplier=0,
                                        allow_small_or_imprecise_dtypes=True), writes=[bA])
        p.op("vector", lambda e: e.tensor_scalar(out=self.c_basis[:], in0=tb8,
                                                 scalar1=self.c_iotaPC[0:8, 0:1], scalar2=None,
                                                 op0=ALU.is_equal),
             reads=[bA, iPC], writes=[self.B("c_basis")])
        p.op("vector", lambda e: e.tensor_scalar(out=self.c_jmod[:], in0=self.c_iotaPC[:, 0:1],
                                                 scalar1=64.0, scalar2=None, op0=ALU.is_ge),
             reads=[iPC], writes=[self.B("c_jmod")])
        p.op("vector", lambda e: e.scalar_tensor_tensor(out=self.c_jmod[:], in0=self.c_jmod[:],
                                                        scalar=-64.0, in1=self.c_iotaPC[:, 0:1],
                                                        op0=ALU.mult, op1=ALU.add),
             reads=[iPC, self.B("c_jmod")], writes=[self.B("c_jmod")])
        for (name, src, dst) in [("c_bk", self.bk_pc, self.c_bk), ("c_bo", self.bo_pc, self.c_bo),
                                 ("c_l1g", self.l1g, self.c_l1g), ("c_l1b", self.l1b, self.c_l1b),
                                 ("c_l2g", self.l2g, self.c_l2g), ("c_l2b", self.l2b, self.c_l2b)]:
            p.op("scalar", lambda e, s=src, d=dst: e.dma_start(out=d[:], in_=s.ap()),
                 writes=[self.B(name)], dma=True)
        p.op("scalar", lambda e: e.dma_start(
            out=self.gw32[:].rearrange("q (kc ex) -> q kc ex", ex=8),
            in_=self.gwt.ap()),
            writes=[self.B("gw32")], dma=True)
        p.op("vector", lambda e: e.tensor_copy(out=self.c_gwhi[:], in_=self.gw32[:]),
             reads=[self.B("gw32")], writes=[self.B("c_gwhi")])
        p.op("vector", lambda e: e.tensor_copy(out=tbg, in_=self.c_gwhi[:]),
             reads=[self.B("c_gwhi")], writes=[bA])
        p.op("vector", lambda e: e.tensor_tensor(out=self.c_gwlo[:], in0=self.gw32[:],
                                                 in1=tbg, op=ALU.subtract),
             reads=[bA, self.B("gw32")], writes=[self.B("c_gwlo")])
        p.op("vector", lambda e: e.memset(self.oh2[:], 0.0), writes=[self.B("oh2")])
        p.op("vector", lambda e: e.memset(self.p_sb[:], 0.0), writes=[self.B("p_sb")])
        for i in range(6):
            p.op("vector", lambda e, i=i: e.memset(self.PS[i][:], 0.0), writes=[self.psb[i]])


    def build_prep(self):
        p = self.p
        casts = [("xhi", self.s_xhi.ap(), self.xt.ap()),
                 ("wqhi", self.s_wqhi.ap(), self.wqt.ap()),
                 ("wk", self.s_wk.ap(), self.wkt.ap()),
                 ("wv", self.s_wv.ap(), self.wvt.ap()),
                 ("wo", self.s_wo.ap(), self.wot.ap()),
                 ("awt", self.s_awt.ap(), self.awt.ap())]
        def fence():
            return [(s, p.cnt[s]) for s in DSEM["gpsimd"] if p.cnt[s]]

        for name, dst, src in casts:
            p.op("gpsimd", lambda e, d=dst, s=src: e.dma_start(out=d, in_=s), dma=True)
        ftok = fence()
        for name, _, _ in casts:
            self.prep_tok[name] = ftok

    def bcast_row(self, src_ap, src_buf, width, dst_ap, dst_buf):
        p = self.p
        for t in range((width + 511) // 512):
            w = min(512, width - t * 512)
            ps, pb = self.ring()
            p.op("tensor", lambda e, ps=ps, t=t, w=w, s=src_ap: e.matmul(
                ps[0:128, 0:w], self.c_ones_row[:], s[0:1, t * 512:t * 512 + w],
                start=True, stop=True),
                reads=[src_buf, self.B("c_ones_row")], writes=[pb])
            p.op("vector", lambda e, ps=ps, t=t, w=w, d=dst_ap: e.tensor_copy(
                out=d[:, t * 512:t * 512 + w], in_=ps[0:128, 0:w]),
                reads=[pb], writes=[dst_buf])

    def ln_rows(self, src, src_buf):
        p = self.p
        sb = [self.ring() for _ in range(2)]
        qb = [self.ring() for _ in range(2)]
        for tt in range(2):
            ps, pb = sb[tt]
            for mc in range(KC):
                p.op("tensor", lambda e, ps=ps, mc=mc, tt=tt, s=src: e.matmul(
                    ps[0:1, 0:512], self.c_ones_col[:], s[:, mc, tt * 512:(tt + 1) * 512],
                    start=(mc == 0), stop=(mc == KC - 1)),
                    reads=[src_buf, self.B("c_ones_col")], writes=[pb], inc=(mc == KC - 1))
            psq, pbq = qb[tt]
            for mc in range(KC):
                scr = getattr(self, "sqscr_%d" % (mc % 2))
                scb = self.B("sqscr_%d" % (mc % 2))
                p.op("vector", lambda e, sc=scr, mc=mc, tt=tt, s=src: e.tensor_tensor(
                    out=sc[:], in0=s[:, mc, tt * 512:(tt + 1) * 512],
                    in1=s[:, mc, tt * 512:(tt + 1) * 512], op=ALU.mult),
                    reads=[src_buf], writes=[scb])
                p.op("tensor", lambda e, ps=psq, sc=scr, mc=mc: e.matmul(
                    ps[0:1, 0:512], self.c_ones_col[:], sc[:],
                    start=(mc == 0), stop=(mc == KC - 1)),
                    reads=[scb, self.B("c_ones_col")],
                    writes=[pbq] if mc == KC - 1 else (), inc=True)
        rS, rQ, rR, rW = self.B("s_row"), self.B("qq_row"), self.B("r_row"), self.B("rowscr")
        for tt in range(2):
            ps, pb = sb[tt]
            p.op("vector", lambda e, ps=ps, tt=tt: e.tensor_scalar(
                out=self.s_row[0:1, tt * 512:(tt + 1) * 512], in0=ps[0:1, 0:512],
                scalar1=1.0 / D, scalar2=None, op0=ALU.mult), reads=[pb], writes=[rS])
            ps, pb = qb[tt]
            p.op("vector", lambda e, ps=ps, tt=tt: e.tensor_scalar(
                out=self.qq_row[0:1, tt * 512:(tt + 1) * 512], in0=ps[0:1, 0:512],
                scalar1=1.0 / D, scalar2=None, op0=ALU.mult), reads=[pb], writes=[rQ])
        p.op("vector", lambda e: e.tensor_tensor(out=self.r_row[:], in0=self.s_row[:],
                                                 in1=self.s_row[:], op=ALU.mult),
             reads=[rS], writes=[rR])
        p.op("vector", lambda e: e.tensor_tensor(out=self.r_row[:], in0=self.qq_row[:],
                                                 in1=self.r_row[:], op=ALU.subtract),
             reads=[rQ, rR], writes=[rR])
        p.op("scalar", lambda e: e.activation(out=self.r_row[:], in_=self.r_row[:],
                                              func=AF.Ln, bias=EPS, scale=1.0),
             reads=[rR, self.B("c_cap")], writes=[rR])
        p.op("scalar", lambda e: e.activation(out=self.r_row[:], in_=self.r_row[:],
                                              func=AF.Exp, bias=0.0, scale=-0.5),
             reads=[rR, self.B("c_cap")], writes=[rR])
        p.op("vector", lambda e: e.tensor_tensor(out=self.rowscr[:], in0=self.s_row[:],
                                                 in1=self.r_row[:], op=ALU.mult),
             reads=[rS, rR], writes=[rW])
        self.bcast_row(self.r_row[:], rR, GTOK, self.r1b[:], self.B("r1b"))
        self.bcast_row(self.rowscr[:], rW, GTOK, self.r2b[:], self.B("r2b"))

    def ln_apply(self, src, src_buf, emit_dst):
        p = self.p
        for mc in range(KC):
            for tt in range(2):
                scr = getattr(self, "aplscr_%d" % tt)
                scb = self.B("aplscr_%d" % tt)
                p.op("vector", lambda e, sc=scr, mc=mc, tt=tt, s=src: e.tensor_tensor(
                    out=sc[:], in0=s[:, mc, tt * 512:(tt + 1) * 512],
                    in1=self.r1b[:, tt * 512:(tt + 1) * 512], op=ALU.mult),
                    reads=[src_buf, self.B("r1b")], writes=[scb])
                p.op("vector", lambda e, sc=scr, tt=tt: e.tensor_tensor(
                    out=sc[:], in0=sc[:], in1=self.r2b[:, tt * 512:(tt + 1) * 512],
                    op=ALU.subtract),
                    reads=[scb, self.B("r2b")], writes=[scb])
                emit_dst(mc, tt, scr, scb)


def _attention_merged(bd):
    p = bd.p
    xt_r = bd.xt.ap()
    BT = [0, L]

    wqhiB, wqloB = bd.B("wqhi"), bd.B("wqlo")
    p.op("scalar", lambda e: e.dma_start(out=bd.wqhi[:], in_=bd.s_wqhi.ap()),
         writes=[wqhiB], extra=bd.prep_tok["wqhi"], dma=True)
    for k in range(KC):
        j = k % 2
        c32, c32b = getattr(bd, "xc32_%d" % j), bd.B("xc32_%d" % j)
        h32, h32b = getattr(bd, "xh32_%d" % j), bd.B("xh32_%d" % j)
        c32f = c32[:].rearrange("q a b -> q (a b)")
        h32f = h32[:].rearrange("q a b -> q (a b)")
        p.op("scalar", lambda e, k=k, c32f=c32f: e.dma_start(
            out=c32f, in_=bd.wqt.ap()[:, k, :]),
            writes=[c32b], dma=True)
        p.op("vector", lambda e, k=k, h32f=h32f: e.tensor_copy(
            out=h32f, in_=bd.wqhi[:, k, :]),
            reads=[wqhiB], writes=[h32b])
        p.op("vector", lambda e, k=k, c32f=c32f, h32f=h32f: e.tensor_tensor(
            out=bd.wqlo[:, k, :], in0=c32f, in1=h32f, op=ALU.subtract),
            reads=[c32b, h32b], writes=[wqloB])

    brB = bd.B("browscr")
    p.op("scalar", lambda e: e.dma_start(out=bd.browscr[:], in_=bd.bq_r.ap()),
         writes=[brB], dma=True)
    bd.bcast_row(bd.browscr[:], brB, D, bd.bqv_b[:], bd.B("bqv_b"))

    QS = [bd.qs0, bd.qs0_1x]
    QSB = [bd.B("qs0"), bd.B("qs0_1x")]

    bd._ringw = 6
    pend_st = []
    for c in range(NCH):
        for b in range(2):
            j = (2 * c + b) % 4
            st = getattr(bd, "stat_%d" % j)
            stB = bd.B("stat_%d" % j)
            q2t, q2B = getattr(bd, "q2s_%d" % j), bd.B("q2s_%d" % j)
            est, escB = getattr(bd, "esc_%d" % j), bd.B("esc_%d" % j)
            xhi, xhib = getattr(bd, "xhic_%d" % j), bd.B("xhic_%d" % j)
            q32, q32b = getattr(bd, "q32_%d" % j), bd.B("q32_%d" % j)
            qev, qevb = getattr(bd, "qevac_%d" % j), bd.B("qevac_%d" % j)
            bt = BT[b]
            p.op("scalar", lambda e, c=c, bt=bt, xhi=xhi: e.dma_start(
                out=xhi[:], in_=bd.s_xhi.ap()[:, :, bt + c * 128: bt + (c + 1) * 128]),
                writes=[xhib], extra=bd.prep_tok["xhi"], dma=True)
            halves = [bd.ring(), bd.ring()]
            for hf in range(2):
                ps, pb = halves[hf]
                passes = [(bd.wqhi, wqhiB), (bd.wqlo, wqloB)]
                for pi, (wb, wbb) in enumerate(passes):
                    for k in range(KC):
                        fst = (pi == 0 and k == 0)
                        lst = (pi == 1 and k == KC - 1)
                        p.op("tensor", lambda e, ps=ps, xhi=xhi, wb=wb, k=k, hf=hf,
                             fst=fst, lst=lst: e.matmul(
                                 ps[0:128, 0:512], xhi[:, k, :],
                                 wb[:, k, hf * 512:(hf + 1) * 512], start=fst, stop=lst),
                             reads=[xhib, wbb], writes=[pb], inc=lst)
            for hf in range(2):
                ps, pb = halves[hf]
                p.op("vector", lambda e, ps=ps, hf=hf, q32=q32: e.tensor_tensor(
                    out=q32[:, hf * 512:(hf + 1) * 512], in0=ps[0:128, 0:512],
                    in1=bd.bqv_b[:, hf * 512:(hf + 1) * 512], op=ALU.add),
                    reads=[pb, bd.B("bqv_b")], writes=[q32b])
            p.op("gpsimd", lambda e, qev=qev, q32=q32: e.tensor_copy(
                out=qev[:], in_=q32[:]),
                reads=[q32b], writes=[qevb])
            p.op("scalar", lambda e, c=c, b=b, qev=qev: e.dma_start(
                out=bd.s_q[b].ap()[c], in_=qev[:]),
                reads=[qevb], writes=[bd.B("s_q%d" % b)], dma=True)

            def mk_stats(c=c, b=b, st=st, stB=stB, q2t=q2t, q2B=q2B, est=est,
                         escB=escB, q32=q32, q32b=q32b):
                mq, sq, sqq, ent, l2t, var, qsc, tmp = (
                    st[:, i * 16:(i + 1) * 16] for i in range(8))
                q3 = q32[:].rearrange("q (h dk) -> q h dk", dk=DK)
                q23 = q2t[:].rearrange("q (h dk) -> q h dk", dk=DK)
                e3 = est[:].rearrange("q (h dk) -> q h dk", dk=DK)
                steps = [
                    lambda: p.op("gpsimd", lambda e: e.tensor_tensor(
                        out=q2t[:], in0=q32[:], in1=q32[:], op=ALU.mult),
                        reads=[q32b], writes=[q2B]),
                    lambda: p.op("vector", lambda e: e.reduce_sum(
                        out=sq, in_=q3, axis=AX.X), reads=[q32b], writes=[stB]),
                    lambda: p.op("vector", lambda e: e.reduce_sum(
                        out=sqq, in_=q23, axis=AX.X), reads=[q2B], writes=[stB]),
                    lambda: p.op("vector", lambda e: e.reduce_max(
                        out=mq, in_=q3, axis=AX.X), reads=[q32b], writes=[stB]),
                    lambda: p.op("vector", lambda e: e.tensor_tensor(
                        out=e3, in0=q3, in1=mq.unsqueeze(2).to_broadcast([128, H, DK]),
                        op=ALU.subtract), reads=[q32b, stB], writes=[escB]),
                    lambda: p.op("scalar", lambda e: e.activation(
                        out=q2t[:], in_=est[:], func=AF.Exp, bias=0.0, scale=1.0),
                        reads=[escB], writes=[q2B]),
                    lambda: p.op("vector", lambda e: e.reduce_sum(
                        out=tmp, in_=q23, axis=AX.X), reads=[q2B], writes=[stB]),
                    lambda: p.op("vector", lambda e: e.reciprocal(out=tmp, in_=tmp),
                                 reads=[stB], writes=[stB]),
                    lambda: p.op("vector", lambda e: e.tensor_tensor(
                        out=q23, in0=q23, in1=tmp.unsqueeze(2).to_broadcast([128, H, DK]),
                        op=ALU.mult), reads=[q2B, stB], writes=[q2B]),
                    lambda: p.op("scalar", lambda e: e.activation(
                        out=est[:], in_=q2t[:], func=AF.Ln, bias=1e-9, scale=1.0),
                        reads=[q2B, bd.B("c_cap")], writes=[escB]),
                    lambda: p.op("gpsimd", lambda e: e.tensor_tensor(
                        out=est[:], in0=q2t[:], in1=est[:], op=ALU.mult),
                        reads=[q2B, escB], writes=[escB]),
                    lambda: p.op("vector", lambda e: e.reduce_sum(
                        out=ent, in_=e3, axis=AX.X, negate=True),
                        reads=[escB], writes=[stB]),
                    lambda: p.op("scalar", lambda e: e.activation(
                        out=l2t, in_=sqq, func=AF.Ln, bias=0.0, scale=1.0),
                        reads=[stB], writes=[stB]),
                    lambda: p.op("scalar", lambda e: e.activation(
                        out=l2t, in_=l2t, func=AF.Exp, bias=0.0, scale=0.5),
                        reads=[stB, bd.B("c_cap")], writes=[stB]),
                    lambda: p.op("vector", lambda e: e.tensor_tensor(
                        out=var, in0=sq, in1=sq, op=ALU.mult), reads=[stB], writes=[stB]),
                    lambda: p.op("vector", lambda e: e.scalar_tensor_tensor(
                        out=var, in0=var, scalar=-1.0 / DK, in1=sqq,
                        op0=ALU.mult, op1=ALU.add), reads=[stB], writes=[stB]),
                    lambda: p.op("vector", lambda e: e.tensor_scalar(
                        out=qsc, in0=l2t, scalar1=0.5, scalar2=None, op0=ALU.mult),
                        reads=[stB], writes=[stB]),
                    lambda: p.op("vector", lambda e: e.scalar_tensor_tensor(
                        out=qsc, in0=ent, scalar=0.3, in1=qsc, op0=ALU.mult, op1=ALU.add),
                        reads=[stB], writes=[stB]),
                    lambda: p.op("vector", lambda e: e.scalar_tensor_tensor(
                        out=qsc, in0=var, scalar=0.2 / (DK - 1), in1=qsc,
                        op0=ALU.mult, op1=ALU.add),
                        reads=[stB], writes=[stB]),
                ]

                def tp_step():
                    ps, pb = bd.ring()
                    p.op("tensor", lambda e, ps=ps: e.transpose(
                        ps[0:16, 0:128], qsc, bd.c_idnF[:]),
                        reads=[stB, bd.B("c_idnF")], writes=[pb])
                    p.op("vector", lambda e, ps=ps: e.tensor_copy(
                        out=QS[b][0:16, c * 128:(c + 1) * 128], in_=ps[0:16, 0:128]),
                        reads=[pb], writes=[QSB[b]])
                steps.append(tp_step)
                return steps
            pend_st.append(mk_stats())
            if len(pend_st) == 4:
                nst = max(len(s_) for s_ in pend_st)
                for si_ in range(nst):
                    for ch_ in pend_st:
                        if si_ < len(ch_):
                            ch_[si_]()
                pend_st = []
    for ch_ in pend_st:
        for f_ in ch_:
            f_()
    bd._ringw = 4

    IDXT = [bd.idxT, bd.idxT2]
    VALID = [bd.valid, bd.valid2]
    QSP = [bd.qspT, bd.qspT2]

    def a4(b):
        qs, qsB = QS[b], QSB[b]
        idxTb = bd.B("idxT%s" % ("" if b == 0 else "2"))
        validb = bd.B("valid%s" % ("" if b == 0 else "2"))
        iB = bd.B("idxf")
        p.op("vector", lambda e, qs=qs: e.max(out=bd.t8a[:], in_=qs[0:16, :]),
             reads=[qsB], writes=[bd.B("t8a")])
        p.op("vector", lambda e, qs=qs: e.max_index(out=bd.i8[:], in_max=bd.t8a[:],
                                                    in_values=qs[0:16, :]),
             reads=[qsB, bd.B("t8a")], writes=[bd.B("i8")])
        uB = bd.B("urow")
        xr0, xr0b = bd.xres_0, bd.B("xres_0")
        p.op("vector", lambda e, qs=qs: e.reduce_sum(out=bd.urow[0:1, 0:1],
                                                     in_=qs[0:1, :], axis=AX.X),
             reads=[qsB], writes=[uB])
        p.op("vector", lambda e, qs=qs, xr0=xr0: e.tensor_tensor(
            out=xr0[0:1, :], in0=qs[0:1, :], in1=qs[0:1, :], op=ALU.mult),
            reads=[qsB], writes=[xr0b])
        p.op("vector", lambda e, xr0=xr0: e.reduce_sum(out=bd.urow[0:1, 1:2],
                                                       in_=xr0[0:1, :], axis=AX.X),
             reads=[xr0b], writes=[uB])
        p.op("vector", lambda e: e.tensor_scalar(out=bd.urow[0:1, 2:3], in0=bd.urow[0:1, 0:1],
                                                 scalar1=1.0 / L, scalar2=None, op0=ALU.mult),
             reads=[uB], writes=[uB])
        p.op("vector", lambda e: e.tensor_tensor(out=bd.urow[0:1, 3:4], in0=bd.urow[0:1, 0:1],
                                                 in1=bd.urow[0:1, 2:3], op=ALU.mult),
             reads=[uB], writes=[uB])
        p.op("vector", lambda e: e.tensor_tensor(out=bd.urow[0:1, 3:4], in0=bd.urow[0:1, 1:2],
                                                 in1=bd.urow[0:1, 3:4], op=ALU.subtract),
             reads=[uB], writes=[uB])
        p.op("vector", lambda e: e.tensor_scalar(out=bd.urow[0:1, 3:4], in0=bd.urow[0:1, 3:4],
                                                 scalar1=1.0 / (L - 1), scalar2=None,
                                                 op0=ALU.mult),
             reads=[uB], writes=[uB])
        p.op("scalar", lambda e: e.activation(out=bd.urow[0:1, 4:5], in_=bd.urow[0:1, 3:4],
                                              func=AF.Ln, bias=0.0, scale=1.0),
             reads=[uB], writes=[uB])
        p.op("scalar", lambda e: e.activation(out=bd.urow[0:1, 4:5], in_=bd.urow[0:1, 4:5],
                                              func=AF.Exp, bias=0.0, scale=0.5),
             reads=[uB, bd.B("c_cap")], writes=[uB])
        p.op("vector", lambda e: e.tensor_scalar(out=bd.urow[0:1, 5:6], in0=bd.urow[0:1, 2:3],
                                                 scalar1=1e-6, scalar2=None, op0=ALU.add),
             reads=[uB], writes=[uB])
        p.op("vector", lambda e: e.reciprocal(out=bd.urow[0:1, 5:6], in_=bd.urow[0:1, 5:6]),
             reads=[uB], writes=[uB])
        p.op("vector", lambda e: e.tensor_tensor(out=bd.urow[0:1, 6:7], in0=bd.urow[0:1, 4:5],
                                                 in1=bd.urow[0:1, 5:6], op=ALU.mult),
             reads=[uB], writes=[uB])
        p.op("vector", lambda e: e.tensor_scalar(out=bd.thr[0:1, 0:1], in0=bd.urow[0:1, 6:7],
                                                 scalar1=10.0, scalar2=-0.5, op0=ALU.mult,
                                                 op1=ALU.add),
             reads=[uB], writes=[bd.B("thr")])
        ps, pb = bd.ring()
        p.op("tensor", lambda e, ps=ps: e.matmul(ps[0:128, 0:1], bd.c_ones_row[:],
                                                 bd.thr[0:1, 0:1], start=True, stop=True),
             reads=[bd.B("thr"), bd.B("c_ones_row")], writes=[pb])
        p.op("vector", lambda e, ps=ps: e.tensor_tensor(out=bd.vtmp[:], in0=ps[0:128, 0:1],
                                                        in1=bd.c_jmod[:], op=ALU.is_gt),
             reads=[pb, bd.B("c_jmod")], writes=[bd.B("vtmp")])
        p.op("vector", lambda e, b=b: e.tensor_scalar(out=VALID[b][:], in0=bd.c_jmod[:],
                                                      scalar1=2.5, scalar2=None,
                                                      op0=ALU.is_lt),
             reads=[bd.B("c_jmod")], writes=[validb])
        p.op("vector", lambda e, b=b: e.tensor_tensor(out=VALID[b][:], in0=VALID[b][:],
                                                      in1=bd.vtmp[:], op=ALU.max),
             reads=[bd.B("vtmp"), validb], writes=[validb])
        p.op("vector", lambda e, qs=qs: e.match_replace(out=qs[0:16, :], in_to_replace=bd.t8a[:],
                                                        in_values=qs[0:16, :],
                                                        imm_value=-1e30),
             reads=[bd.B("t8a")], writes=[qsB])
        p.op("vector", lambda e, qs=qs: e.max(out=bd.t8b[:], in_=qs[0:16, :]),
             reads=[qsB], writes=[bd.B("t8b")])
        p.op("vector", lambda e, qs=qs: e.max_index(out=bd.i2[:], in_max=bd.t8b[:],
                                                    in_values=qs[0:16, :]),
             reads=[qsB, bd.B("t8b")], writes=[bd.B("i2")])
        p.op("vector", lambda e: e.tensor_copy(out=bd.idxf[:, 0:8], in_=bd.i8[:]),
             reads=[bd.B("i8")], writes=[iB])
        p.op("vector", lambda e: e.tensor_copy(out=bd.idxf[:, 8:10], in_=bd.i2[:, 0:2]),
             reads=[bd.B("i2")], writes=[iB])
        ps, pb = bd.ring()
        p.op("tensor", lambda e, ps=ps: e.transpose(
            ps[0:10, 0:16], bd.idxf[:], bd.c_idnF[0:16, 0:16]),
            reads=[iB, bd.B("c_idnF")], writes=[pb])
        p.op("vector", lambda e, ps=ps, b=b: e.tensor_copy(
            out=IDXT[b][0:10, :], in_=ps[0:10, 0:16]),
            reads=[pb], writes=[idxTb])
        p.op("scalar", lambda e, b=b: e.dma_start(out=IDXT[b][64:74, :], in_=IDXT[b][0:10, :]),
             reads=[idxTb], writes=[idxTb], dma=True)
        p.op("scalar", lambda e: e.dma_start(out=bd.idxflat[:], in_=bd.idxf[:]),
             reads=[iB], writes=[bd.B("idxflat")], dma=True)
        ps, pb = bd.ring()
        p.op("tensor", lambda e, ps=ps: e.matmul(ps[0:128, 0:160], bd.c_ones_row[:],
                                                 bd.idxflat[:], start=True, stop=True),
             reads=[bd.B("idxflat"), bd.B("c_ones_row")], writes=[pb])
        p.op("vector", lambda e, ps=ps: e.tensor_copy(out=bd.idx_bc[:], in_=ps[0:128, 0:160]),
             reads=[pb], writes=[bd.B("idx_bc")])
        for c in range(NCH):
            p.op("vector", lambda e, c=c: e.tensor_scalar(
                out=bd.ohT[:, c, :], in0=bd.idx_bc[:], scalar1=bd.c_iotaPC[:, c:c + 1],
                scalar2=None, op0=ALU.is_equal),
                reads=[bd.B("idx_bc"), bd.B("c_iotaPC")], writes=[bd.B("ohT")])

    def a5(b):
        psGA, pbGA = bd.xbank()
        psGB, pbGB = bd.xbank2()
        qspb = bd.B("qspT%s" % ("" if b == 0 else "2"))
        for c in range(NCH):
            j = c % 2
            qcv, qcb = getattr(bd, "qc_%d" % j), bd.B("qc_%d" % j)
            p.op("sync", lambda e, c=c, b=b, qcv=qcv: e.dma_start(
                out=qcv[:], in_=bd.s_q[b].ap()[c]),
                reads=[bd.B("s_q%d" % b)], writes=[qcb], dma=True)
            for pr in range(NPAIR):
                for hh in range(2):
                    h = 2 * pr + hh
                    fst = (c == 0 and pr == 0)
                    lst = (c == NCH - 1 and pr == NPAIR - 1)
                    ctok = (pr == NPAIR - 1)
                    ps = psGA if hh == 0 else psGB
                    pb = pbGA if hh == 0 else pbGB
                    kw = {} if hh == 0 else {"tile_position": (0, 64)}
                    base = 0 if hh == 0 else 64
                    p.op("tensor", lambda e, ps=ps, c=c, qcv=qcv, pr=pr, h=h, base=base,
                         kw=kw, fst=fst, lst=lst: e.matmul(
                             ps[base:base + 64, pr * 10:(pr + 1) * 10],
                             qcv[:, h * 64:(h + 1) * 64],
                             bd.ohT[:, c, h * 10:(h + 1) * 10],
                             start=fst, stop=lst, **kw),
                         reads=[qcb, bd.B("ohT")], writes=[pb] if lst else (),
                         inc=(lst or ctok))
        p.op("vector", lambda e, ps=psGA, b=b: e.tensor_copy(
            out=QSP[b][0:64, :], in_=ps[0:64, 0:80]),
            reads=[pbGA], writes=[qspb])
        p.op("vector", lambda e, ps=psGB, b=b: e.tensor_copy(
            out=QSP[b][64:128, :], in_=ps[64:128, 0:80]),
            reads=[pbGB], writes=[qspb])

    a4(0)
    bd._tapon = True
    bd._tapsuf = ""
    bd.tap("idxf", bd.idxf[:], bd.B("idxf"))
    bd.tap("qs0", bd.qs0[0:16, :], QSB[0])
    bd.tap("idxT", bd.idxT[:], bd.B("idxT"))
    bd.tap("valid", bd.valid[:], bd.B("valid"))
    bd._tapon = False
    a5(0)
    a4(1)
    a5(1)

    bd._ringw = 6
    for tt in range(4):
        for b in range(2):
            xht, xhtb = getattr(bd, "xhit_%d" % b), bd.B("xhit_%d" % b)
            p.op("scalar", lambda e, tt=tt, b=b, xht=xht: e.dma_start(
                out=xht[:], in_=bd.s_xhi.ap()[:, :, BT[b] + tt * 512: BT[b] + (tt + 1) * 512]),
                writes=[xhtb], extra=bd.prep_tok["xhi"], dma=True)
        for m in range(KC):
            wj = (tt * KC + m) % 2
            ws, wsb = getattr(bd, "wstage_%d" % wj), bd.B("wstage_%d" % wj)
            p.op("sync", lambda e, m=m, ws=ws: e.dma_start(
                out=ws[:], in_=bd.s_wk.ap()[:, :, m * 128:(m + 1) * 128]),
                writes=[wsb], extra=bd.prep_tok["wk"], dma=True)
            for b in range(2):
                xht, xhtb = getattr(bd, "xhit_%d" % b), bd.B("xhit_%d" % b)
                ps, pb = bd.ring()
                for k in range(KC):
                    p.op("tensor", lambda e, ps=ps, k=k, ws=ws, xht=xht: e.matmul(
                        ps[0:128, 0:512], ws[:, k, :], xht[:, k, :],
                        start=(k == 0), stop=(k == KC - 1)),
                        reads=[wsb, xhtb], writes=[pb], inc=(k == KC - 1))
                kev, kevb = getattr(bd, "kevac_%d" % b), bd.B("kevac_%d" % b)
                p.op("vector", lambda e, ps=ps, m=m, kev=kev: e.tensor_scalar(
                    out=kev[:], in0=ps[0:128, 0:512], scalar1=bd.c_bk[:, m:m + 1],
                    scalar2=None, op0=ALU.add),
                    reads=[pb, bd.B("c_bk")], writes=[kevb])
                p.op("scalar", lambda e, m=m, tt=tt, b=b, kev=kev: e.dma_start(
                    out=bd.s_k[b].ap()[m][:, tt * 512:(tt + 1) * 512], in_=kev[:]),
                    reads=[kevb], writes=[bd.B("s_k%d" % b)], dma=True)
    p.op("scalar", lambda e: e.dma_start(out=bd.browscr[:], in_=bd.bv_r.ap()),
         writes=[brB], dma=True)
    bd.bcast_row(bd.browscr[:], brB, D, bd.bqv_b[:], bd.B("bqv_b"))
    p.op("scalar", lambda e: e.dma_start(out=bd.wvb[:], in_=bd.s_wv.ap()),
         writes=[bd.B("wvb")], extra=bd.prep_tok["wv"], dma=True)
    for c in range(NCH):
        for b in range(2):
            j = (2 * c + b) % 3
            xhi, xhib = getattr(bd, "xhic_%d" % j), bd.B("xhic_%d" % j)
            p.op("scalar", lambda e, c=c, b=b, xhi=xhi: e.dma_start(
                out=xhi[:], in_=bd.s_xhi.ap()[:, :, BT[b] + c * 128: BT[b] + (c + 1) * 128]),
                writes=[xhib], extra=bd.prep_tok["xhi"], dma=True)
            halves = [bd.ring(), bd.ring()]
            for hf in range(2):
                ps, pb = halves[hf]
                for k in range(KC):
                    p.op("tensor", lambda e, ps=ps, k=k, hf=hf, xhi=xhi: e.matmul(
                        ps[0:128, 0:512], xhi[:, k, :],
                        bd.wvb[:, k, hf * 512:(hf + 1) * 512],
                        start=(k == 0), stop=(k == KC - 1)),
                        reads=[xhib, bd.B("wvb")], writes=[pb], inc=(k == KC - 1))
            vev, vevb = getattr(bd, "vevac_%d" % j), bd.B("vevac_%d" % j)
            for hf in range(2):
                ps, pb = halves[hf]
                p.op("vector", lambda e, ps=ps, hf=hf, vev=vev: e.tensor_tensor(
                    out=vev[:, hf * 512:(hf + 1) * 512], in0=ps[0:128, 0:512],
                    in1=bd.bqv_b[:, hf * 512:(hf + 1) * 512], op=ALU.add),
                    reads=[pb, bd.B("bqv_b")], writes=[vevb])
            p.op("scalar", lambda e, c=c, b=b, vev=vev: e.dma_start(
                out=bd.s_v[b].ap()[c], in_=vev[:]),
                reads=[vevb], writes=[bd.B("s_v%d" % b)], dma=True)

    bd._ringw = 4
    PSB = [bd.p_sb, bd.p_sb2]
    PSBB = [bd.B("p_sb"), bd.B("p_sb2")]
    PRT = [bd.prT, bd.prT2]
    PRTB = [bd.B("prT"), bd.B("prT2")]
    KP = [bd.kpair, bd.kpair2]
    KPB = [bd.B("kpair"), bd.B("kpair2")]
    RV = [bd.rv_all, bd.rv_all2]
    RVB = [bd.B("rv_all"), bd.B("rv_all2")]
    for pr in range(NPAIR):
        for b in range(2):
            st = getattr(bd, "stat_%d" % b)
            stB = bd.B("stat_%d" % b)
            m4, mcol, mbias = st[:, 0:4], st[:, 4:5], st[:, 5:6]
            kp, kpb = KP[b], KPB[b]
            p.op("sync", lambda e, pr=pr, b=b, kp=kp: e.dma_start(
                out=kp[:], in_=bd.s_k[b].ap()[pr]),
                reads=[bd.B("s_k%d" % b)], writes=[kpb], dma=True)
            banks = [bd.ring() for _ in range(4)]
            for jj in range(4):
                ps, pb = banks[jj]
                for hh in range(2):
                    base = 0 if hh == 0 else 64
                    tp = (0, 0) if hh == 0 else (64, 64)
                    p.op("tensor", lambda e, ps=ps, jj=jj, pr=pr, b=b, base=base, tp=tp,
                         kp=kp: e.matmul(
                             ps[base:base + 10, 0:512],
                             QSP[b][base:base + 64, pr * 10:(pr + 1) * 10],
                             kp[base:base + 64, jj * 512:(jj + 1) * 512],
                             start=True, stop=True, tile_position=tp),
                         reads=[bd.B("qspT%s" % ("" if b == 0 else "2")), kpb],
                         writes=[pb])
            for jj in range(4):
                ps, pb = banks[jj]
                p.op("vector", lambda e, ps=ps, jj=jj, m4=m4: e.reduce_max(
                    out=m4[:, jj:jj + 1], in_=ps[0:128, 0:512], axis=AX.X),
                    reads=[pb], writes=[stB])
            p.op("vector", lambda e, m4=m4, mcol=mcol: e.reduce_max(
                out=mcol, in_=m4, axis=AX.X), reads=[stB], writes=[stB])
            p.op("vector", lambda e, mcol=mcol, mbias=mbias: e.tensor_scalar(
                out=mbias, in0=mcol, scalar1=-0.125, scalar2=None, op0=ALU.mult),
                reads=[stB], writes=[stB])
            for jj in range(4):
                ps, pb = banks[jj]
                p.op("scalar", lambda e, ps=ps, jj=jj, b=b, mbias=mbias, m4=m4: e.activation(
                    out=PSB[b][:, jj * 512:(jj + 1) * 512], in_=ps[0:128, 0:512],
                    func=AF.Exp, bias=mbias, scale=0.125, accum_out=m4[:, jj:jj + 1]),
                    reads=[pb, stB, bd.B("c_cap")], writes=[PSBB[b], stB])
            p.op("vector", lambda e, m4=m4, mcol=mcol: e.reduce_sum(
                out=mcol, in_=m4, axis=AX.X), reads=[stB], writes=[stB])
            p.op("vector", lambda e, mcol=mcol: e.reciprocal(out=mcol, in_=mcol),
                 reads=[stB], writes=[stB])
            p.op("vector", lambda e, pr=pr, b=b, mcol=mcol: e.tensor_tensor(
                out=RV[b][:, pr:pr + 1], in0=mcol, in1=VALID[b][:], op=ALU.mult),
                reads=[stB, bd.B("valid%s" % ("" if b == 0 else "2"))], writes=[RVB[b]])
            for c in range(NCH):
                pt, ptb = bd.tring()
                p.op("tensor", lambda e, pt=pt, c=c, b=b: e.transpose(
                    pt[0:128, 0:128], PSB[b][:, c * 128:(c + 1) * 128], bd.c_idn16[:]),
                    reads=[PSBB[b], bd.B("c_idn16")], writes=[ptb])
                p.op("vector", lambda e, pt=pt, c=c, pr=pr, b=b: e.tensor_copy(
                    out=PRT[b][:, c, pr * 20:pr * 20 + 10], in_=pt[0:128, 0:10]),
                    reads=[ptb], writes=[PRTB[b]])
                p.op("vector", lambda e, pt=pt, c=c, pr=pr, b=b: e.tensor_copy(
                    out=PRT[b][:, c, pr * 20 + 10:pr * 20 + 20], in_=pt[0:128, 64:74]),
                    reads=[ptb], writes=[PRTB[b]])

    def a78(b):
        psOA, pbOA = bd.xbank()
        psOB, pbOB = bd.xbank2()
        for c in range(NCH):
            j = c % 2
            vcv, vcb = getattr(bd, "vc_%d" % j), bd.B("vc_%d" % j)
            p.op("sync", lambda e, c=c, b=b, vcv=vcv: e.dma_start(
                out=vcv[:], in_=bd.s_v[b].ap()[c]),
                reads=[bd.B("s_v%d" % b)], writes=[vcb], dma=True)
            for pr in range(NPAIR):
                for hh in range(2):
                    fst = (c == 0 and pr == 0)
                    lst = (c == NCH - 1 and pr == NPAIR - 1)
                    ctok = (pr == NPAIR - 1)
                    ps = psOA if hh == 0 else psOB
                    pb = pbOA if hh == 0 else pbOB
                    base = 0 if hh == 0 else 64
                    kw = {} if hh == 0 else {"tile_position": (0, 64)}
                    p.op("tensor", lambda e, ps=ps, c=c, vcv=vcv, pr=pr, hh=hh, b=b,
                         base=base, kw=kw, fst=fst, lst=lst: e.matmul(
                             ps[base:base + 10, pr * 64:(pr + 1) * 64],
                             PRT[b][:, c, pr * 20 + hh * 10: pr * 20 + (hh + 1) * 10],
                             vcv[:, pr * 128 + hh * 64: pr * 128 + (hh + 1) * 64],
                             start=fst, stop=lst, **kw),
                         reads=[PRTB[b], vcb], writes=[pb] if lst else (),
                         inc=(lst or ctok))
        fB = bd.B("full")
        for pr in range(NPAIR):
            sj = pr % 2
            sc, sb = getattr(bd, "scat_%d" % sj), bd.B("scat_%d" % sj)
            p.op("vector", lambda e, sc=sc: e.memset(sc[:], 0.0), writes=[sb])
            p.op("vector", lambda e, ps=psOA, pr=pr, sc=sc, b=b: e.tensor_scalar(
                out=sc[0:10, 0:64], in0=ps[0:10, pr * 64:(pr + 1) * 64],
                scalar1=RV[b][0:10, pr:pr + 1], scalar2=None, op0=ALU.mult),
                reads=[pbOA, RVB[b]], writes=[sb])
            p.op("vector", lambda e, ps=psOB, pr=pr, sc=sc, b=b: e.tensor_scalar(
                out=sc[64:74, 64:128], in0=ps[64:74, pr * 64:(pr + 1) * 64],
                scalar1=RV[b][64:74, pr:pr + 1], scalar2=None, op0=ALU.mult),
                reads=[pbOB, RVB[b]], writes=[sb])
            ohB = bd.B("oh2")
            p.op("vector", lambda e, pr=pr, b=b: e.tensor_scalar(
                out=bd.oh2[0:10, :], in0=bd.c_iotaF[0:10, :],
                scalar1=IDXT[b][0:10, 2 * pr:2 * pr + 1], scalar2=None, op0=ALU.is_equal),
                reads=[bd.B("c_iotaF"), bd.B("idxT%s" % ("" if b == 0 else "2"))],
                writes=[ohB])
            p.op("vector", lambda e, pr=pr, b=b: e.tensor_scalar(
                out=bd.oh2[64:74, :], in0=bd.c_iotaF[64:74, :],
                scalar1=IDXT[b][64:74, 2 * pr + 1:2 * pr + 2], scalar2=None,
                op0=ALU.is_equal),
                reads=[bd.B("c_iotaF"), bd.B("idxT%s" % ("" if b == 0 else "2"))],
                writes=[ohB])
            for jj in range(4):
                ps, pb = bd.ring()
                p.op("tensor", lambda e, ps=ps, jj=jj, sc=sc: e.matmul(
                    ps[0:128, 0:512], sc[:], bd.oh2[:, jj * 512:(jj + 1) * 512],
                    start=True, stop=True),
                    reads=[sb, ohB], writes=[pb])
                p.op("vector", lambda e, ps=ps, jj=jj, pr=pr: e.tensor_copy(
                    out=bd.full[:, pr, jj * 512:(jj + 1) * 512], in_=ps[0:128, 0:512]),
                    reads=[pb], writes=[fB])
        if b == 0:
            bd._tapon = True
            bd.tap("prT", PRT[0][:], PRTB[0])
            bd._tapon = False
        if b == 0 and "psO" in TAPS:
            bd._tapon = True
            p.op("vector", lambda e, ps=psOA: e.tensor_copy(out=bd.q2s_0[0:10, 0:512],
                                                            in_=ps[0:10, 0:512]),
                 reads=[pbOA], writes=[bd.B("q2s_0")])
            p.op("vector", lambda e, ps=psOB: e.tensor_copy(out=bd.q2s_0[64:74, 0:512],
                                                            in_=ps[64:74, 0:512]),
                 reads=[pbOB], writes=[bd.B("q2s_0")])
            bd.tap("psO", bd.q2s_0[:, 0:512], bd.B("q2s_0"))
            bd._tapon = False
        if b == 0:
            bd._tapon = True
            bd.tap("full", bd.full[:], fB)
            bd.tap("rv0", bd.rv_all[:, 0:1], bd.B("rv_all"))
            bd._tapon = False
        spB = bd.B("s_sp")
        for mc in range(KC):
            j = mc % 2
            xrv, xrb = getattr(bd, "xres_%d" % j), bd.B("xres_%d" % j)
            p.op("scalar", lambda e, mc=mc, b=b, xrv=xrv: e.dma_start(
                out=xrv[:], in_=xt_r[:, mc, BT[b]:BT[b] + L]),
                writes=[xrb], dma=True)
            ws, wsb = getattr(bd, "wstage_%d" % j), bd.B("wstage_%d" % j)
            p.op("sync", lambda e, mc=mc, ws=ws: e.dma_start(
                out=ws[:], in_=bd.s_wo.ap()[:, :, mc * 128:(mc + 1) * 128]),
                writes=[wsb], extra=bd.prep_tok["wo"], dma=True)
            for tt in range(4):
                ps, pb = bd.ring()
                for k in range(KC):
                    p.op("tensor", lambda e, ps=ps, k=k, tt=tt, ws=ws: e.matmul(
                        ps[0:128, 0:512], ws[:, k, :],
                        bd.full[:, k, tt * 512:(tt + 1) * 512],
                        start=(k == 0), stop=(k == KC - 1)),
                        reads=[wsb, fB], writes=[pb], inc=(k == KC - 1))
                si = 2 * b + tt // 2
                if si == 0:
                    p.op("vector", lambda e, ps=ps, tt=tt, mc=mc, xrv=xrv:
                         e.scalar_tensor_tensor(
                             out=bd.strmG[:, mc, tt * 512:(tt + 1) * 512],
                             in0=ps[0:128, 0:512], scalar=bd.c_bo[:, mc:mc + 1],
                             in1=xrv[:, tt * 512:(tt + 1) * 512], op0=ALU.add, op1=ALU.add),
                         reads=[pb, xrb, bd.B("c_bo")], writes=[bd.B("strmG")])
                    continue
                sj = tt % 2
                ssv, ssb = getattr(bd, "spillscr_%d" % sj), bd.B("spillscr_%d" % sj)
                p.op("vector", lambda e, ps=ps, tt=tt, mc=mc, xrv=xrv, ssv=ssv:
                     e.scalar_tensor_tensor(
                         out=ssv[:], in0=ps[0:128, 0:512], scalar=bd.c_bo[:, mc:mc + 1],
                         in1=xrv[:, tt * 512:(tt + 1) * 512], op0=ALU.add, op1=ALU.add),
                     reads=[pb, xrb, bd.B("c_bo")], writes=[ssb])
                p.op("scalar", lambda e, mc=mc, si=si, tt=tt, ssv=ssv: e.dma_start(
                    out=bd.s_sp.ap()[si][:, mc, (tt % 2) * 512:(tt % 2 + 1) * 512],
                    in_=ssv[:]),
                    reads=[ssb], writes=[spB], dma=True)

    a78(0)
    a78(1)


def _ffn_group(bd, si):
    p = bd.p
    par = si % 2
    strm = bd.strmG if par == 0 else bd.strmB
    sgB = bd.B("strmG" if par == 0 else "strmB")
    hbf = bd.hbuf if par == 0 else bd.hbufB
    hbB = bd.B("hbuf" if par == 0 else "hbufB")
    abf = bd.abuf if par == 0 else bd.abufB
    abB = bd.B("abuf" if par == 0 else "abufB")
    bt = si * GTOK
    if si > 0:
        p.op("scalar", lambda e, si=si, strm=strm: e.dma_start(
            out=strm[:], in_=bd.s_sp.ap()[si]),
            reads=[bd.B("s_sp")], writes=[sgB], dma=True)

    bd.ln_rows(strm[:], sgB)

    bd.tap("strm_a8", strm[:], sgB)

    def to_strm(mc, tt, scr, scb):
        p.op("vector", lambda e, sc=scr, mc=mc, tt=tt: e.tensor_copy(
            out=strm[:, mc, tt * 512:(tt + 1) * 512], in_=sc[:]),
            reads=[scb], writes=[sgB])
    bd.ln_apply(strm[:], sgB, to_strm)
    bd.tap("y1", strm[:], sgB)

    rR, rW = bd.B("r_row"), bd.B("rowscr")
    p.op("vector", lambda e: e.tensor_tensor(out=bd.rowscr[:], in0=bd.r_row[:],
                                             in1=bd.r_row[:], op=ALU.mult),
         reads=[rR], writes=[rW])
    p.op("vector", lambda e: e.tensor_scalar(out=bd.rowscr[:], in0=bd.rowscr[:],
                                             scalar1=-EPS, scalar2=1.0 + EPS,
                                             op0=ALU.mult, op1=ALU.add),
         reads=[rW], writes=[rW])
    p.op("scalar", lambda e: e.activation(out=bd.rowscr[:], in_=bd.rowscr[:],
                                          func=AF.Ln, bias=0.0, scale=1.0),
         reads=[rW], writes=[rW])
    p.op("scalar", lambda e: e.activation(out=bd.rowscr[:], in_=bd.rowscr[:],
                                          func=AF.Exp, bias=0.0, scale=-0.5),
         reads=[rW, bd.B("c_cap")], writes=[rW])
    bd.bcast_row(bd.rowscr[:], rW, GTOK, bd.r1b[:], bd.B("r1b"))
    p.op("vector", lambda e: e.memset(bd.r2b[:], 0.0), writes=[bd.B("r2b")])
    zB = bd.B("zbuf")

    def to_z(mc, tt, scr, scb):
        p.op("gpsimd", lambda e, sc=scr, mc=mc, tt=tt: e.tensor_scalar(
            out=bd.zbuf[:, mc, tt * 512:(tt + 1) * 512], in0=sc[:],
            scalar1=bd.c_l1g[:, mc:mc + 1], scalar2=bd.c_l1b[:, mc:mc + 1],
            op0=ALU.mult, op1=ALU.add),
            reads=[scb, bd.B("c_l1g"), bd.B("c_l1b")], writes=[zB])
    bd.ln_apply(strm[:], sgB, to_z)

    for mc in range(KC):
        wj = mc % 2
        ws, wsb = getattr(bd, "wstage_%d" % wj), bd.B("wstage_%d" % wj)
        p.op("sync", lambda e, mc=mc, ws=ws: e.dma_start(
            out=ws[:], in_=bd.s_awt.ap()[:, :, mc * 128:(mc + 1) * 128]),
            writes=[wsb], extra=bd.prep_tok["awt"], dma=True)
        for tt in range(2):
            ps, pb = bd.ring()
            for k in range(KC):
                p.op("tensor", lambda e, ps=ps, k=k, tt=tt, ws=ws: e.matmul(
                    ps[0:128, 0:512], ws[:, k, :], bd.zbuf[:, k, tt * 512:(tt + 1) * 512],
                    start=(k == 0), stop=(k == KC - 1)),
                    reads=[wsb, zB], writes=[pb], inc=(k == KC - 1))
            p.op("vector", lambda e, ps=ps, mc=mc, tt=tt: e.tensor_tensor(
                out=hbf[:, mc, tt * 512:(tt + 1) * 512],
                in0=strm[:, mc, tt * 512:(tt + 1) * 512], in1=ps[0:128, 0:512],
                op=ALU.add),
                reads=[pb, sgB], writes=[hbB])

    bd.ln_rows(hbf[:], hbB)
    tB = bd.B("tbuf")
    lgp = [bd.ring(), bd.ring()]
    lgB = bd.B("lgsb")

    def to_t(mc, tt, scr, scb):
        t32, t32b = getattr(bd, "t32scr_%d" % tt), bd.B("t32scr_%d" % tt)
        tlo, tlob = getattr(bd, "tloscr_%d" % tt), bd.B("tloscr_%d" % tt)
        sgs, sgsb = getattr(bd, "sgscr_%d" % tt), bd.B("sgscr_%d" % tt)
        p.op("vector", lambda e, sc=scr, mc=mc, t32=t32: e.tensor_scalar(
            out=t32[:], in0=sc[:], scalar1=bd.c_l2g[:, mc:mc + 1],
            scalar2=bd.c_l2b[:, mc:mc + 1], op0=ALU.mult, op1=ALU.add),
            reads=[scb, bd.B("c_l2g"), bd.B("c_l2b")], writes=[t32b])
        p.op("vector", lambda e, mc=mc, tt=tt, t32=t32: e.tensor_copy(
            out=bd.tbuf[:, mc, tt * 512:(tt + 1) * 512], in_=t32[:]),
            reads=[t32b], writes=[tB])
        p.op("gpsimd", lambda e, mc=mc, tt=tt, sgs=sgs: e.tensor_copy(
            out=sgs[:], in_=bd.tbuf[:, mc, tt * 512:(tt + 1) * 512]),
            reads=[tB], writes=[sgsb])
        p.op("vector", lambda e, t32=t32, tlo=tlo, sgs=sgs: e.tensor_tensor(
            out=tlo[:], in0=t32[:], in1=sgs[:], op=ALU.subtract),
            reads=[t32b, sgsb], writes=[tlob])
        ps, pb = lgp[tt]
        passes = [(bd.c_gwhi, bd.B("c_gwhi"), None, tB),
                  (bd.c_gwhi, bd.B("c_gwhi"), tlo, tlob),
                  (bd.c_gwlo, bd.B("c_gwlo"), None, tB)]
        for pi, (wv, wvB, rhs, rhsB) in enumerate(passes):
            fst = (mc == 0 and pi == 0)
            lst = (mc == KC - 1 and pi == 2)
            p.op("tensor", lambda e, ps=ps, wv=wv, mc=mc, tt=tt, rhs=rhs,
                 fst=fst, lst=lst: e.matmul(
                     ps[0:8, 0:512], wv[:, mc * 8:(mc + 1) * 8],
                     bd.tbuf[:, mc, tt * 512:(tt + 1) * 512] if rhs is None else rhs[:],
                     start=fst, stop=lst),
                 reads=[wvB, rhsB], writes=[pb] if lst else (), inc=True)
    bd.ln_apply(hbf[:], hbB, to_t)
    for tt in range(2):
        ps, pb = lgp[tt]
        p.op("vector", lambda e, ps=ps, tt=tt: e.tensor_copy(
            out=bd.lgsb[0:8, tt * 512:(tt + 1) * 512], in_=ps[0:8, 0:512]),
            reads=[pb], writes=[lgB])

    bd.tap("hbuf", hbf[:], hbB)
    bd.tap("tbuf", bd.tbuf[:], tB)
    bd.tap("lgsb", bd.lgsb[:], lgB)
    for mc in range(KC):
        p.op("vector", lambda e, mc=mc: e.tensor_tensor(
            out=strm[:, mc, :], in0=strm[:, mc, :], in1=hbf[:, mc, :],
            op=ALU.add),
            reads=[hbB, sgB], writes=[sgB])

    cfB, ctB, ltB, gcB = bd.B("combfm"), bd.B("combtm"), bd.B("lgtm"), bd.B("gcols")
    for c in range(KC):
        ps, pb = bd.ring()
        p.op("tensor", lambda e, ps=ps, c=c: e.transpose(
            ps[0:128, 0:8], bd.lgsb[0:8, c * 128:(c + 1) * 128], bd.c_idnF[0:8, 0:8]),
            reads=[lgB, bd.B("c_idnF")], writes=[pb])
        p.op("vector", lambda e, ps=ps, c=c: e.tensor_copy(
            out=bd.lgtm[:, c * 8:(c + 1) * 8], in_=ps[0:128, 0:8]),
            reads=[pb], writes=[ltB])
    mx = bd.gcols[:, 0:8]
    iu = bd.gcols[:, 8:16].bitcast(U32)
    e12 = bd.gcols[:, 16:18]
    w1c = bd.gcols[:, 18:19]
    w2c = bd.gcols[:, 19:20]
    dc_ = bd.gcols[:, 20:21]
    for c in range(KC):
        p.op("vector", lambda e, c=c: e.max(out=mx, in_=bd.lgtm[:, c * 8:(c + 1) * 8]),
             reads=[ltB], writes=[gcB])
        p.op("vector", lambda e, c=c: e.max_index(out=iu, in_max=mx,
                                                  in_values=bd.lgtm[:, c * 8:(c + 1) * 8]),
             reads=[ltB, gcB], writes=[gcB])
        p.op("vector", lambda e: e.tensor_copy(out=e12, in_=iu[:, 0:2]),
             reads=[gcB], writes=[gcB])
        p.op("vector", lambda e: e.tensor_tensor(out=dc_, in0=mx[:, 1:2], in1=mx[:, 0:1],
                                                 op=ALU.subtract),
             reads=[gcB], writes=[gcB])
        p.op("scalar", lambda e: e.activation(out=dc_, in_=dc_, func=AF.Exp, bias=0.0,
                                              scale=1.0),
             reads=[gcB], writes=[gcB])
        p.op("vector", lambda e: e.tensor_scalar(out=dc_, in0=dc_, scalar1=1.0, scalar2=None,
                                                 op0=ALU.add),
             reads=[gcB], writes=[gcB])
        p.op("vector", lambda e: e.reciprocal(out=w1c, in_=dc_),
             reads=[gcB], writes=[gcB])
        p.op("vector", lambda e: e.tensor_scalar(out=w2c, in0=w1c, scalar1=-1.0, scalar2=1.0,
                                                 op0=ALU.mult, op1=ALU.add),
             reads=[gcB], writes=[gcB])
        p.op("vector", lambda e, c=c: e.tensor_scalar(
            out=bd.lgtm[:, c * 8:(c + 1) * 8], in0=bd.c_iotaF[:, 0:8],
            scalar1=e12[:, 0:1], scalar2=w1c, op0=ALU.is_equal, op1=ALU.mult),
            reads=[bd.B("c_iotaF"), gcB], writes=[ltB])
        p.op("vector", lambda e, c=c: e.tensor_scalar(
            out=bd.combtm[:, c * 8:(c + 1) * 8], in0=bd.c_iotaF[:, 0:8],
            scalar1=e12[:, 1:2], scalar2=w2c, op0=ALU.is_equal, op1=ALU.mult),
            reads=[bd.B("c_iotaF"), gcB], writes=[ctB])
        p.op("vector", lambda e, c=c: e.tensor_tensor(
            out=bd.combtm[:, c * 8:(c + 1) * 8], in0=bd.combtm[:, c * 8:(c + 1) * 8],
            in1=bd.lgtm[:, c * 8:(c + 1) * 8], op=ALU.add),
            reads=[ctB, ltB], writes=[ctB])
    for c in range(KC):
        ps, pb = bd.ring()
        p.op("tensor", lambda e, ps=ps, c=c: e.transpose(
            ps[0:8, 0:128], bd.combtm[:, c * 8:(c + 1) * 8], bd.c_idnF[:]),
            reads=[ctB, bd.B("c_idnF")], writes=[pb])
        p.op("vector", lambda e, ps=ps, c=c: e.tensor_copy(
            out=bd.combfm[0:8, c * 128:(c + 1) * 128], in_=ps[0:8, 0:128]),
            reads=[pb], writes=[cfB])

    bd.tap("combfm", bd.combfm[:], cfB)
    cbB = bd.B("combbc")
    for e_ in range(NEXP):
        for tt in range(2):
            ps, pb = bd.ring()
            p.op("tensor", lambda e, ps=ps, e_=e_, tt=tt: e.matmul(
                ps[0:128, 0:512], bd.c_basis[:, e_ * 128:(e_ + 1) * 128],
                bd.combfm[0:8, tt * 512:(tt + 1) * 512], start=True, stop=True),
                reads=[cfB, bd.B("c_basis")], writes=[pb])
            p.op("vector", lambda e, ps=ps, tt=tt: e.tensor_copy(
                out=bd.combbc[:, tt * 512:(tt + 1) * 512], in_=ps[0:128, 0:512]),
                reads=[pb], writes=[cbB])
        NH = ICH // 2
        for half in range(2):
            for il in range(NH):
                ic = half * NH + il
                sj = il % 3
                gst, gsb = getattr(bd, "gstage_%d" % sj), bd.B("gstage_%d" % sj)
                ust, usb = getattr(bd, "ustage_%d" % sj), bd.B("ustage_%d" % sj)
                p.op("gpsimd", lambda e, e_=e_, ic=ic, gst=gst: e.dma_start(
                    out=gst[:], in_=bd.gpt.ap()[e_, ic]),
                    writes=[gsb], dma=True)
                p.op("gpsimd", lambda e, e_=e_, ic=ic, ust=ust: e.dma_start(
                    out=ust[:], in_=bd.upt.ap()[e_, ic]),
                    writes=[usb], dma=True)
                gb = [bd.ring(), bd.ring()]
                ub = [bd.ring(), bd.ring()]
                for (stage, stb, banks) in ((gst, gsb, gb), (ust, usb, ub)):
                    for k in range(KC):
                        for tt in range(2):
                            ps, pb = banks[tt]
                            p.op("tensor", lambda e, ps=ps, stg=stage, k=k, tt=tt: e.matmul(
                                ps[0:128, 0:512], stg[:, k, :],
                                bd.tbuf[:, k, tt * 512:(tt + 1) * 512],
                                start=(k == 0), stop=(k == KC - 1)),
                                reads=[stb, tB], writes=[pb], inc=(k == KC - 1))
                for tt in range(2):
                    psg, pbg = gb[tt]
                    psu, pbu = ub[tt]
                    sgs, sgsb = getattr(bd, "sgscr_%d" % tt), bd.B("sgscr_%d" % tt)
                    p.op("scalar", lambda e, ps=psg, sgs=sgs: e.activation(
                        out=sgs[:], in_=ps[0:128, 0:512], func=AF.Silu, bias=0.0, scale=1.0),
                        reads=[pbg], writes=[sgsb])
                    p.op("vector", lambda e, ps=psu, sgs=sgs, il=il, tt=tt: e.tensor_tensor(
                        out=abf[:, il, tt * 512:(tt + 1) * 512], in0=sgs[:],
                        in1=ps[0:128, 0:512], op=ALU.mult),
                        reads=[sgsb, pbu], writes=[abB])
            for dcc in range(KC):
                sj = dcc % 3
                dst, dsb = getattr(bd, "dstage_%d" % sj), bd.B("dstage_%d" % sj)
                p.op("gpsimd", lambda e, e_=e_, dcc=dcc, dst=dst, half=half: e.dma_start(
                    out=dst[:], in_=bd.dpt.ap()[e_, dcc][:, half * NH:(half + 1) * NH, :]),
                    writes=[dsb], dma=True)
                banks = [bd.ring(), bd.ring()]
                for il in range(NH):
                    for tt in range(2):
                        ps, pb = banks[tt]
                        p.op("tensor", lambda e, ps=ps, dst=dst, il=il, tt=tt: e.matmul(
                            ps[0:128, 0:512], dst[:, il, :],
                            abf[:, il, tt * 512:(tt + 1) * 512],
                            start=(il == 0), stop=(il == NH - 1)),
                            reads=[dsb, abB], writes=[pb], inc=(il == NH - 1))
                for tt in range(2):
                    ps, pb = banks[tt]
                    hs, hsb = getattr(bd, "haccscr_%d" % tt), bd.B("haccscr_%d" % tt)
                    p.op("vector", lambda e, ps=ps, hs=hs, tt=tt: e.tensor_tensor(
                        out=hs[:], in0=ps[0:128, 0:512],
                        in1=bd.combbc[:, tt * 512:(tt + 1) * 512], op=ALU.mult),
                        reads=[pb, cbB], writes=[hsb])
                    p.op("vector", lambda e, hs=hs, dcc=dcc, tt=tt: e.tensor_tensor(
                        out=strm[:, dcc, tt * 512:(tt + 1) * 512],
                        in0=strm[:, dcc, tt * 512:(tt + 1) * 512], in1=hs[:], op=ALU.add),
                        reads=[hsb], writes=[sgB])

    bd.tap("strm_pre", strm[:], sgB)
    bd.ln_rows(strm[:], sgB)

    out_r = bd.out.ap().rearrange("(kc q) t -> q kc t", q=128)

    def to_out(mc, tt, scr, scb):
        p.op("vector", lambda e, sc=scr, mc=mc, tt=tt: e.tensor_copy(
            out=strm[:, mc, tt * 512:(tt + 1) * 512], in_=sc[:]),
            reads=[scb], writes=[sgB])
        if tt == 1:
            p.op("sync", lambda e, mc=mc: e.dma_start(
                out=out_r[:, mc, bt:bt + GTOK], in_=strm[:, mc, :]),
                reads=[sgB], writes=[bd.B("out")], dma=True)
    bd.ln_apply(strm[:], sgB, to_out)


def build_program():
    bd = Build()
    p = bd.p
    bd.build_consts()
    bd.build_prep()
    bd._tapon = False
    _attention_merged(bd)
    for si in range(4):
        bd._tapon = (si < 2)
        bd._tapsuf = "" if si == 0 else "_g1"
        _ffn_group(bd, si)
    alldma = [(s, p.cnt[s]) for q in DSEM for s in DSEM[q] if p.cnt[s]]
    p.items["sync"].append((lambda e: e.nop(), alldma, None, 0))
    for s in SEMS:
        assert p.cnt[s] < 60000, (s, p.cnt[s])
    nc = bd.nc
    import contextlib
    with contextlib.ExitStack() as stk:
        sems = {s: stk.enter_context(nc.semaphore(s)) for s in SEMS}
        block = stk.enter_context(nc.Block())
        p.install(block, sems)
    return bd


def kernel(**inputs):
    f = lambda n: np.asarray(inputs[n], np.float32)
    x = f("L0_attn_x")
    pc = lambda v: np.ascontiguousarray(v.reshape(KC, 128).T)
    row = lambda v: np.ascontiguousarray(v.reshape(1, D))
    def wfm(w):
        return np.ascontiguousarray(w.T.reshape(KC, 128, w.shape[0]).transpose(1, 0, 2))

    gp, up, dp = f("L0_ffn_gate_proj"), f("L0_ffn_up_proj"), f("L0_ffn_down_proj")
    com = {
        "wqt": wfm(f("L0_attn_Wq")), "wkt": wfm(f("L0_attn_Wk")),
        "wvt": wfm(f("L0_attn_Wv")), "wot": wfm(f("L0_attn_Wo")),
        "awt": wfm(f("L0_ffn_attn_w")),
        "gwt": np.ascontiguousarray(
            f("L0_ffn_gate_w").T.reshape(KC, 128, NEXP).transpose(1, 0, 2)),
        "gpt": np.ascontiguousarray(
            gp.reshape(NEXP, ICH, 128, KC, 128).transpose(0, 1, 4, 3, 2)),
        "upt": np.ascontiguousarray(
            up.reshape(NEXP, ICH, 128, KC, 128).transpose(0, 1, 4, 3, 2)),
        "dpt": np.ascontiguousarray(
            dp.reshape(NEXP, KC, 128, ICH, 128).transpose(0, 1, 4, 3, 2)),
        "bq_r": row(f("L0_attn_bq")), "bv_r": row(f("L0_attn_bv")),
        "bk_pc": pc(f("L0_attn_bk")), "bo_pc": pc(f("L0_attn_bo")),
        "l1g": pc(f("L0_ffn_ln1_g")), "l1b": pc(f("L0_ffn_ln1_b")),
        "l2g": pc(f("L0_ffn_ln2_g")), "l2b": pc(f("L0_ffn_ln2_b")),
    }
    in_maps = []
    for c in range(N_CORES):
        m = dict(com)
        xc = np.concatenate([x[2 * c].T, x[2 * c + 1].T], axis=1)
        m["xt"] = np.ascontiguousarray(xc.reshape(KC, 128, TOKS).transpose(1, 0, 2))
        in_maps.append(m)
    bd = build_program()
    res = run_bass_kernel_spmd(bd.nc, in_maps, core_ids=list(range(N_CORES)))
    outs = []
    for c in range(N_CORES):
        o = np.asarray(res.results[c]["out"])
        outs.append(o[:, :L].T)
        outs.append(o[:, L:].T)
    return np.ascontiguousarray(np.stack(outs)).astype(np.float32)


# version 20
# speedup vs baseline: 1.0494x; 1.0122x over previous
import numpy as np
import concourse.bass as bass
import concourse.mybir as mybir
from concourse.bass_utils import run_bass_kernel_spmd

F32 = mybir.dt.float32
F16 = mybir.dt.float16
BF16 = mybir.dt.bfloat16
U32 = mybir.dt.uint32
AF = mybir.ActivationFunctionType
ALU = mybir.AluOpType
AX = mybir.AxisListType

N_CORES = 8
B_GLOB, L, D = 16, 2048, 1024
B_LOC = 2
H, DK = 16, 64
NPAIR = 8
KC = 8
NCH = 16
NEXP, INTER = 8, 4096
ICH = 32
GTOK = 1024
TOKS = B_LOC * L
EPS = 1e-5

ENGS = ("sync", "gpsimd", "vector", "scalar", "tensor")
CSEM = {"tensor": "sT", "vector": "sV", "scalar": "sS", "gpsimd": "sP", "sync": "sY"}
NDQ = 8
DSEM = {"sync": ["dS%d" % i for i in range(NDQ)],
        "gpsimd": ["dP%d" % i for i in range(NDQ)],
        "scalar": ["dA%d" % i for i in range(NDQ)]}
SEMS = tuple(["sT", "sV", "sS", "sP", "sY"] + DSEM["sync"] + DSEM["gpsimd"]
             + DSEM["scalar"])

TAPS = {}


class Buf(object):
    __slots__ = ("name", "w", "r", "ov")

    def __init__(self, name):
        self.name = name
        self.w = None
        self.r = []
        self.ov = []


class Prog(object):
    def __init__(self):
        self.items = {e: [] for e in ENGS}
        self.cnt = {s: 0 for s in SEMS}
        self.strict_self = True
        self.drot = {e: 0 for e in DSEM}

    def op(self, eng, fn, reads=(), writes=(), extra=(), inc=True, dma=False):
        deps = [d for d in extra if d is not None]
        for b in reads:
            if b.w is not None:
                deps.append(b.w)
        for b in writes:
            for v in (b,) + tuple(b.ov):
                if v.w is not None:
                    deps.append(v.w)
                deps.extend(v.r)
        if dma:
            sem, amt = DSEM[eng][self.drot[eng] % NDQ], 16
            self.drot[eng] += 1
        elif inc:
            sem, amt = CSEM[eng], 1
        else:
            sem, amt = None, 0
        self.items[eng].append((fn, deps, sem, amt))
        tok = None
        if sem is not None:
            self.cnt[sem] += amt
            tok = (sem, self.cnt[sem])
            for b in reads:
                b.r.append(tok)
            for b in writes:
                for v in (b,) + tuple(b.ov):
                    v.w = tok
                    v.r = []
        return tok

    def install(self, block, sems):
        for name in ENGS:
            items = self.items[name]
            own = CSEM[name]

            def body(eng, items=items, own=own, name=name):
                wm = {s: 0 for s in SEMS}
                for fn, deps, sem, amt in items:
                    best = {}
                    for (s, v) in deps:
                        if s == own and (name == "tensor" or not self.strict_self):
                            continue
                        if v > best.get(s, 0):
                            best[s] = v
                    for s in sorted(best):
                        if best[s] > wm[s]:
                            eng.wait_ge(sems[s], best[s])
                            wm[s] = best[s]
                    inst = fn(eng)
                    if sem is not None:
                        inst.then_inc(sems[sem], amt)

            getattr(block, name)(body)


class Build(object):
    def __init__(self):
        self.nc = bass.Bass("TRN2", target_bir_lowering=False)
        self.p = Prog()
        self.bufs = {}
        self.ranges = {}
        self.prep_tok = {}
        self._mk_io()
        self._mk_sbuf()
        self.CAPV = [1e-9, 1e-5, 0.5, -0.5, 0.125]
        for i, v in enumerate(self.CAPV):
            self.nc.const_aps.register(F32, v, self.c_cap[:, i:i + 1])
        self._link_overlaps()
        self._mk_psum()

    def B(self, name):
        if name not in self.bufs:
            self.bufs[name] = Buf(name)
        return self.bufs[name]

    def _mk_io(self):
        nc = self.nc
        di = lambda n, s: nc.dram_tensor(n, s, F32, kind="ExternalInput")
        self.xt = di("xt", [128, KC, TOKS])
        self.wqt, self.wkt = di("wqt", [128, KC, D]), di("wkt", [128, KC, D])
        self.wvt, self.wot = di("wvt", [128, KC, D]), di("wot", [128, KC, D])
        self.awt = di("awt", [128, KC, D])
        self.bq_r, self.bv_r = di("bq_r", [1, D]), di("bv_r", [1, D])
        self.bk_pc, self.bo_pc = di("bk_pc", [128, KC]), di("bo_pc", [128, KC])
        self.l1g, self.l1b = di("l1g", [128, KC]), di("l1b", [128, KC])
        self.l2g, self.l2b = di("l2g", [128, KC]), di("l2b", [128, KC])
        self.gwt = di("gwt", [128, KC, NEXP])
        self.gpt = di("gpt", [NEXP, ICH, 128, KC, 128])
        self.upt = di("upt", [NEXP, ICH, 128, KC, 128])
        self.dpt = di("dpt", [NEXP, KC, 128, ICH, 128])
        self.out = nc.dram_tensor("out", [D, TOKS], F32, kind="ExternalOutput")
        ds = lambda n, s, t: nc.dram_tensor(n, s, t, kind="Internal")
        self.s_xhi = ds("s_xhi", [128, KC, TOKS], BF16)
        self.s_wqhi = ds("s_wqhi", [128, KC, D], BF16)
        self.s_wk = ds("s_wk", [128, KC, D], BF16)
        self.s_wv = ds("s_wv", [128, KC, D], BF16)
        self.s_wo = ds("s_wo", [128, KC, D], BF16)
        self.s_awt = ds("s_awt", [128, KC, D], BF16)
        self.s_q = [ds("s_q0", [NCH, 128, D], BF16), ds("s_q1", [NCH, 128, D], BF16)]
        self.s_k = [ds("s_k0", [KC, 128, L], BF16), ds("s_k1", [KC, 128, L], BF16)]
        self.s_v = [ds("s_v0", [NCH, 128, D], BF16), ds("s_v1", [NCH, 128, D], BF16)]
        self.s_sp = ds("s_sp", [4, 128, KC, GTOK], F32)

    def _mk_sbuf(self):
        nc = self.nc
        self._cur = 16512
        top = nc.sbuf_top

        def A(name, shape, dtype, off=None):
            nbytes = int(np.prod(shape[1:])) * mybir.dt.size(dtype)
            if off is None:
                off = (self._cur + 63) // 64 * 64
                self._cur = off + nbytes
            assert off + nbytes <= top, (name, off, nbytes, top)
            self.ranges[name] = (off, off + nbytes)
            t = nc.alloc_sbuf_tensor_at(name, list(shape), dtype, offset=off)
            setattr(self, name, t)
            return off

        for nm, sh, dt in [
            ("c_iotaF", [128, L], F16), ("c_iotaPC", [128, 16], F32),
            ("c_idn16", [128, 128], BF16), ("c_idnF", [128, 128], F32),
            ("c_ones_row", [1, 128], F32), ("c_ones_col", [128, 1], F32),
            ("c_basis", [8, 1024], F32), ("c_gwhi", [128, 64], BF16),
            ("c_gwlo", [128, 64], BF16), ("gw32", [128, 64], F32),
            ("c_bk", [128, KC], F32), ("c_bo", [128, KC], F32),
            ("c_l1g", [128, KC], F32), ("c_l1b", [128, KC], F32),
            ("c_l2g", [128, KC], F32), ("c_l2b", [128, KC], F32),
            ("c_jmod", [128, 1], F32),
            ("c_cap", [128, 8], F32),
        ]:
            A(nm, sh, dt)

        R1 = (self._cur + 63) // 64 * 64
        A("strmG", [128, KC, GTOK], F32, R1)
        self._cur = R1 + 32768

        R2 = self._cur
        A("wqhi", [128, KC, D], BF16, R2)
        A("wqlo", [128, KC, D], BF16, R2 + 16384)
        A("wvb", [128, KC, D], BF16, R2 + 32768)
        A("full", [128, KC, L], BF16, R2)
        A("hbuf", [128, KC, GTOK], F32, R2)
        A("zbuf", [128, KC, GTOK], BF16, R2 + 32768)
        A("abuf", [128, ICH // 2, GTOK], BF16, R2)
        A("strmB", [128, KC, GTOK], F32, R2)
        A("hbufB", [128, KC, GTOK], F32, R1)
        A("abufB", [128, ICH // 2, GTOK], BF16, R1)
        self._cur = R2 + 49152

        R3 = self._cur
        A("p_sb", [128, L], BF16, R3)
        A("prT", [128, NCH, NPAIR * 20], BF16, R3 + 4096)
        A("ohT", [128, NCH, H * 10], BF16, R3 + 14336)
        A("oh2", [128, L], BF16, R3 + 24576)
        A("kpair", [128, L], BF16, R3 + 28672)
        A("tbuf", [128, KC, GTOK], BF16, R3)
        A("dstage_0", [128, ICH // 2, 128], BF16, R3 + 16384)
        A("dstage_1", [128, ICH // 2, 128], BF16, R3 + 20480)
        A("gstage_0", [128, KC, 128], BF16, R3 + 32768)
        A("gstage_1", [128, KC, 128], BF16, R3 + 34816)
        A("ustage_0", [128, KC, 128], BF16, R3 + 36864)
        A("ustage_1", [128, KC, 128], BF16, R3 + 38912)
        A("gstage_2", [128, KC, 128], BF16, R3 + 24576)
        A("ustage_2", [128, KC, 128], BF16, R3 + 26624)
        A("dstage_2", [128, ICH // 2, 128], BF16, R3 + 28672)
        A("lgsb", [8, GTOK], F32, R3 + 40960)
        A("combfm", [8, GTOK], F32, R3 + 40960)
        A("combbc", [128, GTOK], F32, R3 + 45056)
        A("q2s_1", [128, D], F32, R3 + 40960)
        A("esc_1", [128, D], F32, R3 + 45056)
        A("stat_1", [128, 16 * 8], F32, R3 + 32768)
        A("qevac_1", [128, D], BF16, R3 + 36864)
        self._cur = R3 + 49152

        R4 = self._cur
        for j in range(2):
            A("xc32_%d" % j, [128, KC, 128], F32, R4 + 4096 * j)
            A("xh32_%d" % j, [128, KC, 128], F32, R4 + 8192 + 4096 * j)
            A("xhic_%d" % j, [128, KC, 128], BF16, R4 + 16384 + 2048 * j)
            A("q32_%d" % j, [128, D], F32, R4 + 24576 + 4096 * j)
            A("xhit_%d" % j, [128, KC, 512], BF16, R4 + 8192 * j)
            A("kevac_%d" % j, [128, 512], BF16, R4 + 20480 + 1024 * j)
            A("vevac_%d" % j, [128, D], BF16, R4 + 18432 + 2048 * j)
            A("qc_%d" % j, [128, D], BF16, R4 + 2048 * j)
            A("vc_%d" % j, [128, D], BF16, R4 + 4096 + 2048 * j)
            A("xres_%d" % j, [128, L], F32, R4 + 8192 + 8192 * j)
            A("sqscr_%d" % j, [128, 512], F32, R4 + 2048 * j)
            A("sgscr_%d" % j, [128, 512], F32, R4 + 4096 + 2048 * j)
            A("t32scr_%d" % j, [128, 512], F32, R4 + 8192 + 2048 * j)
            A("tloscr_%d" % j, [128, 512], BF16, R4 + 12288 + 1024 * j)
            A("haccscr_%d" % j, [128, 512], F32, R4 + 14336 + 2048 * j)
            A("aplscr_%d" % j, [128, 512], F32, R4 + 18432 + 2048 * j)
            A("spillscr_%d" % j, [128, 512], F32, R4 + 0 + 2048 * j)
        A("qevac_0", [128, D], BF16, R4 + 40960)
        A("q2s_0", [128, D], F32, R4 + 32768)
        A("esc_0", [128, D], F32, R4 + 36864)
        A("lgtm", [128, 64], F32, R4 + 28672)
        A("combtm", [128, 64], F32, R4 + 28928)
        A("gcols", [128, 24], F32, R4 + 29184)
        A("p_sb2", [128, L], BF16, R4 + 20480)
        A("prT2", [128, NCH, NPAIR * 20], BF16, 201216)
        A("kpair2", [128, L], BF16, R4 + 0)
        A("r1b", [128, GTOK], F32, R4 + 32768)
        A("r2b", [128, GTOK], F32, R4 + 36864)
        self._cur = R4 + 43008

        R5 = self._cur
        A("qs0", [16, L], F32, R5)
        A("bqv_b", [128, D], F32, R5 + 8192)
        A("browscr", [1, D], F32, R5 + 12288)
        A("idxflat", [1, H * 10], F32, R5 + 12288)
        A("thr", [1, 8], F32, R5 + 12928)
        A("s_row", [1, GTOK], F32, R5)
        A("qq_row", [1, GTOK], F32, R5 + 4096)
        A("r_row", [1, GTOK], F32, R5 + 8192)
        A("rowscr", [1, GTOK], F32, R5 + 12992)
        self._cur = R5 + 12992 + 4096

        for nm, sh, dt in [
            ("wstage_0", [128, KC, 128], BF16), ("wstage_1", [128, KC, 128], BF16),
            ("m4", [128, 4], F32), ("mcol", [128, 1], F32), ("mbias", [128, 1], F32),
            ("rscol", [128, 1], F32), ("rv_all", [128, NPAIR], F32),
            ("valid", [128, 1], F32), ("vtmp", [128, 1], F32),
            ("idxT", [128, 16], F32), ("idxf", [16, 10], F32),
            ("i8", [16, 8], U32), ("i2", [16, 8], U32),
            ("t8a", [16, 8], F32), ("t8b", [16, 8], F32),
            ("idx_bc", [128, H * 10], F32),
            ("scat_0", [128, 128], BF16), ("scat_1", [128, 128], BF16),
            ("stat_0", [128, 16 * 8], F32),
            ("urow", [1, 8], F32),
            ("idxT2", [128, 16], F32), ("valid2", [128, 1], F32),
            ("rv_all2", [128, NPAIR], F32), ("qspT2", [128, NPAIR * 10], BF16),
            ("qspT", [128, NPAIR * 10], BF16),
        ]:
            A(nm, sh, dt)
        A("qs0_1x", [16, L], F32, self.ranges["strmG"][0])
        R1_, R2_ = self.ranges["strmG"][0], self.ranges["wqhi"][0]
        A("xhic_2", [128, KC, 128], BF16, R1_ + 8192)
        A("qevac_2", [128, D], BF16, R1_ + 10240)
        A("q32_2", [128, D], F32, R1_ + 12288)
        A("q2s_2", [128, D], F32, R1_ + 16384)
        A("esc_2", [128, D], F32, R1_ + 20480)
        A("stat_2", [128, 16 * 8], F32, R1_ + 24576)
        A("stat_3", [128, 16 * 8], F32, R1_ + 25088)
        A("vevac_2", [128, D], BF16, R1_ + 10240)
        A("xhic_3", [128, KC, 128], BF16, R2_ + 32768)
        A("qevac_3", [128, D], BF16, R2_ + 34816)
        A("q32_3", [128, D], F32, R2_ + 36864)
        A("q2s_3", [128, D], F32, R2_ + 40960)
        A("esc_3", [128, D], F32, R2_ + 45056)
        A("vevac_3", [128, D], BF16, R2_ + 34816)
        assert self._cur <= self.nc.sbuf_top, (self._cur, self.nc.sbuf_top)

    def _link_overlaps(self):
        names = list(self.ranges)
        for i, a in enumerate(names):
            sa, ea = self.ranges[a]
            ba = self.B(a)
            for b_ in names[i + 1:]:
                sb, eb = self.ranges[b_]
                if sa < eb and sb < ea:
                    bb = self.B(b_)
                    ba.ov.append(bb)
                    bb.ov.append(ba)

    def _mk_psum(self):
        nc = self.nc
        self.PS = [nc.alloc_psum_tensor("PS%d" % i, [128, 512], F32) for i in range(6)]
        self.PT = [nc.alloc_psum_tensor("PT%d" % i, [128, 256], BF16) for i in range(2)]
        self.psb = [self.B("PS%d" % i) for i in range(6)]
        self.ptb = [self.B("PT%d" % i) for i in range(2)]
        self._ring = 0
        self._ringw = 4
        self._tring = 0

    def tap(self, name, ap, buf):
        name = name + getattr(self, "_tapsuf", "")
        if not getattr(self, "_tapon", False) or name not in TAPS:
            return
        t = self.nc.dram_tensor("tap_" + name, list(ap.shape), ap.dtype,
                                kind="ExternalOutput")
        self.p.op("sync", lambda e, t=t, ap=ap: e.dma_start(out=t.ap(), in_=ap),
                  reads=[buf], dma=True)

    def ring(self):
        i = self._ring % self._ringw
        self._ring += 1
        return self.PS[i], self.psb[i]

    def xbank(self):
        return self.PS[5], self.psb[5]

    def xbank2(self):
        return self.PS[4], self.psb[4]

    def tring(self):
        i = self._tring % 2
        self._tring += 1
        return self.PT[i], self.ptb[i]

    def build_consts(self):
        p = self.p
        scr = self.strmG[:].rearrange("q a b -> q (a b)")
        bA = self.B("strmG")
        t1, t2 = scr[:, 0:128], scr[:, 128:256]
        tb8 = scr[0:8, 256:1280]
        tbg = scr[:, 1280:1344]
        iF, iPC = self.B("c_iotaF"), self.B("c_iotaPC")
        for i, v in enumerate(self.CAPV):
            p.op("vector", lambda e, i=i, v=v: e.memset(self.c_cap[:, i:i + 1], v),
                 writes=[self.B("c_cap")])
        p.op("gpsimd", lambda e: e.iota(self.c_iotaF[:], [[1, L]], channel_multiplier=0,
                                        allow_small_or_imprecise_dtypes=True), writes=[iF])
        p.op("gpsimd", lambda e: e.iota(self.c_iotaPC[:], [[128, 16]], channel_multiplier=1,
                                        allow_small_or_imprecise_dtypes=True), writes=[iPC])
        p.op("gpsimd", lambda e: e.iota(t1, [[0, 128]], channel_multiplier=1,
                                        allow_small_or_imprecise_dtypes=True), writes=[bA])
        p.op("gpsimd", lambda e: e.iota(t2, [[1, 128]], channel_multiplier=0,
                                        allow_small_or_imprecise_dtypes=True), writes=[bA])
        p.op("vector", lambda e: e.tensor_tensor(out=self.c_idnF[:], in0=t1, in1=t2,
                                                 op=ALU.is_equal),
             reads=[bA], writes=[self.B("c_idnF")])
        p.op("vector", lambda e: e.tensor_copy(out=self.c_idn16[:], in_=self.c_idnF[:]),
             reads=[self.B("c_idnF")], writes=[self.B("c_idn16")])
        p.op("vector", lambda e: e.memset(self.c_ones_row[:], 1.0), writes=[self.B("c_ones_row")])
        p.op("vector", lambda e: e.memset(self.c_ones_col[:], 1.0), writes=[self.B("c_ones_col")])
        p.op("gpsimd", lambda e: e.iota(tb8.rearrange("q (a b) -> q a b", b=128),
                                        [[1, 8], [0, 128]], channel_multiplier=0,
                                        allow_small_or_imprecise_dtypes=True), writes=[bA])
        p.op("vector", lambda e: e.tensor_scalar(out=self.c_basis[:], in0=tb8,
                                                 scalar1=self.c_iotaPC[0:8, 0:1], scalar2=None,
                                                 op0=ALU.is_equal),
             reads=[bA, iPC], writes=[self.B("c_basis")])
        p.op("vector", lambda e: e.tensor_scalar(out=self.c_jmod[:], in0=self.c_iotaPC[:, 0:1],
                                                 scalar1=64.0, scalar2=None, op0=ALU.is_ge),
             reads=[iPC], writes=[self.B("c_jmod")])
        p.op("vector", lambda e: e.scalar_tensor_tensor(out=self.c_jmod[:], in0=self.c_jmod[:],
                                                        scalar=-64.0, in1=self.c_iotaPC[:, 0:1],
                                                        op0=ALU.mult, op1=ALU.add),
             reads=[iPC, self.B("c_jmod")], writes=[self.B("c_jmod")])
        for (name, src, dst) in [("c_bk", self.bk_pc, self.c_bk), ("c_bo", self.bo_pc, self.c_bo),
                                 ("c_l1g", self.l1g, self.c_l1g), ("c_l1b", self.l1b, self.c_l1b),
                                 ("c_l2g", self.l2g, self.c_l2g), ("c_l2b", self.l2b, self.c_l2b)]:
            p.op("scalar", lambda e, s=src, d=dst: e.dma_start(out=d[:], in_=s.ap()),
                 writes=[self.B(name)], dma=True)
        p.op("scalar", lambda e: e.dma_start(
            out=self.gw32[:].rearrange("q (kc ex) -> q kc ex", ex=8),
            in_=self.gwt.ap()),
            writes=[self.B("gw32")], dma=True)
        p.op("vector", lambda e: e.tensor_copy(out=self.c_gwhi[:], in_=self.gw32[:]),
             reads=[self.B("gw32")], writes=[self.B("c_gwhi")])
        p.op("vector", lambda e: e.tensor_copy(out=tbg, in_=self.c_gwhi[:]),
             reads=[self.B("c_gwhi")], writes=[bA])
        p.op("vector", lambda e: e.tensor_tensor(out=self.c_gwlo[:], in0=self.gw32[:],
                                                 in1=tbg, op=ALU.subtract),
             reads=[bA, self.B("gw32")], writes=[self.B("c_gwlo")])
        p.op("vector", lambda e: e.memset(self.oh2[:], 0.0), writes=[self.B("oh2")])
        p.op("vector", lambda e: e.memset(self.p_sb[:], 0.0), writes=[self.B("p_sb")])
        for i in range(6):
            p.op("vector", lambda e, i=i: e.memset(self.PS[i][:], 0.0), writes=[self.psb[i]])


    def build_prep(self):
        p = self.p
        casts = [("xhi", self.s_xhi.ap(), self.xt.ap()),
                 ("wqhi", self.s_wqhi.ap(), self.wqt.ap()),
                 ("wk", self.s_wk.ap(), self.wkt.ap()),
                 ("wv", self.s_wv.ap(), self.wvt.ap()),
                 ("wo", self.s_wo.ap(), self.wot.ap()),
                 ("awt", self.s_awt.ap(), self.awt.ap())]
        def fence():
            return [(s, p.cnt[s]) for s in DSEM["gpsimd"] if p.cnt[s]]

        for name, dst, src in casts:
            p.op("gpsimd", lambda e, d=dst, s=src: e.dma_start(out=d, in_=s), dma=True)
        ftok = fence()
        for name, _, _ in casts:
            self.prep_tok[name] = ftok

    def bcast_row(self, src_ap, src_buf, width, dst_ap, dst_buf):
        p = self.p
        for t in range((width + 511) // 512):
            w = min(512, width - t * 512)
            ps, pb = self.ring()
            p.op("tensor", lambda e, ps=ps, t=t, w=w, s=src_ap: e.matmul(
                ps[0:128, 0:w], self.c_ones_row[:], s[0:1, t * 512:t * 512 + w],
                start=True, stop=True),
                reads=[src_buf, self.B("c_ones_row")], writes=[pb])
            p.op("vector", lambda e, ps=ps, t=t, w=w, d=dst_ap: e.tensor_copy(
                out=d[:, t * 512:t * 512 + w], in_=ps[0:128, 0:w]),
                reads=[pb], writes=[dst_buf])

    def ln_rows(self, src, src_buf):
        p = self.p
        sb = [self.ring() for _ in range(2)]
        qb = [self.ring() for _ in range(2)]
        for tt in range(2):
            ps, pb = sb[tt]
            for mc in range(KC):
                p.op("tensor", lambda e, ps=ps, mc=mc, tt=tt, s=src: e.matmul(
                    ps[0:1, 0:512], self.c_ones_col[:], s[:, mc, tt * 512:(tt + 1) * 512],
                    start=(mc == 0), stop=(mc == KC - 1)),
                    reads=[src_buf, self.B("c_ones_col")], writes=[pb], inc=(mc == KC - 1))
            psq, pbq = qb[tt]
            for mc in range(KC):
                scr = getattr(self, "sqscr_%d" % (mc % 2))
                scb = self.B("sqscr_%d" % (mc % 2))
                p.op("vector", lambda e, sc=scr, mc=mc, tt=tt, s=src: e.tensor_tensor(
                    out=sc[:], in0=s[:, mc, tt * 512:(tt + 1) * 512],
                    in1=s[:, mc, tt * 512:(tt + 1) * 512], op=ALU.mult),
                    reads=[src_buf], writes=[scb])
                p.op("tensor", lambda e, ps=psq, sc=scr, mc=mc: e.matmul(
                    ps[0:1, 0:512], self.c_ones_col[:], sc[:],
                    start=(mc == 0), stop=(mc == KC - 1)),
                    reads=[scb, self.B("c_ones_col")],
                    writes=[pbq] if mc == KC - 1 else (), inc=True)
        rS, rQ, rR, rW = self.B("s_row"), self.B("qq_row"), self.B("r_row"), self.B("rowscr")
        for tt in range(2):
            ps, pb = sb[tt]
            p.op("vector", lambda e, ps=ps, tt=tt: e.tensor_scalar(
                out=self.s_row[0:1, tt * 512:(tt + 1) * 512], in0=ps[0:1, 0:512],
                scalar1=1.0 / D, scalar2=None, op0=ALU.mult), reads=[pb], writes=[rS])
            ps, pb = qb[tt]
            p.op("vector", lambda e, ps=ps, tt=tt: e.tensor_scalar(
                out=self.qq_row[0:1, tt * 512:(tt + 1) * 512], in0=ps[0:1, 0:512],
                scalar1=1.0 / D, scalar2=None, op0=ALU.mult), reads=[pb], writes=[rQ])
        p.op("vector", lambda e: e.tensor_tensor(out=self.r_row[:], in0=self.s_row[:],
                                                 in1=self.s_row[:], op=ALU.mult),
             reads=[rS], writes=[rR])
        p.op("vector", lambda e: e.tensor_tensor(out=self.r_row[:], in0=self.qq_row[:],
                                                 in1=self.r_row[:], op=ALU.subtract),
             reads=[rQ, rR], writes=[rR])
        p.op("scalar", lambda e: e.activation(out=self.r_row[:], in_=self.r_row[:],
                                              func=AF.Ln, bias=EPS, scale=1.0),
             reads=[rR, self.B("c_cap")], writes=[rR])
        p.op("scalar", lambda e: e.activation(out=self.r_row[:], in_=self.r_row[:],
                                              func=AF.Exp, bias=0.0, scale=-0.5),
             reads=[rR, self.B("c_cap")], writes=[rR])
        p.op("vector", lambda e: e.tensor_tensor(out=self.rowscr[:], in0=self.s_row[:],
                                                 in1=self.r_row[:], op=ALU.mult),
             reads=[rS, rR], writes=[rW])
        self.bcast_row(self.r_row[:], rR, GTOK, self.r1b[:], self.B("r1b"))
        self.bcast_row(self.rowscr[:], rW, GTOK, self.r2b[:], self.B("r2b"))

    def ln_apply(self, src, src_buf, emit_dst):
        p = self.p
        for mc in range(KC):
            for tt in range(2):
                scr = getattr(self, "aplscr_%d" % tt)
                scb = self.B("aplscr_%d" % tt)
                p.op("vector", lambda e, sc=scr, mc=mc, tt=tt, s=src: e.tensor_tensor(
                    out=sc[:], in0=s[:, mc, tt * 512:(tt + 1) * 512],
                    in1=self.r1b[:, tt * 512:(tt + 1) * 512], op=ALU.mult),
                    reads=[src_buf, self.B("r1b")], writes=[scb])
                p.op("vector", lambda e, sc=scr, tt=tt: e.tensor_tensor(
                    out=sc[:], in0=sc[:], in1=self.r2b[:, tt * 512:(tt + 1) * 512],
                    op=ALU.subtract),
                    reads=[scb, self.B("r2b")], writes=[scb])
                emit_dst(mc, tt, scr, scb)


def _attention_merged(bd):
    p = bd.p
    xt_r = bd.xt.ap()
    BT = [0, L]

    wqhiB, wqloB = bd.B("wqhi"), bd.B("wqlo")
    p.op("scalar", lambda e: e.dma_start(out=bd.wqhi[:], in_=bd.s_wqhi.ap()),
         writes=[wqhiB], extra=bd.prep_tok["wqhi"], dma=True)
    for k in range(KC):
        j = k % 2
        c32, c32b = getattr(bd, "xc32_%d" % j), bd.B("xc32_%d" % j)
        h32, h32b = getattr(bd, "xh32_%d" % j), bd.B("xh32_%d" % j)
        c32f = c32[:].rearrange("q a b -> q (a b)")
        h32f = h32[:].rearrange("q a b -> q (a b)")
        p.op("scalar", lambda e, k=k, c32f=c32f: e.dma_start(
            out=c32f, in_=bd.wqt.ap()[:, k, :]),
            writes=[c32b], dma=True)
        p.op("vector", lambda e, k=k, h32f=h32f: e.tensor_copy(
            out=h32f, in_=bd.wqhi[:, k, :]),
            reads=[wqhiB], writes=[h32b])
        p.op("vector", lambda e, k=k, c32f=c32f, h32f=h32f: e.tensor_tensor(
            out=bd.wqlo[:, k, :], in0=c32f, in1=h32f, op=ALU.subtract),
            reads=[c32b, h32b], writes=[wqloB])

    brB = bd.B("browscr")
    p.op("scalar", lambda e: e.dma_start(out=bd.browscr[:], in_=bd.bq_r.ap()),
         writes=[brB], dma=True)
    bd.bcast_row(bd.browscr[:], brB, D, bd.bqv_b[:], bd.B("bqv_b"))

    QS = [bd.qs0, bd.qs0_1x]
    QSB = [bd.B("qs0"), bd.B("qs0_1x")]

    def mk_kblock(tt, m):
        def em():
            if m == 0:
                for b_ in range(2):
                    xht, xhtb = getattr(bd, "xhit_%d" % b_), bd.B("xhit_%d" % b_)
                    p.op("scalar", lambda e, tt=tt, b_=b_, xht=xht: e.dma_start(
                        out=xht[:],
                        in_=bd.s_xhi.ap()[:, :, BT[b_] + tt * 512: BT[b_] + (tt + 1) * 512]),
                        writes=[xhtb], extra=bd.prep_tok["xhi"], dma=True)
            wj = (tt * KC + m) % 2
            ws, wsb = getattr(bd, "wstage_%d" % wj), bd.B("wstage_%d" % wj)
            p.op("sync", lambda e, m=m, ws=ws: e.dma_start(
                out=ws[:], in_=bd.s_wk.ap()[:, :, m * 128:(m + 1) * 128]),
                writes=[wsb], extra=bd.prep_tok["wk"], dma=True)
            for b_ in range(2):
                xht, xhtb = getattr(bd, "xhit_%d" % b_), bd.B("xhit_%d" % b_)
                ps, pb = bd.ring()
                for k in range(KC):
                    p.op("tensor", lambda e, ps=ps, k=k, ws=ws, xht=xht: e.matmul(
                        ps[0:128, 0:512], ws[:, k, :], xht[:, k, :],
                        start=(k == 0), stop=(k == KC - 1)),
                        reads=[wsb, xhtb], writes=[pb], inc=(k == KC - 1))
                kev, kevb = getattr(bd, "kevac_%d" % b_), bd.B("kevac_%d" % b_)
                p.op("vector", lambda e, ps=ps, m=m, kev=kev: e.tensor_scalar(
                    out=kev[:], in0=ps[0:128, 0:512], scalar1=bd.c_bk[:, m:m + 1],
                    scalar2=None, op0=ALU.add),
                    reads=[pb, bd.B("c_bk")], writes=[kevb])
                p.op("scalar", lambda e, m=m, tt=tt, b_=b_, kev=kev: e.dma_start(
                    out=bd.s_k[b_].ap()[m][:, tt * 512:(tt + 1) * 512], in_=kev[:]),
                    reads=[kevb], writes=[bd.B("s_k%d" % b_)], dma=True)
        return em

    kblocks = [mk_kblock(tt_, m_) for tt_ in range(4) for m_ in range(KC)]

    def v_pre():
        p.op("scalar", lambda e: e.dma_start(out=bd.browscr[:], in_=bd.bv_r.ap()),
             writes=[brB], dma=True)
        bd.bcast_row(bd.browscr[:], brB, D, bd.bqv_b[:], bd.B("bqv_b"))
        p.op("scalar", lambda e: e.dma_start(out=bd.wvb[:], in_=bd.s_wv.ap()),
             writes=[bd.B("wvb")], extra=bd.prep_tok["wv"], dma=True)

    def mk_vblock(c, b):
        def em():
            j = (2 * c + b) % 3
            xhi, xhib = getattr(bd, "xhic_%d" % j), bd.B("xhic_%d" % j)
            p.op("scalar", lambda e, c=c, b=b, xhi=xhi: e.dma_start(
                out=xhi[:], in_=bd.s_xhi.ap()[:, :, BT[b] + c * 128: BT[b] + (c + 1) * 128]),
                writes=[xhib], extra=bd.prep_tok["xhi"], dma=True)
            halves = [bd.ring(), bd.ring()]
            for hf in range(2):
                ps, pb = halves[hf]
                for k in range(KC):
                    p.op("tensor", lambda e, ps=ps, k=k, hf=hf, xhi=xhi: e.matmul(
                        ps[0:128, 0:512], xhi[:, k, :],
                        bd.wvb[:, k, hf * 512:(hf + 1) * 512],
                        start=(k == 0), stop=(k == KC - 1)),
                        reads=[xhib, bd.B("wvb")], writes=[pb], inc=(k == KC - 1))
            vev, vevb = getattr(bd, "vevac_%d" % j), bd.B("vevac_%d" % j)
            for hf in range(2):
                ps, pb = halves[hf]
                p.op("vector", lambda e, ps=ps, hf=hf, vev=vev: e.tensor_tensor(
                    out=vev[:, hf * 512:(hf + 1) * 512], in0=ps[0:128, 0:512],
                    in1=bd.bqv_b[:, hf * 512:(hf + 1) * 512], op=ALU.add),
                    reads=[pb, bd.B("bqv_b")], writes=[vevb])
            p.op("scalar", lambda e, c=c, b=b, vev=vev: e.dma_start(
                out=bd.s_v[b].ap()[c], in_=vev[:]),
                reads=[vevb], writes=[bd.B("s_v%d" % b)], dma=True)
        return em

    vblocks = [mk_vblock(c_, b_) for c_ in range(NCH) for b_ in range(2)]

    bd._ringw = 6
    pend_st = []
    for c in range(NCH):
        for b in range(2):
            j = (2 * c + b) % 4
            st = getattr(bd, "stat_%d" % j)
            stB = bd.B("stat_%d" % j)
            q2t, q2B = getattr(bd, "q2s_%d" % j), bd.B("q2s_%d" % j)
            est, escB = getattr(bd, "esc_%d" % j), bd.B("esc_%d" % j)
            xhi, xhib = getattr(bd, "xhic_%d" % j), bd.B("xhic_%d" % j)
            q32, q32b = getattr(bd, "q32_%d" % j), bd.B("q32_%d" % j)
            qev, qevb = getattr(bd, "qevac_%d" % j), bd.B("qevac_%d" % j)
            bt = BT[b]
            p.op("scalar", lambda e, c=c, bt=bt, xhi=xhi: e.dma_start(
                out=xhi[:], in_=bd.s_xhi.ap()[:, :, bt + c * 128: bt + (c + 1) * 128]),
                writes=[xhib], extra=bd.prep_tok["xhi"], dma=True)
            halves = [bd.ring(), bd.ring()]
            for hf in range(2):
                ps, pb = halves[hf]
                passes = [(bd.wqhi, wqhiB), (bd.wqlo, wqloB)]
                for pi, (wb, wbb) in enumerate(passes):
                    for k in range(KC):
                        fst = (pi == 0 and k == 0)
                        lst = (pi == 1 and k == KC - 1)
                        p.op("tensor", lambda e, ps=ps, xhi=xhi, wb=wb, k=k, hf=hf,
                             fst=fst, lst=lst: e.matmul(
                                 ps[0:128, 0:512], xhi[:, k, :],
                                 wb[:, k, hf * 512:(hf + 1) * 512], start=fst, stop=lst),
                             reads=[xhib, wbb], writes=[pb], inc=lst)
            for hf in range(2):
                ps, pb = halves[hf]
                p.op("vector", lambda e, ps=ps, hf=hf, q32=q32: e.tensor_tensor(
                    out=q32[:, hf * 512:(hf + 1) * 512], in0=ps[0:128, 0:512],
                    in1=bd.bqv_b[:, hf * 512:(hf + 1) * 512], op=ALU.add),
                    reads=[pb, bd.B("bqv_b")], writes=[q32b])
            p.op("gpsimd", lambda e, qev=qev, q32=q32: e.tensor_copy(
                out=qev[:], in_=q32[:]),
                reads=[q32b], writes=[qevb])
            p.op("scalar", lambda e, c=c, b=b, qev=qev: e.dma_start(
                out=bd.s_q[b].ap()[c], in_=qev[:]),
                reads=[qevb], writes=[bd.B("s_q%d" % b)], dma=True)

            def mk_stats(c=c, b=b, st=st, stB=stB, q2t=q2t, q2B=q2B, est=est,
                         escB=escB, q32=q32, q32b=q32b):
                mq, sq, sqq, ent, l2t, var, qsc, tmp = (
                    st[:, i * 16:(i + 1) * 16] for i in range(8))
                q3 = q32[:].rearrange("q (h dk) -> q h dk", dk=DK)
                q23 = q2t[:].rearrange("q (h dk) -> q h dk", dk=DK)
                e3 = est[:].rearrange("q (h dk) -> q h dk", dk=DK)
                steps = [
                    lambda: p.op("gpsimd", lambda e: e.tensor_tensor(
                        out=q2t[:], in0=q32[:], in1=q32[:], op=ALU.mult),
                        reads=[q32b], writes=[q2B]),
                    lambda: p.op("vector", lambda e: e.reduce_sum(
                        out=sq, in_=q3, axis=AX.X), reads=[q32b], writes=[stB]),
                    lambda: p.op("vector", lambda e: e.reduce_sum(
                        out=sqq, in_=q23, axis=AX.X), reads=[q2B], writes=[stB]),
                    lambda: p.op("vector", lambda e: e.reduce_max(
                        out=mq, in_=q3, axis=AX.X), reads=[q32b], writes=[stB]),
                    lambda: p.op("vector", lambda e: e.tensor_tensor(
                        out=e3, in0=q3, in1=mq.unsqueeze(2).to_broadcast([128, H, DK]),
                        op=ALU.subtract), reads=[q32b, stB], writes=[escB]),
                    lambda: p.op("scalar", lambda e: e.activation(
                        out=q2t[:], in_=est[:], func=AF.Exp, bias=0.0, scale=1.0),
                        reads=[escB], writes=[q2B]),
                    lambda: p.op("vector", lambda e: e.reduce_sum(
                        out=tmp, in_=q23, axis=AX.X), reads=[q2B], writes=[stB]),
                    lambda: p.op("vector", lambda e: e.reciprocal(out=tmp, in_=tmp),
                                 reads=[stB], writes=[stB]),
                    lambda: p.op("vector", lambda e: e.tensor_tensor(
                        out=q23, in0=q23, in1=tmp.unsqueeze(2).to_broadcast([128, H, DK]),
                        op=ALU.mult), reads=[q2B, stB], writes=[q2B]),
                    lambda: p.op("scalar", lambda e: e.activation(
                        out=est[:], in_=q2t[:], func=AF.Ln, bias=1e-9, scale=1.0),
                        reads=[q2B, bd.B("c_cap")], writes=[escB]),
                    lambda: p.op("gpsimd", lambda e: e.tensor_tensor(
                        out=est[:], in0=q2t[:], in1=est[:], op=ALU.mult),
                        reads=[q2B, escB], writes=[escB]),
                    lambda: p.op("vector", lambda e: e.reduce_sum(
                        out=ent, in_=e3, axis=AX.X, negate=True),
                        reads=[escB], writes=[stB]),
                    lambda: p.op("scalar", lambda e: e.activation(
                        out=l2t, in_=sqq, func=AF.Ln, bias=0.0, scale=1.0),
                        reads=[stB], writes=[stB]),
                    lambda: p.op("scalar", lambda e: e.activation(
                        out=l2t, in_=l2t, func=AF.Exp, bias=0.0, scale=0.5),
                        reads=[stB, bd.B("c_cap")], writes=[stB]),
                    lambda: p.op("vector", lambda e: e.tensor_tensor(
                        out=var, in0=sq, in1=sq, op=ALU.mult), reads=[stB], writes=[stB]),
                    lambda: p.op("vector", lambda e: e.scalar_tensor_tensor(
                        out=var, in0=var, scalar=-1.0 / DK, in1=sqq,
                        op0=ALU.mult, op1=ALU.add), reads=[stB], writes=[stB]),
                    lambda: p.op("vector", lambda e: e.tensor_scalar(
                        out=qsc, in0=l2t, scalar1=0.5, scalar2=None, op0=ALU.mult),
                        reads=[stB], writes=[stB]),
                    lambda: p.op("vector", lambda e: e.scalar_tensor_tensor(
                        out=qsc, in0=ent, scalar=0.3, in1=qsc, op0=ALU.mult, op1=ALU.add),
                        reads=[stB], writes=[stB]),
                    lambda: p.op("vector", lambda e: e.scalar_tensor_tensor(
                        out=qsc, in0=var, scalar=0.2 / (DK - 1), in1=qsc,
                        op0=ALU.mult, op1=ALU.add),
                        reads=[stB], writes=[stB]),
                ]

                def tp_step():
                    ps, pb = bd.ring()
                    p.op("tensor", lambda e, ps=ps: e.transpose(
                        ps[0:16, 0:128], qsc, bd.c_idnF[:]),
                        reads=[stB, bd.B("c_idnF")], writes=[pb])
                    p.op("vector", lambda e, ps=ps: e.tensor_copy(
                        out=QS[b][0:16, c * 128:(c + 1) * 128], in_=ps[0:16, 0:128]),
                        reads=[pb], writes=[QSB[b]])
                steps.append(tp_step)
                return steps
            if kblocks:
                kblocks.pop(0)()
            pend_st.append(mk_stats())
            if len(pend_st) == 4:
                nst = max(len(s_) for s_ in pend_st)
                for si_ in range(nst):
                    for ch_ in pend_st:
                        if si_ < len(ch_):
                            ch_[si_]()
                pend_st = []
    for ch_ in pend_st:
        for f_ in ch_:
            f_()
    bd._ringw = 4

    IDXT = [bd.idxT, bd.idxT2]
    VALID = [bd.valid, bd.valid2]
    QSP = [bd.qspT, bd.qspT2]

    def a4(b):
        qs, qsB = QS[b], QSB[b]
        idxTb = bd.B("idxT%s" % ("" if b == 0 else "2"))
        validb = bd.B("valid%s" % ("" if b == 0 else "2"))
        iB = bd.B("idxf")
        p.op("vector", lambda e, qs=qs: e.max(out=bd.t8a[:], in_=qs[0:16, :]),
             reads=[qsB], writes=[bd.B("t8a")])
        p.op("vector", lambda e, qs=qs: e.max_index(out=bd.i8[:], in_max=bd.t8a[:],
                                                    in_values=qs[0:16, :]),
             reads=[qsB, bd.B("t8a")], writes=[bd.B("i8")])
        uB = bd.B("urow")
        xr0, xr0b = bd.xres_0, bd.B("xres_0")
        p.op("vector", lambda e, qs=qs: e.reduce_sum(out=bd.urow[0:1, 0:1],
                                                     in_=qs[0:1, :], axis=AX.X),
             reads=[qsB], writes=[uB])
        p.op("vector", lambda e, qs=qs, xr0=xr0: e.tensor_tensor(
            out=xr0[0:1, :], in0=qs[0:1, :], in1=qs[0:1, :], op=ALU.mult),
            reads=[qsB], writes=[xr0b])
        p.op("vector", lambda e, xr0=xr0: e.reduce_sum(out=bd.urow[0:1, 1:2],
                                                       in_=xr0[0:1, :], axis=AX.X),
             reads=[xr0b], writes=[uB])
        p.op("vector", lambda e: e.tensor_scalar(out=bd.urow[0:1, 2:3], in0=bd.urow[0:1, 0:1],
                                                 scalar1=1.0 / L, scalar2=None, op0=ALU.mult),
             reads=[uB], writes=[uB])
        p.op("vector", lambda e: e.tensor_tensor(out=bd.urow[0:1, 3:4], in0=bd.urow[0:1, 0:1],
                                                 in1=bd.urow[0:1, 2:3], op=ALU.mult),
             reads=[uB], writes=[uB])
        p.op("vector", lambda e: e.tensor_tensor(out=bd.urow[0:1, 3:4], in0=bd.urow[0:1, 1:2],
                                                 in1=bd.urow[0:1, 3:4], op=ALU.subtract),
             reads=[uB], writes=[uB])
        p.op("vector", lambda e: e.tensor_scalar(out=bd.urow[0:1, 3:4], in0=bd.urow[0:1, 3:4],
                                                 scalar1=1.0 / (L - 1), scalar2=None,
                                                 op0=ALU.mult),
             reads=[uB], writes=[uB])
        p.op("scalar", lambda e: e.activation(out=bd.urow[0:1, 4:5], in_=bd.urow[0:1, 3:4],
                                              func=AF.Ln, bias=0.0, scale=1.0),
             reads=[uB], writes=[uB])
        p.op("scalar", lambda e: e.activation(out=bd.urow[0:1, 4:5], in_=bd.urow[0:1, 4:5],
                                              func=AF.Exp, bias=0.0, scale=0.5),
             reads=[uB, bd.B("c_cap")], writes=[uB])
        p.op("vector", lambda e: e.tensor_scalar(out=bd.urow[0:1, 5:6], in0=bd.urow[0:1, 2:3],
                                                 scalar1=1e-6, scalar2=None, op0=ALU.add),
             reads=[uB], writes=[uB])
        p.op("vector", lambda e: e.reciprocal(out=bd.urow[0:1, 5:6], in_=bd.urow[0:1, 5:6]),
             reads=[uB], writes=[uB])
        p.op("vector", lambda e: e.tensor_tensor(out=bd.urow[0:1, 6:7], in0=bd.urow[0:1, 4:5],
                                                 in1=bd.urow[0:1, 5:6], op=ALU.mult),
             reads=[uB], writes=[uB])
        p.op("vector", lambda e: e.tensor_scalar(out=bd.thr[0:1, 0:1], in0=bd.urow[0:1, 6:7],
                                                 scalar1=10.0, scalar2=-0.5, op0=ALU.mult,
                                                 op1=ALU.add),
             reads=[uB], writes=[bd.B("thr")])
        ps, pb = bd.ring()
        p.op("tensor", lambda e, ps=ps: e.matmul(ps[0:128, 0:1], bd.c_ones_row[:],
                                                 bd.thr[0:1, 0:1], start=True, stop=True),
             reads=[bd.B("thr"), bd.B("c_ones_row")], writes=[pb])
        p.op("vector", lambda e, ps=ps: e.tensor_tensor(out=bd.vtmp[:], in0=ps[0:128, 0:1],
                                                        in1=bd.c_jmod[:], op=ALU.is_gt),
             reads=[pb, bd.B("c_jmod")], writes=[bd.B("vtmp")])
        p.op("vector", lambda e, b=b: e.tensor_scalar(out=VALID[b][:], in0=bd.c_jmod[:],
                                                      scalar1=2.5, scalar2=None,
                                                      op0=ALU.is_lt),
             reads=[bd.B("c_jmod")], writes=[validb])
        p.op("vector", lambda e, b=b: e.tensor_tensor(out=VALID[b][:], in0=VALID[b][:],
                                                      in1=bd.vtmp[:], op=ALU.max),
             reads=[bd.B("vtmp"), validb], writes=[validb])
        p.op("vector", lambda e, qs=qs: e.match_replace(out=qs[0:16, :], in_to_replace=bd.t8a[:],
                                                        in_values=qs[0:16, :],
                                                        imm_value=-1e30),
             reads=[bd.B("t8a")], writes=[qsB])
        p.op("vector", lambda e, qs=qs: e.max(out=bd.t8b[:], in_=qs[0:16, :]),
             reads=[qsB], writes=[bd.B("t8b")])
        p.op("vector", lambda e, qs=qs: e.max_index(out=bd.i2[:], in_max=bd.t8b[:],
                                                    in_values=qs[0:16, :]),
             reads=[qsB, bd.B("t8b")], writes=[bd.B("i2")])
        p.op("vector", lambda e: e.tensor_copy(out=bd.idxf[:, 0:8], in_=bd.i8[:]),
             reads=[bd.B("i8")], writes=[iB])
        p.op("vector", lambda e: e.tensor_copy(out=bd.idxf[:, 8:10], in_=bd.i2[:, 0:2]),
             reads=[bd.B("i2")], writes=[iB])
        ps, pb = bd.ring()
        p.op("tensor", lambda e, ps=ps: e.transpose(
            ps[0:10, 0:16], bd.idxf[:], bd.c_idnF[0:16, 0:16]),
            reads=[iB, bd.B("c_idnF")], writes=[pb])
        p.op("vector", lambda e, ps=ps, b=b: e.tensor_copy(
            out=IDXT[b][0:10, :], in_=ps[0:10, 0:16]),
            reads=[pb], writes=[idxTb])
        p.op("scalar", lambda e, b=b: e.dma_start(out=IDXT[b][64:74, :], in_=IDXT[b][0:10, :]),
             reads=[idxTb], writes=[idxTb], dma=True)
        p.op("scalar", lambda e: e.dma_start(out=bd.idxflat[:], in_=bd.idxf[:]),
             reads=[iB], writes=[bd.B("idxflat")], dma=True)
        ps, pb = bd.ring()
        p.op("tensor", lambda e, ps=ps: e.matmul(ps[0:128, 0:160], bd.c_ones_row[:],
                                                 bd.idxflat[:], start=True, stop=True),
             reads=[bd.B("idxflat"), bd.B("c_ones_row")], writes=[pb])
        p.op("vector", lambda e, ps=ps: e.tensor_copy(out=bd.idx_bc[:], in_=ps[0:128, 0:160]),
             reads=[pb], writes=[bd.B("idx_bc")])
        for c in range(NCH):
            p.op("vector", lambda e, c=c: e.tensor_scalar(
                out=bd.ohT[:, c, :], in0=bd.idx_bc[:], scalar1=bd.c_iotaPC[:, c:c + 1],
                scalar2=None, op0=ALU.is_equal),
                reads=[bd.B("idx_bc"), bd.B("c_iotaPC")], writes=[bd.B("ohT")])

    def a5(b):
        psGA, pbGA = bd.xbank()
        psGB, pbGB = bd.xbank2()
        qspb = bd.B("qspT%s" % ("" if b == 0 else "2"))
        for c in range(NCH):
            j = c % 2
            qcv, qcb = getattr(bd, "qc_%d" % j), bd.B("qc_%d" % j)
            p.op("sync", lambda e, c=c, b=b, qcv=qcv: e.dma_start(
                out=qcv[:], in_=bd.s_q[b].ap()[c]),
                reads=[bd.B("s_q%d" % b)], writes=[qcb], dma=True)
            for pr in range(NPAIR):
                for hh in range(2):
                    h = 2 * pr + hh
                    fst = (c == 0 and pr == 0)
                    lst = (c == NCH - 1 and pr == NPAIR - 1)
                    ctok = (pr == NPAIR - 1)
                    ps = psGA if hh == 0 else psGB
                    pb = pbGA if hh == 0 else pbGB
                    kw = {} if hh == 0 else {"tile_position": (0, 64)}
                    base = 0 if hh == 0 else 64
                    p.op("tensor", lambda e, ps=ps, c=c, qcv=qcv, pr=pr, h=h, base=base,
                         kw=kw, fst=fst, lst=lst: e.matmul(
                             ps[base:base + 64, pr * 10:(pr + 1) * 10],
                             qcv[:, h * 64:(h + 1) * 64],
                             bd.ohT[:, c, h * 10:(h + 1) * 10],
                             start=fst, stop=lst, **kw),
                         reads=[qcb, bd.B("ohT")], writes=[pb] if lst else (),
                         inc=(lst or ctok))
        p.op("vector", lambda e, ps=psGA, b=b: e.tensor_copy(
            out=QSP[b][0:64, :], in_=ps[0:64, 0:80]),
            reads=[pbGA], writes=[qspb])
        p.op("vector", lambda e, ps=psGB, b=b: e.tensor_copy(
            out=QSP[b][64:128, :], in_=ps[64:128, 0:80]),
            reads=[pbGB], writes=[qspb])

    a4(0)
    bd._tapon = True
    bd._tapsuf = ""
    bd.tap("idxf", bd.idxf[:], bd.B("idxf"))
    bd.tap("qs0", bd.qs0[0:16, :], QSB[0])
    bd.tap("idxT", bd.idxT[:], bd.B("idxT"))
    bd.tap("valid", bd.valid[:], bd.B("valid"))
    bd._tapon = False
    a5(0)
    v_pre()
    for _ in range(12):
        vblocks.pop(0)()
    a4(1)
    for _ in range(10):
        vblocks.pop(0)()
    a5(1)
    while vblocks:
        vblocks.pop(0)()
    bd._ringw = 4

    bd._ringw = 4
    PSB = [bd.p_sb, bd.p_sb2]
    PSBB = [bd.B("p_sb"), bd.B("p_sb2")]
    PRT = [bd.prT, bd.prT2]
    PRTB = [bd.B("prT"), bd.B("prT2")]
    KP = [bd.kpair, bd.kpair2]
    KPB = [bd.B("kpair"), bd.B("kpair2")]
    RV = [bd.rv_all, bd.rv_all2]
    RVB = [bd.B("rv_all"), bd.B("rv_all2")]
    for pr in range(NPAIR):
        for b in range(2):
            st = getattr(bd, "stat_%d" % b)
            stB = bd.B("stat_%d" % b)
            m4, mcol, mbias = st[:, 0:4], st[:, 4:5], st[:, 5:6]
            kp, kpb = KP[b], KPB[b]
            p.op("sync", lambda e, pr=pr, b=b, kp=kp: e.dma_start(
                out=kp[:], in_=bd.s_k[b].ap()[pr]),
                reads=[bd.B("s_k%d" % b)], writes=[kpb], dma=True)
            banks = [bd.ring() for _ in range(4)]
            for jj in range(4):
                ps, pb = banks[jj]
                for hh in range(2):
                    base = 0 if hh == 0 else 64
                    tp = (0, 0) if hh == 0 else (64, 64)
                    p.op("tensor", lambda e, ps=ps, jj=jj, pr=pr, b=b, base=base, tp=tp,
                         kp=kp: e.matmul(
                             ps[base:base + 10, 0:512],
                             QSP[b][base:base + 64, pr * 10:(pr + 1) * 10],
                             kp[base:base + 64, jj * 512:(jj + 1) * 512],
                             start=True, stop=True, tile_position=tp),
                         reads=[bd.B("qspT%s" % ("" if b == 0 else "2")), kpb],
                         writes=[pb])
            for jj in range(4):
                ps, pb = banks[jj]
                p.op("vector", lambda e, ps=ps, jj=jj, m4=m4: e.reduce_max(
                    out=m4[:, jj:jj + 1], in_=ps[0:128, 0:512], axis=AX.X),
                    reads=[pb], writes=[stB])
            p.op("vector", lambda e, m4=m4, mcol=mcol: e.reduce_max(
                out=mcol, in_=m4, axis=AX.X), reads=[stB], writes=[stB])
            p.op("vector", lambda e, mcol=mcol, mbias=mbias: e.tensor_scalar(
                out=mbias, in0=mcol, scalar1=-0.125, scalar2=None, op0=ALU.mult),
                reads=[stB], writes=[stB])
            for jj in range(4):
                ps, pb = banks[jj]
                p.op("scalar", lambda e, ps=ps, jj=jj, b=b, mbias=mbias, m4=m4: e.activation(
                    out=PSB[b][:, jj * 512:(jj + 1) * 512], in_=ps[0:128, 0:512],
                    func=AF.Exp, bias=mbias, scale=0.125, accum_out=m4[:, jj:jj + 1]),
                    reads=[pb, stB, bd.B("c_cap")], writes=[PSBB[b], stB])
            p.op("vector", lambda e, m4=m4, mcol=mcol: e.reduce_sum(
                out=mcol, in_=m4, axis=AX.X), reads=[stB], writes=[stB])
            p.op("vector", lambda e, mcol=mcol: e.reciprocal(out=mcol, in_=mcol),
                 reads=[stB], writes=[stB])
            p.op("vector", lambda e, pr=pr, b=b, mcol=mcol: e.tensor_tensor(
                out=RV[b][:, pr:pr + 1], in0=mcol, in1=VALID[b][:], op=ALU.mult),
                reads=[stB, bd.B("valid%s" % ("" if b == 0 else "2"))], writes=[RVB[b]])
            for c in range(NCH):
                pt, ptb = bd.tring()
                p.op("tensor", lambda e, pt=pt, c=c, b=b: e.transpose(
                    pt[0:128, 0:128], PSB[b][:, c * 128:(c + 1) * 128], bd.c_idn16[:]),
                    reads=[PSBB[b], bd.B("c_idn16")], writes=[ptb])
                p.op("vector", lambda e, pt=pt, c=c, pr=pr, b=b: e.tensor_copy(
                    out=PRT[b][:, c, pr * 20:pr * 20 + 10], in_=pt[0:128, 0:10]),
                    reads=[ptb], writes=[PRTB[b]])
                p.op("vector", lambda e, pt=pt, c=c, pr=pr, b=b: e.tensor_copy(
                    out=PRT[b][:, c, pr * 20 + 10:pr * 20 + 20], in_=pt[0:128, 64:74]),
                    reads=[ptb], writes=[PRTB[b]])

    def a78(b):
        psOA, pbOA = bd.xbank()
        psOB, pbOB = bd.xbank2()
        for c in range(NCH):
            j = c % 2
            vcv, vcb = getattr(bd, "vc_%d" % j), bd.B("vc_%d" % j)
            p.op("sync", lambda e, c=c, b=b, vcv=vcv: e.dma_start(
                out=vcv[:], in_=bd.s_v[b].ap()[c]),
                reads=[bd.B("s_v%d" % b)], writes=[vcb], dma=True)
            for pr in range(NPAIR):
                for hh in range(2):
                    fst = (c == 0 and pr == 0)
                    lst = (c == NCH - 1 and pr == NPAIR - 1)
                    ctok = (pr == NPAIR - 1)
                    ps = psOA if hh == 0 else psOB
                    pb = pbOA if hh == 0 else pbOB
                    base = 0 if hh == 0 else 64
                    kw = {} if hh == 0 else {"tile_position": (0, 64)}
                    p.op("tensor", lambda e, ps=ps, c=c, vcv=vcv, pr=pr, hh=hh, b=b,
                         base=base, kw=kw, fst=fst, lst=lst: e.matmul(
                             ps[base:base + 10, pr * 64:(pr + 1) * 64],
                             PRT[b][:, c, pr * 20 + hh * 10: pr * 20 + (hh + 1) * 10],
                             vcv[:, pr * 128 + hh * 64: pr * 128 + (hh + 1) * 64],
                             start=fst, stop=lst, **kw),
                         reads=[PRTB[b], vcb], writes=[pb] if lst else (),
                         inc=(lst or ctok))
        fB = bd.B("full")
        for pr in range(NPAIR):
            sj = pr % 2
            sc, sb = getattr(bd, "scat_%d" % sj), bd.B("scat_%d" % sj)
            p.op("vector", lambda e, sc=sc: e.memset(sc[:], 0.0), writes=[sb])
            p.op("vector", lambda e, ps=psOA, pr=pr, sc=sc, b=b: e.tensor_scalar(
                out=sc[0:10, 0:64], in0=ps[0:10, pr * 64:(pr + 1) * 64],
                scalar1=RV[b][0:10, pr:pr + 1], scalar2=None, op0=ALU.mult),
                reads=[pbOA, RVB[b]], writes=[sb])
            p.op("vector", lambda e, ps=psOB, pr=pr, sc=sc, b=b: e.tensor_scalar(
                out=sc[64:74, 64:128], in0=ps[64:74, pr * 64:(pr + 1) * 64],
                scalar1=RV[b][64:74, pr:pr + 1], scalar2=None, op0=ALU.mult),
                reads=[pbOB, RVB[b]], writes=[sb])
            ohB = bd.B("oh2")
            p.op("vector", lambda e, pr=pr, b=b: e.tensor_scalar(
                out=bd.oh2[0:10, :], in0=bd.c_iotaF[0:10, :],
                scalar1=IDXT[b][0:10, 2 * pr:2 * pr + 1], scalar2=None, op0=ALU.is_equal),
                reads=[bd.B("c_iotaF"), bd.B("idxT%s" % ("" if b == 0 else "2"))],
                writes=[ohB])
            p.op("vector", lambda e, pr=pr, b=b: e.tensor_scalar(
                out=bd.oh2[64:74, :], in0=bd.c_iotaF[64:74, :],
                scalar1=IDXT[b][64:74, 2 * pr + 1:2 * pr + 2], scalar2=None,
                op0=ALU.is_equal),
                reads=[bd.B("c_iotaF"), bd.B("idxT%s" % ("" if b == 0 else "2"))],
                writes=[ohB])
            for jj in range(4):
                ps, pb = bd.ring()
                p.op("tensor", lambda e, ps=ps, jj=jj, sc=sc: e.matmul(
                    ps[0:128, 0:512], sc[:], bd.oh2[:, jj * 512:(jj + 1) * 512],
                    start=True, stop=True),
                    reads=[sb, ohB], writes=[pb])
                p.op("vector", lambda e, ps=ps, jj=jj, pr=pr: e.tensor_copy(
                    out=bd.full[:, pr, jj * 512:(jj + 1) * 512], in_=ps[0:128, 0:512]),
                    reads=[pb], writes=[fB])
        if b == 0:
            bd._tapon = True
            bd.tap("prT", PRT[0][:], PRTB[0])
            bd._tapon = False
        if b == 0 and "psO" in TAPS:
            bd._tapon = True
            p.op("vector", lambda e, ps=psOA: e.tensor_copy(out=bd.q2s_0[0:10, 0:512],
                                                            in_=ps[0:10, 0:512]),
                 reads=[pbOA], writes=[bd.B("q2s_0")])
            p.op("vector", lambda e, ps=psOB: e.tensor_copy(out=bd.q2s_0[64:74, 0:512],
                                                            in_=ps[64:74, 0:512]),
                 reads=[pbOB], writes=[bd.B("q2s_0")])
            bd.tap("psO", bd.q2s_0[:, 0:512], bd.B("q2s_0"))
            bd._tapon = False
        if b == 0:
            bd._tapon = True
            bd.tap("full", bd.full[:], fB)
            bd.tap("rv0", bd.rv_all[:, 0:1], bd.B("rv_all"))
            bd._tapon = False
        spB = bd.B("s_sp")
        for mc in range(KC):
            j = mc % 2
            xrv, xrb = getattr(bd, "xres_%d" % j), bd.B("xres_%d" % j)
            p.op("scalar", lambda e, mc=mc, b=b, xrv=xrv: e.dma_start(
                out=xrv[:], in_=xt_r[:, mc, BT[b]:BT[b] + L]),
                writes=[xrb], dma=True)
            ws, wsb = getattr(bd, "wstage_%d" % j), bd.B("wstage_%d" % j)
            p.op("sync", lambda e, mc=mc, ws=ws: e.dma_start(
                out=ws[:], in_=bd.s_wo.ap()[:, :, mc * 128:(mc + 1) * 128]),
                writes=[wsb], extra=bd.prep_tok["wo"], dma=True)
            for tt in range(4):
                ps, pb = bd.ring()
                for k in range(KC):
                    p.op("tensor", lambda e, ps=ps, k=k, tt=tt, ws=ws: e.matmul(
                        ps[0:128, 0:512], ws[:, k, :],
                        bd.full[:, k, tt * 512:(tt + 1) * 512],
                        start=(k == 0), stop=(k == KC - 1)),
                        reads=[wsb, fB], writes=[pb], inc=(k == KC - 1))
                si = 2 * b + tt // 2
                if si == 0:
                    p.op("vector", lambda e, ps=ps, tt=tt, mc=mc, xrv=xrv:
                         e.scalar_tensor_tensor(
                             out=bd.strmG[:, mc, tt * 512:(tt + 1) * 512],
                             in0=ps[0:128, 0:512], scalar=bd.c_bo[:, mc:mc + 1],
                             in1=xrv[:, tt * 512:(tt + 1) * 512], op0=ALU.add, op1=ALU.add),
                         reads=[pb, xrb, bd.B("c_bo")], writes=[bd.B("strmG")])
                    continue
                sj = tt % 2
                ssv, ssb = getattr(bd, "spillscr_%d" % sj), bd.B("spillscr_%d" % sj)
                p.op("vector", lambda e, ps=ps, tt=tt, mc=mc, xrv=xrv, ssv=ssv:
                     e.scalar_tensor_tensor(
                         out=ssv[:], in0=ps[0:128, 0:512], scalar=bd.c_bo[:, mc:mc + 1],
                         in1=xrv[:, tt * 512:(tt + 1) * 512], op0=ALU.add, op1=ALU.add),
                     reads=[pb, xrb, bd.B("c_bo")], writes=[ssb])
                p.op("scalar", lambda e, mc=mc, si=si, tt=tt, ssv=ssv: e.dma_start(
                    out=bd.s_sp.ap()[si][:, mc, (tt % 2) * 512:(tt % 2 + 1) * 512],
                    in_=ssv[:]),
                    reads=[ssb], writes=[spB], dma=True)

    a78(0)
    a78(1)


def _ffn_group(bd, si):
    p = bd.p
    par = si % 2
    strm = bd.strmG if par == 0 else bd.strmB
    sgB = bd.B("strmG" if par == 0 else "strmB")
    hbf = bd.hbuf if par == 0 else bd.hbufB
    hbB = bd.B("hbuf" if par == 0 else "hbufB")
    abf = bd.abuf if par == 0 else bd.abufB
    abB = bd.B("abuf" if par == 0 else "abufB")
    bt = si * GTOK
    if si > 0:
        p.op("scalar", lambda e, si=si, strm=strm: e.dma_start(
            out=strm[:], in_=bd.s_sp.ap()[si]),
            reads=[bd.B("s_sp")], writes=[sgB], dma=True)

    bd.ln_rows(strm[:], sgB)

    bd.tap("strm_a8", strm[:], sgB)

    def to_strm(mc, tt, scr, scb):
        p.op("vector", lambda e, sc=scr, mc=mc, tt=tt: e.tensor_copy(
            out=strm[:, mc, tt * 512:(tt + 1) * 512], in_=sc[:]),
            reads=[scb], writes=[sgB])
    bd.ln_apply(strm[:], sgB, to_strm)
    bd.tap("y1", strm[:], sgB)

    rR, rW = bd.B("r_row"), bd.B("rowscr")
    p.op("vector", lambda e: e.tensor_tensor(out=bd.rowscr[:], in0=bd.r_row[:],
                                             in1=bd.r_row[:], op=ALU.mult),
         reads=[rR], writes=[rW])
    p.op("vector", lambda e: e.tensor_scalar(out=bd.rowscr[:], in0=bd.rowscr[:],
                                             scalar1=-EPS, scalar2=1.0 + EPS,
                                             op0=ALU.mult, op1=ALU.add),
         reads=[rW], writes=[rW])
    p.op("scalar", lambda e: e.activation(out=bd.rowscr[:], in_=bd.rowscr[:],
                                          func=AF.Ln, bias=0.0, scale=1.0),
         reads=[rW], writes=[rW])
    p.op("scalar", lambda e: e.activation(out=bd.rowscr[:], in_=bd.rowscr[:],
                                          func=AF.Exp, bias=0.0, scale=-0.5),
         reads=[rW, bd.B("c_cap")], writes=[rW])
    bd.bcast_row(bd.rowscr[:], rW, GTOK, bd.r1b[:], bd.B("r1b"))
    p.op("vector", lambda e: e.memset(bd.r2b[:], 0.0), writes=[bd.B("r2b")])
    zB = bd.B("zbuf")

    def to_z(mc, tt, scr, scb):
        p.op("gpsimd", lambda e, sc=scr, mc=mc, tt=tt: e.tensor_scalar(
            out=bd.zbuf[:, mc, tt * 512:(tt + 1) * 512], in0=sc[:],
            scalar1=bd.c_l1g[:, mc:mc + 1], scalar2=bd.c_l1b[:, mc:mc + 1],
            op0=ALU.mult, op1=ALU.add),
            reads=[scb, bd.B("c_l1g"), bd.B("c_l1b")], writes=[zB])
    bd.ln_apply(strm[:], sgB, to_z)

    for mc in range(KC):
        wj = mc % 2
        ws, wsb = getattr(bd, "wstage_%d" % wj), bd.B("wstage_%d" % wj)
        p.op("sync", lambda e, mc=mc, ws=ws: e.dma_start(
            out=ws[:], in_=bd.s_awt.ap()[:, :, mc * 128:(mc + 1) * 128]),
            writes=[wsb], extra=bd.prep_tok["awt"], dma=True)
        for tt in range(2):
            ps, pb = bd.ring()
            for k in range(KC):
                p.op("tensor", lambda e, ps=ps, k=k, tt=tt, ws=ws: e.matmul(
                    ps[0:128, 0:512], ws[:, k, :], bd.zbuf[:, k, tt * 512:(tt + 1) * 512],
                    start=(k == 0), stop=(k == KC - 1)),
                    reads=[wsb, zB], writes=[pb], inc=(k == KC - 1))
            p.op("vector", lambda e, ps=ps, mc=mc, tt=tt: e.tensor_tensor(
                out=hbf[:, mc, tt * 512:(tt + 1) * 512],
                in0=strm[:, mc, tt * 512:(tt + 1) * 512], in1=ps[0:128, 0:512],
                op=ALU.add),
                reads=[pb, sgB], writes=[hbB])

    bd.ln_rows(hbf[:], hbB)
    tB = bd.B("tbuf")
    lgp = [bd.ring(), bd.ring()]
    lgB = bd.B("lgsb")

    def to_t(mc, tt, scr, scb):
        t32, t32b = getattr(bd, "t32scr_%d" % tt), bd.B("t32scr_%d" % tt)
        tlo, tlob = getattr(bd, "tloscr_%d" % tt), bd.B("tloscr_%d" % tt)
        sgs, sgsb = getattr(bd, "sgscr_%d" % tt), bd.B("sgscr_%d" % tt)
        p.op("vector", lambda e, sc=scr, mc=mc, t32=t32: e.tensor_scalar(
            out=t32[:], in0=sc[:], scalar1=bd.c_l2g[:, mc:mc + 1],
            scalar2=bd.c_l2b[:, mc:mc + 1], op0=ALU.mult, op1=ALU.add),
            reads=[scb, bd.B("c_l2g"), bd.B("c_l2b")], writes=[t32b])
        p.op("vector", lambda e, mc=mc, tt=tt, t32=t32: e.tensor_copy(
            out=bd.tbuf[:, mc, tt * 512:(tt + 1) * 512], in_=t32[:]),
            reads=[t32b], writes=[tB])
        p.op("gpsimd", lambda e, mc=mc, tt=tt, sgs=sgs: e.tensor_copy(
            out=sgs[:], in_=bd.tbuf[:, mc, tt * 512:(tt + 1) * 512]),
            reads=[tB], writes=[sgsb])
        p.op("vector", lambda e, t32=t32, tlo=tlo, sgs=sgs: e.tensor_tensor(
            out=tlo[:], in0=t32[:], in1=sgs[:], op=ALU.subtract),
            reads=[t32b, sgsb], writes=[tlob])
        ps, pb = lgp[tt]
        passes = [(bd.c_gwhi, bd.B("c_gwhi"), None, tB),
                  (bd.c_gwhi, bd.B("c_gwhi"), tlo, tlob),
                  (bd.c_gwlo, bd.B("c_gwlo"), None, tB)]
        for pi, (wv, wvB, rhs, rhsB) in enumerate(passes):
            fst = (mc == 0 and pi == 0)
            lst = (mc == KC - 1 and pi == 2)
            p.op("tensor", lambda e, ps=ps, wv=wv, mc=mc, tt=tt, rhs=rhs,
                 fst=fst, lst=lst: e.matmul(
                     ps[0:8, 0:512], wv[:, mc * 8:(mc + 1) * 8],
                     bd.tbuf[:, mc, tt * 512:(tt + 1) * 512] if rhs is None else rhs[:],
                     start=fst, stop=lst),
                 reads=[wvB, rhsB], writes=[pb] if lst else (), inc=True)
    bd.ln_apply(hbf[:], hbB, to_t)
    for tt in range(2):
        ps, pb = lgp[tt]
        p.op("vector", lambda e, ps=ps, tt=tt: e.tensor_copy(
            out=bd.lgsb[0:8, tt * 512:(tt + 1) * 512], in_=ps[0:8, 0:512]),
            reads=[pb], writes=[lgB])

    bd.tap("hbuf", hbf[:], hbB)
    bd.tap("tbuf", bd.tbuf[:], tB)
    bd.tap("lgsb", bd.lgsb[:], lgB)
    for mc in range(KC):
        p.op("vector", lambda e, mc=mc: e.tensor_tensor(
            out=strm[:, mc, :], in0=strm[:, mc, :], in1=hbf[:, mc, :],
            op=ALU.add),
            reads=[hbB, sgB], writes=[sgB])

    cfB, ctB, ltB, gcB = bd.B("combfm"), bd.B("combtm"), bd.B("lgtm"), bd.B("gcols")
    for c in range(KC):
        ps, pb = bd.ring()
        p.op("tensor", lambda e, ps=ps, c=c: e.transpose(
            ps[0:128, 0:8], bd.lgsb[0:8, c * 128:(c + 1) * 128], bd.c_idnF[0:8, 0:8]),
            reads=[lgB, bd.B("c_idnF")], writes=[pb])
        p.op("vector", lambda e, ps=ps, c=c: e.tensor_copy(
            out=bd.lgtm[:, c * 8:(c + 1) * 8], in_=ps[0:128, 0:8]),
            reads=[pb], writes=[ltB])
    mx = bd.gcols[:, 0:8]
    iu = bd.gcols[:, 8:16].bitcast(U32)
    e12 = bd.gcols[:, 16:18]
    w1c = bd.gcols[:, 18:19]
    w2c = bd.gcols[:, 19:20]
    dc_ = bd.gcols[:, 20:21]
    for c in range(KC):
        p.op("vector", lambda e, c=c: e.max(out=mx, in_=bd.lgtm[:, c * 8:(c + 1) * 8]),
             reads=[ltB], writes=[gcB])
        p.op("vector", lambda e, c=c: e.max_index(out=iu, in_max=mx,
                                                  in_values=bd.lgtm[:, c * 8:(c + 1) * 8]),
             reads=[ltB, gcB], writes=[gcB])
        p.op("vector", lambda e: e.tensor_copy(out=e12, in_=iu[:, 0:2]),
             reads=[gcB], writes=[gcB])
        p.op("vector", lambda e: e.tensor_tensor(out=dc_, in0=mx[:, 1:2], in1=mx[:, 0:1],
                                                 op=ALU.subtract),
             reads=[gcB], writes=[gcB])
        p.op("scalar", lambda e: e.activation(out=dc_, in_=dc_, func=AF.Exp, bias=0.0,
                                              scale=1.0),
             reads=[gcB], writes=[gcB])
        p.op("vector", lambda e: e.tensor_scalar(out=dc_, in0=dc_, scalar1=1.0, scalar2=None,
                                                 op0=ALU.add),
             reads=[gcB], writes=[gcB])
        p.op("vector", lambda e: e.reciprocal(out=w1c, in_=dc_),
             reads=[gcB], writes=[gcB])
        p.op("vector", lambda e: e.tensor_scalar(out=w2c, in0=w1c, scalar1=-1.0, scalar2=1.0,
                                                 op0=ALU.mult, op1=ALU.add),
             reads=[gcB], writes=[gcB])
        p.op("vector", lambda e, c=c: e.tensor_scalar(
            out=bd.lgtm[:, c * 8:(c + 1) * 8], in0=bd.c_iotaF[:, 0:8],
            scalar1=e12[:, 0:1], scalar2=w1c, op0=ALU.is_equal, op1=ALU.mult),
            reads=[bd.B("c_iotaF"), gcB], writes=[ltB])
        p.op("vector", lambda e, c=c: e.tensor_scalar(
            out=bd.combtm[:, c * 8:(c + 1) * 8], in0=bd.c_iotaF[:, 0:8],
            scalar1=e12[:, 1:2], scalar2=w2c, op0=ALU.is_equal, op1=ALU.mult),
            reads=[bd.B("c_iotaF"), gcB], writes=[ctB])
        p.op("vector", lambda e, c=c: e.tensor_tensor(
            out=bd.combtm[:, c * 8:(c + 1) * 8], in0=bd.combtm[:, c * 8:(c + 1) * 8],
            in1=bd.lgtm[:, c * 8:(c + 1) * 8], op=ALU.add),
            reads=[ctB, ltB], writes=[ctB])
    for c in range(KC):
        ps, pb = bd.ring()
        p.op("tensor", lambda e, ps=ps, c=c: e.transpose(
            ps[0:8, 0:128], bd.combtm[:, c * 8:(c + 1) * 8], bd.c_idnF[:]),
            reads=[ctB, bd.B("c_idnF")], writes=[pb])
        p.op("vector", lambda e, ps=ps, c=c: e.tensor_copy(
            out=bd.combfm[0:8, c * 128:(c + 1) * 128], in_=ps[0:8, 0:128]),
            reads=[pb], writes=[cfB])

    bd.tap("combfm", bd.combfm[:], cfB)
    cbB = bd.B("combbc")
    for e_ in range(NEXP):
        for tt in range(2):
            ps, pb = bd.ring()
            p.op("tensor", lambda e, ps=ps, e_=e_, tt=tt: e.matmul(
                ps[0:128, 0:512], bd.c_basis[:, e_ * 128:(e_ + 1) * 128],
                bd.combfm[0:8, tt * 512:(tt + 1) * 512], start=True, stop=True),
                reads=[cfB, bd.B("c_basis")], writes=[pb])
            p.op("vector", lambda e, ps=ps, tt=tt: e.tensor_copy(
                out=bd.combbc[:, tt * 512:(tt + 1) * 512], in_=ps[0:128, 0:512]),
                reads=[pb], writes=[cbB])
        NH = ICH // 2
        for half in range(2):
            for il in range(NH):
                ic = half * NH + il
                sj = il % 3
                gst, gsb = getattr(bd, "gstage_%d" % sj), bd.B("gstage_%d" % sj)
                ust, usb = getattr(bd, "ustage_%d" % sj), bd.B("ustage_%d" % sj)
                p.op("gpsimd", lambda e, e_=e_, ic=ic, gst=gst: e.dma_start(
                    out=gst[:], in_=bd.gpt.ap()[e_, ic]),
                    writes=[gsb], dma=True)
                p.op("gpsimd", lambda e, e_=e_, ic=ic, ust=ust: e.dma_start(
                    out=ust[:], in_=bd.upt.ap()[e_, ic]),
                    writes=[usb], dma=True)
                gb = [bd.ring(), bd.ring()]
                ub = [bd.ring(), bd.ring()]
                for (stage, stb, banks) in ((gst, gsb, gb), (ust, usb, ub)):
                    for k in range(KC):
                        for tt in range(2):
                            ps, pb = banks[tt]
                            p.op("tensor", lambda e, ps=ps, stg=stage, k=k, tt=tt: e.matmul(
                                ps[0:128, 0:512], stg[:, k, :],
                                bd.tbuf[:, k, tt * 512:(tt + 1) * 512],
                                start=(k == 0), stop=(k == KC - 1)),
                                reads=[stb, tB], writes=[pb], inc=(k == KC - 1))
                for tt in range(2):
                    psg, pbg = gb[tt]
                    psu, pbu = ub[tt]
                    sgs, sgsb = getattr(bd, "sgscr_%d" % tt), bd.B("sgscr_%d" % tt)
                    p.op("scalar", lambda e, ps=psg, sgs=sgs: e.activation(
                        out=sgs[:], in_=ps[0:128, 0:512], func=AF.Silu, bias=0.0, scale=1.0),
                        reads=[pbg], writes=[sgsb])
                    p.op("vector", lambda e, ps=psu, sgs=sgs, il=il, tt=tt: e.tensor_tensor(
                        out=abf[:, il, tt * 512:(tt + 1) * 512], in0=sgs[:],
                        in1=ps[0:128, 0:512], op=ALU.mult),
                        reads=[sgsb, pbu], writes=[abB])
            for dcc in range(KC):
                sj = dcc % 3
                dst, dsb = getattr(bd, "dstage_%d" % sj), bd.B("dstage_%d" % sj)
                p.op("gpsimd", lambda e, e_=e_, dcc=dcc, dst=dst, half=half: e.dma_start(
                    out=dst[:], in_=bd.dpt.ap()[e_, dcc][:, half * NH:(half + 1) * NH, :]),
                    writes=[dsb], dma=True)
                banks = [bd.ring(), bd.ring()]
                for il in range(NH):
                    for tt in range(2):
                        ps, pb = banks[tt]
                        p.op("tensor", lambda e, ps=ps, dst=dst, il=il, tt=tt: e.matmul(
                            ps[0:128, 0:512], dst[:, il, :],
                            abf[:, il, tt * 512:(tt + 1) * 512],
                            start=(il == 0), stop=(il == NH - 1)),
                            reads=[dsb, abB], writes=[pb], inc=(il == NH - 1))
                for tt in range(2):
                    ps, pb = banks[tt]
                    hs, hsb = getattr(bd, "haccscr_%d" % tt), bd.B("haccscr_%d" % tt)
                    p.op("vector", lambda e, ps=ps, hs=hs, tt=tt: e.tensor_tensor(
                        out=hs[:], in0=ps[0:128, 0:512],
                        in1=bd.combbc[:, tt * 512:(tt + 1) * 512], op=ALU.mult),
                        reads=[pb, cbB], writes=[hsb])
                    p.op("vector", lambda e, hs=hs, dcc=dcc, tt=tt: e.tensor_tensor(
                        out=strm[:, dcc, tt * 512:(tt + 1) * 512],
                        in0=strm[:, dcc, tt * 512:(tt + 1) * 512], in1=hs[:], op=ALU.add),
                        reads=[hsb], writes=[sgB])

    bd.tap("strm_pre", strm[:], sgB)
    bd.ln_rows(strm[:], sgB)

    out_r = bd.out.ap().rearrange("(kc q) t -> q kc t", q=128)

    def to_out(mc, tt, scr, scb):
        p.op("vector", lambda e, sc=scr, mc=mc, tt=tt: e.tensor_copy(
            out=strm[:, mc, tt * 512:(tt + 1) * 512], in_=sc[:]),
            reads=[scb], writes=[sgB])
        if tt == 1:
            p.op("sync", lambda e, mc=mc: e.dma_start(
                out=out_r[:, mc, bt:bt + GTOK], in_=strm[:, mc, :]),
                reads=[sgB], writes=[bd.B("out")], dma=True)
    bd.ln_apply(strm[:], sgB, to_out)


def build_program():
    bd = Build()
    p = bd.p
    bd.build_consts()
    bd.build_prep()
    bd._tapon = False
    _attention_merged(bd)
    for si in range(4):
        bd._tapon = (si < 2)
        bd._tapsuf = "" if si == 0 else "_g1"
        _ffn_group(bd, si)
    alldma = [(s, p.cnt[s]) for q in DSEM for s in DSEM[q] if p.cnt[s]]
    p.items["sync"].append((lambda e: e.nop(), alldma, None, 0))
    for s in SEMS:
        assert p.cnt[s] < 60000, (s, p.cnt[s])
    nc = bd.nc
    import contextlib
    with contextlib.ExitStack() as stk:
        sems = {s: stk.enter_context(nc.semaphore(s)) for s in SEMS}
        block = stk.enter_context(nc.Block())
        p.install(block, sems)
    return bd


def kernel(**inputs):
    f = lambda n: np.asarray(inputs[n], np.float32)
    x = f("L0_attn_x")
    pc = lambda v: np.ascontiguousarray(v.reshape(KC, 128).T)
    row = lambda v: np.ascontiguousarray(v.reshape(1, D))
    def wfm(w):
        return np.ascontiguousarray(w.T.reshape(KC, 128, w.shape[0]).transpose(1, 0, 2))

    gp, up, dp = f("L0_ffn_gate_proj"), f("L0_ffn_up_proj"), f("L0_ffn_down_proj")
    com = {
        "wqt": wfm(f("L0_attn_Wq")), "wkt": wfm(f("L0_attn_Wk")),
        "wvt": wfm(f("L0_attn_Wv")), "wot": wfm(f("L0_attn_Wo")),
        "awt": wfm(f("L0_ffn_attn_w")),
        "gwt": np.ascontiguousarray(
            f("L0_ffn_gate_w").T.reshape(KC, 128, NEXP).transpose(1, 0, 2)),
        "gpt": np.ascontiguousarray(
            gp.reshape(NEXP, ICH, 128, KC, 128).transpose(0, 1, 4, 3, 2)),
        "upt": np.ascontiguousarray(
            up.reshape(NEXP, ICH, 128, KC, 128).transpose(0, 1, 4, 3, 2)),
        "dpt": np.ascontiguousarray(
            dp.reshape(NEXP, KC, 128, ICH, 128).transpose(0, 1, 4, 3, 2)),
        "bq_r": row(f("L0_attn_bq")), "bv_r": row(f("L0_attn_bv")),
        "bk_pc": pc(f("L0_attn_bk")), "bo_pc": pc(f("L0_attn_bo")),
        "l1g": pc(f("L0_ffn_ln1_g")), "l1b": pc(f("L0_ffn_ln1_b")),
        "l2g": pc(f("L0_ffn_ln2_g")), "l2b": pc(f("L0_ffn_ln2_b")),
    }
    in_maps = []
    for c in range(N_CORES):
        m = dict(com)
        xc = np.concatenate([x[2 * c].T, x[2 * c + 1].T], axis=1)
        m["xt"] = np.ascontiguousarray(xc.reshape(KC, 128, TOKS).transpose(1, 0, 2))
        in_maps.append(m)
    bd = build_program()
    res = run_bass_kernel_spmd(bd.nc, in_maps, core_ids=list(range(N_CORES)))
    outs = []
    for c in range(N_CORES):
        o = np.asarray(res.results[c]["out"])
        outs.append(o[:, :L].T)
        outs.append(o[:, L:].T)
    return np.ascontiguousarray(np.stack(outs)).astype(np.float32)


# version 21 (final)
# speedup vs baseline: 1.1188x; 1.0661x over previous
import numpy as np
import concourse.bass as bass
import concourse.mybir as mybir
from concourse.bass_utils import run_bass_kernel_spmd

F32 = mybir.dt.float32
F16 = mybir.dt.float16
BF16 = mybir.dt.bfloat16
U32 = mybir.dt.uint32
AF = mybir.ActivationFunctionType
ALU = mybir.AluOpType
AX = mybir.AxisListType

N_CORES = 8
B_GLOB, L, D = 16, 2048, 1024
B_LOC = 2
H, DK = 16, 64
NPAIR = 8
KC = 8
NCH = 16
NEXP, INTER = 8, 4096
ICH = 32
GTOK = 1024
TOKS = B_LOC * L
EPS = 1e-5

ENGS = ("sync", "gpsimd", "vector", "scalar", "tensor")
CSEM = {"tensor": "sT", "vector": "sV", "scalar": "sS", "gpsimd": "sP", "sync": "sY"}
NDQ = 8
DSEM = {"sync": ["dS%d" % i for i in range(NDQ)],
        "gpsimd": ["dP%d" % i for i in range(NDQ)],
        "scalar": ["dA%d" % i for i in range(NDQ)]}
SEMS = tuple(["sT", "sV", "sS", "sP", "sY"] + DSEM["sync"] + DSEM["gpsimd"]
             + DSEM["scalar"])

TAPS = {}


class Buf(object):
    __slots__ = ("name", "w", "r", "ov")

    def __init__(self, name):
        self.name = name
        self.w = None
        self.r = []
        self.ov = []


class Prog(object):
    def __init__(self):
        self.items = {e: [] for e in ENGS}
        self.cnt = {s: 0 for s in SEMS}
        self.strict_self = True
        self.drot = {e: 0 for e in DSEM}

    def op(self, eng, fn, reads=(), writes=(), extra=(), inc=True, dma=False):
        deps = [d for d in extra if d is not None]
        for b in reads:
            if b.w is not None:
                deps.append(b.w)
        for b in writes:
            for v in (b,) + tuple(b.ov):
                if v.w is not None:
                    deps.append(v.w)
                deps.extend(v.r)
        if dma:
            sem, amt = DSEM[eng][self.drot[eng] % NDQ], 16
            self.drot[eng] += 1
        elif inc:
            sem, amt = CSEM[eng], 1
        else:
            sem, amt = None, 0
        self.items[eng].append((fn, deps, sem, amt))
        tok = None
        if sem is not None:
            self.cnt[sem] += amt
            tok = (sem, self.cnt[sem])
            for b in reads:
                b.r.append(tok)
            for b in writes:
                for v in (b,) + tuple(b.ov):
                    v.w = tok
                    v.r = []
        return tok

    def install(self, block, sems):
        for name in ENGS:
            items = self.items[name]
            own = CSEM[name]

            def body(eng, items=items, own=own, name=name):
                wm = {s: 0 for s in SEMS}
                for fn, deps, sem, amt in items:
                    best = {}
                    for (s, v) in deps:
                        if s == own and (name == "tensor" or not self.strict_self):
                            continue
                        if v > best.get(s, 0):
                            best[s] = v
                    for s in sorted(best):
                        if best[s] > wm[s]:
                            eng.wait_ge(sems[s], best[s])
                            wm[s] = best[s]
                    inst = fn(eng)
                    if sem is not None:
                        inst.then_inc(sems[sem], amt)

            getattr(block, name)(body)


class Build(object):
    def __init__(self):
        self.nc = bass.Bass("TRN2", target_bir_lowering=False)
        self.p = Prog()
        self.bufs = {}
        self.ranges = {}
        self.prep_tok = {}
        self._mk_io()
        self._mk_sbuf()
        self.CAPV = [1e-9, 1e-5, 0.5, -0.5, 0.125]
        for i, v in enumerate(self.CAPV):
            self.nc.const_aps.register(F32, v, self.c_cap[:, i:i + 1])
        self._link_overlaps()
        self._mk_psum()

    def B(self, name):
        if name not in self.bufs:
            self.bufs[name] = Buf(name)
        return self.bufs[name]

    def _mk_io(self):
        nc = self.nc
        di = lambda n, s: nc.dram_tensor(n, s, F32, kind="ExternalInput")
        self.xt = di("xt", [128, KC, TOKS])
        self.wqt, self.wkt = di("wqt", [128, KC, D]), di("wkt", [128, KC, D])
        self.wvt, self.wot = di("wvt", [128, KC, D]), di("wot", [128, KC, D])
        self.awt = di("awt", [128, KC, D])
        self.bq_r, self.bv_r = di("bq_r", [1, D]), di("bv_r", [1, D])
        self.bk_pc, self.bo_pc = di("bk_pc", [128, KC]), di("bo_pc", [128, KC])
        self.l1g, self.l1b = di("l1g", [128, KC]), di("l1b", [128, KC])
        self.l2g, self.l2b = di("l2g", [128, KC]), di("l2b", [128, KC])
        self.gwt = di("gwt", [128, KC, NEXP])
        self.gpt = di("gpt", [NEXP, ICH, 128, KC, 128])
        self.upt = di("upt", [NEXP, ICH, 128, KC, 128])
        self.dpt = di("dpt", [NEXP, KC, 128, ICH, 128])
        self.out = nc.dram_tensor("out", [D, TOKS], F32, kind="ExternalOutput")
        ds = lambda n, s, t: nc.dram_tensor(n, s, t, kind="Internal")
        self.s_xhi = ds("s_xhi", [128, KC, TOKS], BF16)
        self.s_wqhi = ds("s_wqhi", [128, KC, D], BF16)
        self.s_wk = ds("s_wk", [128, KC, D], BF16)
        self.s_wv = ds("s_wv", [128, KC, D], BF16)
        self.s_wo = ds("s_wo", [128, KC, D], BF16)
        self.s_awt = ds("s_awt", [128, KC, D], BF16)
        self.s_q = [ds("s_q0", [NCH, 128, D], BF16), ds("s_q1", [NCH, 128, D], BF16)]
        self.s_k = [ds("s_k0", [KC, 128, L], BF16), ds("s_k1", [KC, 128, L], BF16)]
        self.s_v = [ds("s_v0", [NCH, 128, D], BF16), ds("s_v1", [NCH, 128, D], BF16)]
        self.s_sp = ds("s_sp", [4, 128, KC, GTOK], F32)

    def _mk_sbuf(self):
        nc = self.nc
        self._cur = 16512
        top = nc.sbuf_top

        def A(name, shape, dtype, off=None):
            nbytes = int(np.prod(shape[1:])) * mybir.dt.size(dtype)
            if off is None:
                off = (self._cur + 63) // 64 * 64
                self._cur = off + nbytes
            assert off + nbytes <= top, (name, off, nbytes, top)
            self.ranges[name] = (off, off + nbytes)
            t = nc.alloc_sbuf_tensor_at(name, list(shape), dtype, offset=off)
            setattr(self, name, t)
            return off

        for nm, sh, dt in [
            ("c_iotaF", [128, L], F16), ("c_iotaPC", [128, 16], F32),
            ("c_idn16", [128, 128], BF16), ("c_idnF", [128, 128], F32),
            ("c_ones_row", [1, 128], F32), ("c_ones_col", [128, 1], F32),
            ("c_basis", [8, 1024], F32), ("c_gwhi", [128, 64], BF16),
            ("c_gwlo", [128, 64], BF16), ("gw32", [128, 64], F32),
            ("c_bk", [128, KC], F32), ("c_bo", [128, KC], F32),
            ("c_l1g", [128, KC], F32), ("c_l1b", [128, KC], F32),
            ("c_l2g", [128, KC], F32), ("c_l2b", [128, KC], F32),
            ("c_jmod", [128, 1], F32),
            ("c_cap", [128, 8], F32),
        ]:
            A(nm, sh, dt)

        R1 = (self._cur + 63) // 64 * 64
        A("strmG", [128, KC, GTOK], F32, R1)
        self._cur = R1 + 32768

        R2 = self._cur
        A("wqhi", [128, KC, D], BF16, R2)
        A("wqlo", [128, KC, D], BF16, R2 + 16384)
        A("wvb", [128, KC, D], BF16, R2 + 32768)
        A("full", [128, KC, L], BF16, R2)
        A("hbuf", [128, KC, GTOK], F32, R2)
        A("zbuf", [128, KC, GTOK], BF16, R2 + 32768)
        A("abuf", [128, ICH // 2, GTOK], BF16, R2)
        A("strmB", [128, KC, GTOK], F32, R2)
        A("hbufB", [128, KC, GTOK], F32, R1)
        A("abufB", [128, ICH // 2, GTOK], BF16, R1)
        self._cur = R2 + 49152

        R3 = self._cur
        A("p_sb", [128, L], BF16, R3)
        A("prT", [128, NCH, NPAIR * 20], BF16, R3 + 4096)
        A("ohT", [128, NCH, H * 10], BF16, R3 + 14336)
        A("oh2", [128, L], BF16, R3 + 24576)
        A("kpair", [128, L], BF16, R3 + 28672)
        A("tbuf", [128, KC, GTOK], BF16, R3)
        A("dstage_0", [128, ICH // 2, 128], BF16, R3 + 16384)
        A("dstage_1", [128, ICH // 2, 128], BF16, R3 + 20480)
        A("gstage_0", [128, KC, 128], BF16, R3 + 32768)
        A("gstage_1", [128, KC, 128], BF16, R3 + 34816)
        A("ustage_0", [128, KC, 128], BF16, R3 + 36864)
        A("ustage_1", [128, KC, 128], BF16, R3 + 38912)
        A("gstage_2", [128, KC, 128], BF16, R3 + 24576)
        A("ustage_2", [128, KC, 128], BF16, R3 + 26624)
        A("dstage_2", [128, ICH // 2, 128], BF16, R3 + 28672)
        A("lgsb", [8, GTOK], F32, R3 + 40960)
        A("combfm", [8, GTOK], F32, R3 + 40960)
        A("combbc", [128, GTOK], F32, R3 + 45056)
        A("q2s_1", [128, D], F32, R3 + 40960)
        A("esc_1", [128, D], F32, R3 + 45056)
        A("stat_1", [128, 16 * 8], F32, R3 + 32768)
        A("qevac_1", [128, D], BF16, R3 + 36864)
        self._cur = R3 + 49152

        R4 = self._cur
        for j in range(2):
            A("xc32_%d" % j, [128, KC, 128], F32, R4 + 4096 * j)
            A("xh32_%d" % j, [128, KC, 128], F32, R4 + 8192 + 4096 * j)
            A("xhic_%d" % j, [128, KC, 128], BF16, R4 + 16384 + 2048 * j)
            A("q32_%d" % j, [128, D], F32, R4 + 24576 + 4096 * j)
            A("xhit_%d" % j, [128, KC, 512], BF16, R4 + 8192 * j)
            A("kevac_%d" % j, [128, 512], BF16, R4 + 20480 + 1024 * j)
            A("vevac_%d" % j, [128, D], BF16, R4 + 18432 + 2048 * j)
            A("qc_%d" % j, [128, D], BF16, R4 + 2048 * j)
            A("vc_%d" % j, [128, D], BF16, R4 + 4096 + 2048 * j)
            A("xres_%d" % j, [128, L], F32, R4 + 8192 + 8192 * j)
            A("sqscr_%d" % j, [128, 512], F32, R4 + 2048 * j)
            A("sgscr_%d" % j, [128, 512], F32, R4 + 4096 + 2048 * j)
            A("t32scr_%d" % j, [128, 512], F32, R4 + 8192 + 2048 * j)
            A("tloscr_%d" % j, [128, 512], BF16, R4 + 12288 + 1024 * j)
            A("haccscr_%d" % j, [128, 512], F32, R4 + 14336 + 2048 * j)
            A("aplscr_%d" % j, [128, 512], F32, R4 + 18432 + 2048 * j)
            A("spillscr_%d" % j, [128, 512], F32, R4 + 0 + 2048 * j)
        A("qevac_0", [128, D], BF16, R4 + 40960)
        A("q2s_0", [128, D], F32, R4 + 32768)
        A("esc_0", [128, D], F32, R4 + 36864)
        A("lgtm", [128, 64], F32, R4 + 28672)
        A("combtm", [128, 64], F32, R4 + 28928)
        A("gcols", [128, 24], F32, R4 + 29184)
        A("p_sb2", [128, L], BF16, R4 + 20480)
        A("prT2", [128, NCH, NPAIR * 20], BF16, 201216)
        A("kpair2", [128, L], BF16, R4 + 0)
        A("r1b", [128, GTOK], F32, R4 + 32768)
        A("r2b", [128, GTOK], F32, R4 + 36864)
        self._cur = R4 + 43008

        R5 = self._cur
        A("qs0", [16, L], F32, R5)
        A("bqv_b", [128, D], F32, R5 + 8192)
        A("browscr", [1, D], F32, R5 + 12288)
        A("idxflat", [1, H * 10], F32, R5 + 12288)
        A("thr", [1, 8], F32, R5 + 12928)
        A("s_row", [1, GTOK], F32, R5)
        A("qq_row", [1, GTOK], F32, R5 + 4096)
        A("r_row", [1, GTOK], F32, R5 + 8192)
        A("rowscr", [1, GTOK], F32, R5 + 12992)
        self._cur = R5 + 12992 + 4096

        for nm, sh, dt in [
            ("wstage_0", [128, KC, 128], BF16), ("wstage_1", [128, KC, 128], BF16),
            ("m4", [128, 4], F32), ("mcol", [128, 1], F32), ("mbias", [128, 1], F32),
            ("rscol", [128, 1], F32), ("rv_all", [128, NPAIR], F32),
            ("valid", [128, 1], F32), ("vtmp", [128, 1], F32),
            ("idxT", [128, 16], F32), ("idxf", [16, 10], F32),
            ("i8", [16, 8], U32), ("i2", [16, 8], U32),
            ("t8a", [16, 8], F32), ("t8b", [16, 8], F32),
            ("idx_bc", [128, H * 10], F32),
            ("scat_0", [128, 128], BF16), ("scat_1", [128, 128], BF16),
            ("stat_0", [128, 16 * 8], F32),
            ("urow", [1, 8], F32),
            ("idxT2", [128, 16], F32), ("valid2", [128, 1], F32),
            ("rv_all2", [128, NPAIR], F32), ("qspT2", [128, NPAIR * 10], BF16),
            ("qspT", [128, NPAIR * 10], BF16),
        ]:
            A(nm, sh, dt)
        A("qs0_1x", [16, L], F32, self.ranges["strmG"][0])
        R1_, R2_ = self.ranges["strmG"][0], self.ranges["wqhi"][0]
        A("xhic_2", [128, KC, 128], BF16, R1_ + 8192)
        A("qevac_2", [128, D], BF16, R1_ + 10240)
        A("q32_2", [128, D], F32, R1_ + 12288)
        A("q2s_2", [128, D], F32, R1_ + 16384)
        A("esc_2", [128, D], F32, R1_ + 20480)
        A("stat_2", [128, 16 * 8], F32, R1_ + 24576)
        A("stat_3", [128, 16 * 8], F32, R1_ + 25088)
        A("vevac_2", [128, D], BF16, R1_ + 10240)
        A("xhic_3", [128, KC, 128], BF16, R2_ + 32768)
        A("qevac_3", [128, D], BF16, R2_ + 34816)
        A("q32_3", [128, D], F32, R2_ + 36864)
        A("q2s_3", [128, D], F32, R2_ + 40960)
        A("esc_3", [128, D], F32, R2_ + 45056)
        A("vevac_3", [128, D], BF16, R2_ + 34816)
        assert self._cur <= self.nc.sbuf_top, (self._cur, self.nc.sbuf_top)

    def _link_overlaps(self):
        names = list(self.ranges)
        for i, a in enumerate(names):
            sa, ea = self.ranges[a]
            ba = self.B(a)
            for b_ in names[i + 1:]:
                sb, eb = self.ranges[b_]
                if sa < eb and sb < ea:
                    bb = self.B(b_)
                    ba.ov.append(bb)
                    bb.ov.append(ba)

    def _mk_psum(self):
        nc = self.nc
        self.PS = [nc.alloc_psum_tensor("PS%d" % i, [128, 512], F32) for i in range(6)]
        self.PT = [nc.alloc_psum_tensor("PT%d" % i, [128, 256], BF16) for i in range(2)]
        self.psb = [self.B("PS%d" % i) for i in range(6)]
        self.ptb = [self.B("PT%d" % i) for i in range(2)]
        self._ring = 0
        self._ringw = 4
        self._tring = 0

    def tap(self, name, ap, buf):
        name = name + getattr(self, "_tapsuf", "")
        if not getattr(self, "_tapon", False) or name not in TAPS:
            return
        t = self.nc.dram_tensor("tap_" + name, list(ap.shape), ap.dtype,
                                kind="ExternalOutput")
        self.p.op("sync", lambda e, t=t, ap=ap: e.dma_start(out=t.ap(), in_=ap),
                  reads=[buf], dma=True)

    def ring(self):
        i = self._ring % self._ringw
        self._ring += 1
        return self.PS[i], self.psb[i]

    def xbank(self):
        return self.PS[5], self.psb[5]

    def xbank2(self):
        return self.PS[4], self.psb[4]

    def tring(self):
        i = self._tring % 2
        self._tring += 1
        return self.PT[i], self.ptb[i]

    def build_consts(self):
        p = self.p
        scr = self.strmG[:].rearrange("q a b -> q (a b)")
        bA = self.B("strmG")
        t1, t2 = scr[:, 0:128], scr[:, 128:256]
        tb8 = scr[0:8, 256:1280]
        tbg = scr[:, 1280:1344]
        iF, iPC = self.B("c_iotaF"), self.B("c_iotaPC")
        for i, v in enumerate(self.CAPV):
            p.op("vector", lambda e, i=i, v=v: e.memset(self.c_cap[:, i:i + 1], v),
                 writes=[self.B("c_cap")])
        p.op("gpsimd", lambda e: e.iota(self.c_iotaF[:], [[1, L]], channel_multiplier=0,
                                        allow_small_or_imprecise_dtypes=True), writes=[iF])
        p.op("gpsimd", lambda e: e.iota(self.c_iotaPC[:], [[128, 16]], channel_multiplier=1,
                                        allow_small_or_imprecise_dtypes=True), writes=[iPC])
        p.op("gpsimd", lambda e: e.iota(t1, [[0, 128]], channel_multiplier=1,
                                        allow_small_or_imprecise_dtypes=True), writes=[bA])
        p.op("gpsimd", lambda e: e.iota(t2, [[1, 128]], channel_multiplier=0,
                                        allow_small_or_imprecise_dtypes=True), writes=[bA])
        p.op("vector", lambda e: e.tensor_tensor(out=self.c_idnF[:], in0=t1, in1=t2,
                                                 op=ALU.is_equal),
             reads=[bA], writes=[self.B("c_idnF")])
        p.op("vector", lambda e: e.tensor_copy(out=self.c_idn16[:], in_=self.c_idnF[:]),
             reads=[self.B("c_idnF")], writes=[self.B("c_idn16")])
        p.op("vector", lambda e: e.memset(self.c_ones_row[:], 1.0), writes=[self.B("c_ones_row")])
        p.op("vector", lambda e: e.memset(self.c_ones_col[:], 1.0), writes=[self.B("c_ones_col")])
        p.op("gpsimd", lambda e: e.iota(tb8.rearrange("q (a b) -> q a b", b=128),
                                        [[1, 8], [0, 128]], channel_multiplier=0,
                                        allow_small_or_imprecise_dtypes=True), writes=[bA])
        p.op("vector", lambda e: e.tensor_scalar(out=self.c_basis[:], in0=tb8,
                                                 scalar1=self.c_iotaPC[0:8, 0:1], scalar2=None,
                                                 op0=ALU.is_equal),
             reads=[bA, iPC], writes=[self.B("c_basis")])
        p.op("vector", lambda e: e.tensor_scalar(out=self.c_jmod[:], in0=self.c_iotaPC[:, 0:1],
                                                 scalar1=64.0, scalar2=None, op0=ALU.is_ge),
             reads=[iPC], writes=[self.B("c_jmod")])
        p.op("vector", lambda e: e.scalar_tensor_tensor(out=self.c_jmod[:], in0=self.c_jmod[:],
                                                        scalar=-64.0, in1=self.c_iotaPC[:, 0:1],
                                                        op0=ALU.mult, op1=ALU.add),
             reads=[iPC, self.B("c_jmod")], writes=[self.B("c_jmod")])
        for (name, src, dst) in [("c_bk", self.bk_pc, self.c_bk), ("c_bo", self.bo_pc, self.c_bo),
                                 ("c_l1g", self.l1g, self.c_l1g), ("c_l1b", self.l1b, self.c_l1b),
                                 ("c_l2g", self.l2g, self.c_l2g), ("c_l2b", self.l2b, self.c_l2b)]:
            p.op("scalar", lambda e, s=src, d=dst: e.dma_start(out=d[:], in_=s.ap()),
                 writes=[self.B(name)], dma=True)
        p.op("scalar", lambda e: e.dma_start(
            out=self.gw32[:].rearrange("q (kc ex) -> q kc ex", ex=8),
            in_=self.gwt.ap()),
            writes=[self.B("gw32")], dma=True)
        p.op("vector", lambda e: e.tensor_copy(out=self.c_gwhi[:], in_=self.gw32[:]),
             reads=[self.B("gw32")], writes=[self.B("c_gwhi")])
        p.op("vector", lambda e: e.tensor_copy(out=tbg, in_=self.c_gwhi[:]),
             reads=[self.B("c_gwhi")], writes=[bA])
        p.op("vector", lambda e: e.tensor_tensor(out=self.c_gwlo[:], in0=self.gw32[:],
                                                 in1=tbg, op=ALU.subtract),
             reads=[bA, self.B("gw32")], writes=[self.B("c_gwlo")])
        p.op("vector", lambda e: e.memset(self.oh2[:], 0.0), writes=[self.B("oh2")])
        p.op("vector", lambda e: e.memset(self.p_sb[:], 0.0), writes=[self.B("p_sb")])
        for i in range(6):
            p.op("vector", lambda e, i=i: e.memset(self.PS[i][:], 0.0), writes=[self.psb[i]])


    def build_prep(self):
        p = self.p
        casts = [("xhi", self.s_xhi.ap(), self.xt.ap()),
                 ("wqhi", self.s_wqhi.ap(), self.wqt.ap()),
                 ("wk", self.s_wk.ap(), self.wkt.ap()),
                 ("wv", self.s_wv.ap(), self.wvt.ap()),
                 ("wo", self.s_wo.ap(), self.wot.ap()),
                 ("awt", self.s_awt.ap(), self.awt.ap())]
        def fence():
            return [(s, p.cnt[s]) for s in DSEM["gpsimd"] if p.cnt[s]]

        for name, dst, src in casts:
            p.op("gpsimd", lambda e, d=dst, s=src: e.dma_start(out=d, in_=s), dma=True)
        ftok = fence()
        for name, _, _ in casts:
            self.prep_tok[name] = ftok

    def bcast_row(self, src_ap, src_buf, width, dst_ap, dst_buf):
        p = self.p
        for t in range((width + 511) // 512):
            w = min(512, width - t * 512)
            ps, pb = self.ring()
            p.op("tensor", lambda e, ps=ps, t=t, w=w, s=src_ap: e.matmul(
                ps[0:128, 0:w], self.c_ones_row[:], s[0:1, t * 512:t * 512 + w],
                start=True, stop=True),
                reads=[src_buf, self.B("c_ones_row")], writes=[pb])
            p.op("vector", lambda e, ps=ps, t=t, w=w, d=dst_ap: e.tensor_copy(
                out=d[:, t * 512:t * 512 + w], in_=ps[0:128, 0:w]),
                reads=[pb], writes=[dst_buf])

    def ln_rows(self, src, src_buf):
        p = self.p
        sb = [self.ring() for _ in range(2)]
        qb = [self.ring() for _ in range(2)]
        for tt in range(2):
            ps, pb = sb[tt]
            for mc in range(KC):
                p.op("tensor", lambda e, ps=ps, mc=mc, tt=tt, s=src: e.matmul(
                    ps[0:1, 0:512], self.c_ones_col[:], s[:, mc, tt * 512:(tt + 1) * 512],
                    start=(mc == 0), stop=(mc == KC - 1)),
                    reads=[src_buf, self.B("c_ones_col")], writes=[pb], inc=(mc == KC - 1))
            psq, pbq = qb[tt]
            for mc in range(KC):
                scr = getattr(self, "sqscr_%d" % (mc % 2))
                scb = self.B("sqscr_%d" % (mc % 2))
                p.op("vector", lambda e, sc=scr, mc=mc, tt=tt, s=src: e.tensor_tensor(
                    out=sc[:], in0=s[:, mc, tt * 512:(tt + 1) * 512],
                    in1=s[:, mc, tt * 512:(tt + 1) * 512], op=ALU.mult),
                    reads=[src_buf], writes=[scb])
                p.op("tensor", lambda e, ps=psq, sc=scr, mc=mc: e.matmul(
                    ps[0:1, 0:512], self.c_ones_col[:], sc[:],
                    start=(mc == 0), stop=(mc == KC - 1)),
                    reads=[scb, self.B("c_ones_col")],
                    writes=[pbq] if mc == KC - 1 else (), inc=True)
        rS, rQ, rR, rW = self.B("s_row"), self.B("qq_row"), self.B("r_row"), self.B("rowscr")
        for tt in range(2):
            ps, pb = sb[tt]
            p.op("vector", lambda e, ps=ps, tt=tt: e.tensor_scalar(
                out=self.s_row[0:1, tt * 512:(tt + 1) * 512], in0=ps[0:1, 0:512],
                scalar1=1.0 / D, scalar2=None, op0=ALU.mult), reads=[pb], writes=[rS])
            ps, pb = qb[tt]
            p.op("vector", lambda e, ps=ps, tt=tt: e.tensor_scalar(
                out=self.qq_row[0:1, tt * 512:(tt + 1) * 512], in0=ps[0:1, 0:512],
                scalar1=1.0 / D, scalar2=None, op0=ALU.mult), reads=[pb], writes=[rQ])
        p.op("vector", lambda e: e.tensor_tensor(out=self.r_row[:], in0=self.s_row[:],
                                                 in1=self.s_row[:], op=ALU.mult),
             reads=[rS], writes=[rR])
        p.op("vector", lambda e: e.tensor_tensor(out=self.r_row[:], in0=self.qq_row[:],
                                                 in1=self.r_row[:], op=ALU.subtract),
             reads=[rQ, rR], writes=[rR])
        p.op("scalar", lambda e: e.activation(out=self.r_row[:], in_=self.r_row[:],
                                              func=AF.Ln, bias=EPS, scale=1.0),
             reads=[rR, self.B("c_cap")], writes=[rR])
        p.op("scalar", lambda e: e.activation(out=self.r_row[:], in_=self.r_row[:],
                                              func=AF.Exp, bias=0.0, scale=-0.5),
             reads=[rR, self.B("c_cap")], writes=[rR])
        p.op("vector", lambda e: e.tensor_tensor(out=self.rowscr[:], in0=self.s_row[:],
                                                 in1=self.r_row[:], op=ALU.mult),
             reads=[rS, rR], writes=[rW])
        self.bcast_row(self.r_row[:], rR, GTOK, self.r1b[:], self.B("r1b"))
        self.bcast_row(self.rowscr[:], rW, GTOK, self.r2b[:], self.B("r2b"))

    def ln_apply(self, src, src_buf, emit_dst):
        p = self.p
        for mc in range(KC):
            for tt in range(2):
                scr = getattr(self, "aplscr_%d" % tt)
                scb = self.B("aplscr_%d" % tt)
                p.op("vector", lambda e, sc=scr, mc=mc, tt=tt, s=src: e.tensor_tensor(
                    out=sc[:], in0=s[:, mc, tt * 512:(tt + 1) * 512],
                    in1=self.r1b[:, tt * 512:(tt + 1) * 512], op=ALU.mult),
                    reads=[src_buf, self.B("r1b")], writes=[scb])
                p.op("vector", lambda e, sc=scr, tt=tt: e.tensor_tensor(
                    out=sc[:], in0=sc[:], in1=self.r2b[:, tt * 512:(tt + 1) * 512],
                    op=ALU.subtract),
                    reads=[scb, self.B("r2b")], writes=[scb])
                emit_dst(mc, tt, scr, scb)


def _attention_merged(bd):
    p = bd.p
    xt_r = bd.xt.ap()
    BT = [0, L]

    wqhiB, wqloB = bd.B("wqhi"), bd.B("wqlo")
    p.op("scalar", lambda e: e.dma_start(out=bd.wqhi[:], in_=bd.s_wqhi.ap()),
         writes=[wqhiB], extra=bd.prep_tok["wqhi"], dma=True)
    for k in range(KC):
        j = k % 2
        c32, c32b = getattr(bd, "xc32_%d" % j), bd.B("xc32_%d" % j)
        h32, h32b = getattr(bd, "xh32_%d" % j), bd.B("xh32_%d" % j)
        c32f = c32[:].rearrange("q a b -> q (a b)")
        h32f = h32[:].rearrange("q a b -> q (a b)")
        p.op("scalar", lambda e, k=k, c32f=c32f: e.dma_start(
            out=c32f, in_=bd.wqt.ap()[:, k, :]),
            writes=[c32b], dma=True)
        p.op("vector", lambda e, k=k, h32f=h32f: e.tensor_copy(
            out=h32f, in_=bd.wqhi[:, k, :]),
            reads=[wqhiB], writes=[h32b])
        p.op("vector", lambda e, k=k, c32f=c32f, h32f=h32f: e.tensor_tensor(
            out=bd.wqlo[:, k, :], in0=c32f, in1=h32f, op=ALU.subtract),
            reads=[c32b, h32b], writes=[wqloB])

    brB = bd.B("browscr")
    p.op("scalar", lambda e: e.dma_start(out=bd.browscr[:], in_=bd.bq_r.ap()),
         writes=[brB], dma=True)
    bd.bcast_row(bd.browscr[:], brB, D, bd.bqv_b[:], bd.B("bqv_b"))

    QS = [bd.qs0, bd.qs0_1x]
    QSB = [bd.B("qs0"), bd.B("qs0_1x")]

    def mk_kblock(tt, m):
        def em():
            if m == 0:
                for b_ in range(2):
                    xht, xhtb = getattr(bd, "xhit_%d" % b_), bd.B("xhit_%d" % b_)
                    p.op("scalar", lambda e, tt=tt, b_=b_, xht=xht: e.dma_start(
                        out=xht[:],
                        in_=bd.s_xhi.ap()[:, :, BT[b_] + tt * 512: BT[b_] + (tt + 1) * 512]),
                        writes=[xhtb], extra=bd.prep_tok["xhi"], dma=True)
            wj = (tt * KC + m) % 2
            ws, wsb = getattr(bd, "wstage_%d" % wj), bd.B("wstage_%d" % wj)
            p.op("sync", lambda e, m=m, ws=ws: e.dma_start(
                out=ws[:], in_=bd.s_wk.ap()[:, :, m * 128:(m + 1) * 128]),
                writes=[wsb], extra=bd.prep_tok["wk"], dma=True)
            for b_ in range(2):
                xht, xhtb = getattr(bd, "xhit_%d" % b_), bd.B("xhit_%d" % b_)
                ps, pb = bd.ring()
                for k in range(KC):
                    p.op("tensor", lambda e, ps=ps, k=k, ws=ws, xht=xht: e.matmul(
                        ps[0:128, 0:512], ws[:, k, :], xht[:, k, :],
                        start=(k == 0), stop=(k == KC - 1)),
                        reads=[wsb, xhtb], writes=[pb], inc=(k == KC - 1))
                kev, kevb = getattr(bd, "kevac_%d" % b_), bd.B("kevac_%d" % b_)
                p.op("vector", lambda e, ps=ps, m=m, kev=kev: e.tensor_scalar(
                    out=kev[:], in0=ps[0:128, 0:512], scalar1=bd.c_bk[:, m:m + 1],
                    scalar2=None, op0=ALU.add),
                    reads=[pb, bd.B("c_bk")], writes=[kevb])
                p.op("scalar", lambda e, m=m, tt=tt, b_=b_, kev=kev: e.dma_start(
                    out=bd.s_k[b_].ap()[m][:, tt * 512:(tt + 1) * 512], in_=kev[:]),
                    reads=[kevb], writes=[bd.B("s_k%d" % b_)], dma=True)
        return em

    kblocks = [mk_kblock(tt_, m_) for tt_ in range(4) for m_ in range(KC)]

    def v_pre():
        p.op("scalar", lambda e: e.dma_start(out=bd.browscr[:], in_=bd.bv_r.ap()),
             writes=[brB], dma=True)
        bd.bcast_row(bd.browscr[:], brB, D, bd.bqv_b[:], bd.B("bqv_b"))
        p.op("scalar", lambda e: e.dma_start(out=bd.wvb[:], in_=bd.s_wv.ap()),
             writes=[bd.B("wvb")], extra=bd.prep_tok["wv"], dma=True)

    def mk_vblock(c, b):
        def em():
            j = (2 * c + b) % 3
            xhi, xhib = getattr(bd, "xhic_%d" % j), bd.B("xhic_%d" % j)
            p.op("scalar", lambda e, c=c, b=b, xhi=xhi: e.dma_start(
                out=xhi[:], in_=bd.s_xhi.ap()[:, :, BT[b] + c * 128: BT[b] + (c + 1) * 128]),
                writes=[xhib], extra=bd.prep_tok["xhi"], dma=True)
            halves = [bd.ring(), bd.ring()]
            for hf in range(2):
                ps, pb = halves[hf]
                for k in range(KC):
                    p.op("tensor", lambda e, ps=ps, k=k, hf=hf, xhi=xhi: e.matmul(
                        ps[0:128, 0:512], xhi[:, k, :],
                        bd.wvb[:, k, hf * 512:(hf + 1) * 512],
                        start=(k == 0), stop=(k == KC - 1)),
                        reads=[xhib, bd.B("wvb")], writes=[pb], inc=(k == KC - 1))
            vev, vevb = getattr(bd, "vevac_%d" % j), bd.B("vevac_%d" % j)
            for hf in range(2):
                ps, pb = halves[hf]
                p.op("vector", lambda e, ps=ps, hf=hf, vev=vev: e.tensor_tensor(
                    out=vev[:, hf * 512:(hf + 1) * 512], in0=ps[0:128, 0:512],
                    in1=bd.bqv_b[:, hf * 512:(hf + 1) * 512], op=ALU.add),
                    reads=[pb, bd.B("bqv_b")], writes=[vevb])
            p.op("scalar", lambda e, c=c, b=b, vev=vev: e.dma_start(
                out=bd.s_v[b].ap()[c], in_=vev[:]),
                reads=[vevb], writes=[bd.B("s_v%d" % b)], dma=True)
        return em

    vblocks = [mk_vblock(c_, b_) for c_ in range(NCH) for b_ in range(2)]

    bd._ringw = 6
    pend_st = []
    for c in range(NCH):
        for b in range(2):
            j = (2 * c + b) % 4
            st = getattr(bd, "stat_%d" % j)
            stB = bd.B("stat_%d" % j)
            q2t, q2B = getattr(bd, "q2s_%d" % j), bd.B("q2s_%d" % j)
            est, escB = getattr(bd, "esc_%d" % j), bd.B("esc_%d" % j)
            xhi, xhib = getattr(bd, "xhic_%d" % j), bd.B("xhic_%d" % j)
            q32, q32b = getattr(bd, "q32_%d" % j), bd.B("q32_%d" % j)
            qev, qevb = getattr(bd, "qevac_%d" % j), bd.B("qevac_%d" % j)
            bt = BT[b]
            p.op("scalar", lambda e, c=c, bt=bt, xhi=xhi: e.dma_start(
                out=xhi[:], in_=bd.s_xhi.ap()[:, :, bt + c * 128: bt + (c + 1) * 128]),
                writes=[xhib], extra=bd.prep_tok["xhi"], dma=True)
            halves = [bd.ring(), bd.ring()]
            for hf in range(2):
                ps, pb = halves[hf]
                passes = [(bd.wqhi, wqhiB), (bd.wqlo, wqloB)]
                for pi, (wb, wbb) in enumerate(passes):
                    for k in range(KC):
                        fst = (pi == 0 and k == 0)
                        lst = (pi == 1 and k == KC - 1)
                        p.op("tensor", lambda e, ps=ps, xhi=xhi, wb=wb, k=k, hf=hf,
                             fst=fst, lst=lst: e.matmul(
                                 ps[0:128, 0:512], xhi[:, k, :],
                                 wb[:, k, hf * 512:(hf + 1) * 512], start=fst, stop=lst),
                             reads=[xhib, wbb], writes=[pb], inc=lst)
            for hf in range(2):
                ps, pb = halves[hf]
                p.op("vector", lambda e, ps=ps, hf=hf, q32=q32: e.tensor_tensor(
                    out=q32[:, hf * 512:(hf + 1) * 512], in0=ps[0:128, 0:512],
                    in1=bd.bqv_b[:, hf * 512:(hf + 1) * 512], op=ALU.add),
                    reads=[pb, bd.B("bqv_b")], writes=[q32b])
            p.op("gpsimd", lambda e, qev=qev, q32=q32: e.tensor_copy(
                out=qev[:], in_=q32[:]),
                reads=[q32b], writes=[qevb])
            p.op("scalar", lambda e, c=c, b=b, qev=qev: e.dma_start(
                out=bd.s_q[b].ap()[c], in_=qev[:]),
                reads=[qevb], writes=[bd.B("s_q%d" % b)], dma=True)

            def mk_stats(c=c, b=b, st=st, stB=stB, q2t=q2t, q2B=q2B, est=est,
                         escB=escB, q32=q32, q32b=q32b):
                mq, sq, sqq, ent, l2t, var, qsc, tmp = (
                    st[:, i * 16:(i + 1) * 16] for i in range(8))
                q3 = q32[:].rearrange("q (h dk) -> q h dk", dk=DK)
                q23 = q2t[:].rearrange("q (h dk) -> q h dk", dk=DK)
                e3 = est[:].rearrange("q (h dk) -> q h dk", dk=DK)
                steps = [
                    lambda: p.op("gpsimd", lambda e: e.tensor_tensor(
                        out=q2t[:], in0=q32[:], in1=q32[:], op=ALU.mult),
                        reads=[q32b], writes=[q2B]),
                    lambda: p.op("vector", lambda e: e.reduce_sum(
                        out=sq, in_=q3, axis=AX.X), reads=[q32b], writes=[stB]),
                    lambda: p.op("vector", lambda e: e.reduce_sum(
                        out=sqq, in_=q23, axis=AX.X), reads=[q2B], writes=[stB]),
                    lambda: p.op("vector", lambda e: e.reduce_max(
                        out=mq, in_=q3, axis=AX.X), reads=[q32b], writes=[stB]),
                    lambda: p.op("vector", lambda e: e.tensor_tensor(
                        out=e3, in0=q3, in1=mq.unsqueeze(2).to_broadcast([128, H, DK]),
                        op=ALU.subtract), reads=[q32b, stB], writes=[escB]),
                    lambda: p.op("scalar", lambda e: e.activation(
                        out=q2t[:], in_=est[:], func=AF.Exp, bias=0.0, scale=1.0),
                        reads=[escB], writes=[q2B]),
                    lambda: p.op("vector", lambda e: e.reduce_sum(
                        out=tmp, in_=q23, axis=AX.X), reads=[q2B], writes=[stB]),
                    lambda: p.op("vector", lambda e: e.reciprocal(out=tmp, in_=tmp),
                                 reads=[stB], writes=[stB]),
                    lambda: p.op("vector", lambda e: e.tensor_tensor(
                        out=q23, in0=q23, in1=tmp.unsqueeze(2).to_broadcast([128, H, DK]),
                        op=ALU.mult), reads=[q2B, stB], writes=[q2B]),
                    lambda: p.op("scalar", lambda e: e.activation(
                        out=est[:], in_=q2t[:], func=AF.Ln, bias=1e-9, scale=1.0),
                        reads=[q2B, bd.B("c_cap")], writes=[escB]),
                    lambda: p.op("gpsimd", lambda e: e.tensor_tensor(
                        out=est[:], in0=q2t[:], in1=est[:], op=ALU.mult),
                        reads=[q2B, escB], writes=[escB]),
                    lambda: p.op("vector", lambda e: e.reduce_sum(
                        out=ent, in_=e3, axis=AX.X, negate=True),
                        reads=[escB], writes=[stB]),
                    lambda: p.op("scalar", lambda e: e.activation(
                        out=l2t, in_=sqq, func=AF.Ln, bias=0.0, scale=1.0),
                        reads=[stB], writes=[stB]),
                    lambda: p.op("scalar", lambda e: e.activation(
                        out=l2t, in_=l2t, func=AF.Exp, bias=0.0, scale=0.5),
                        reads=[stB, bd.B("c_cap")], writes=[stB]),
                    lambda: p.op("vector", lambda e: e.tensor_tensor(
                        out=var, in0=sq, in1=sq, op=ALU.mult), reads=[stB], writes=[stB]),
                    lambda: p.op("vector", lambda e: e.scalar_tensor_tensor(
                        out=var, in0=var, scalar=-1.0 / DK, in1=sqq,
                        op0=ALU.mult, op1=ALU.add), reads=[stB], writes=[stB]),
                    lambda: p.op("vector", lambda e: e.tensor_scalar(
                        out=qsc, in0=l2t, scalar1=0.5, scalar2=None, op0=ALU.mult),
                        reads=[stB], writes=[stB]),
                    lambda: p.op("vector", lambda e: e.scalar_tensor_tensor(
                        out=qsc, in0=ent, scalar=0.3, in1=qsc, op0=ALU.mult, op1=ALU.add),
                        reads=[stB], writes=[stB]),
                    lambda: p.op("vector", lambda e: e.scalar_tensor_tensor(
                        out=qsc, in0=var, scalar=0.2 / (DK - 1), in1=qsc,
                        op0=ALU.mult, op1=ALU.add),
                        reads=[stB], writes=[stB]),
                ]

                def tp_step():
                    ps, pb = bd.ring()
                    p.op("tensor", lambda e, ps=ps: e.transpose(
                        ps[0:16, 0:128], qsc, bd.c_idnF[:]),
                        reads=[stB, bd.B("c_idnF")], writes=[pb])
                    p.op("vector", lambda e, ps=ps: e.tensor_copy(
                        out=QS[b][0:16, c * 128:(c + 1) * 128], in_=ps[0:16, 0:128]),
                        reads=[pb], writes=[QSB[b]])
                steps.append(tp_step)
                return steps
            if kblocks:
                kblocks.pop(0)()
            pend_st.append(mk_stats())
            if len(pend_st) == 4:
                nst = max(len(s_) for s_ in pend_st)
                for si_ in range(nst):
                    for ch_ in pend_st:
                        if si_ < len(ch_):
                            ch_[si_]()
                pend_st = []
    for ch_ in pend_st:
        for f_ in ch_:
            f_()
    bd._ringw = 4

    IDXT = [bd.idxT, bd.idxT2]
    VALID = [bd.valid, bd.valid2]
    QSP = [bd.qspT, bd.qspT2]

    def a4(b):
        qs, qsB = QS[b], QSB[b]
        idxTb = bd.B("idxT%s" % ("" if b == 0 else "2"))
        validb = bd.B("valid%s" % ("" if b == 0 else "2"))
        iB = bd.B("idxf")
        p.op("vector", lambda e, qs=qs: e.max(out=bd.t8a[:], in_=qs[0:16, :]),
             reads=[qsB], writes=[bd.B("t8a")])
        p.op("vector", lambda e, qs=qs: e.max_index(out=bd.i8[:], in_max=bd.t8a[:],
                                                    in_values=qs[0:16, :]),
             reads=[qsB, bd.B("t8a")], writes=[bd.B("i8")])
        uB = bd.B("urow")
        xr0, xr0b = bd.xres_0, bd.B("xres_0")
        p.op("vector", lambda e, qs=qs: e.reduce_sum(out=bd.urow[0:1, 0:1],
                                                     in_=qs[0:1, :], axis=AX.X),
             reads=[qsB], writes=[uB])
        p.op("vector", lambda e, qs=qs, xr0=xr0: e.tensor_tensor(
            out=xr0[0:1, :], in0=qs[0:1, :], in1=qs[0:1, :], op=ALU.mult),
            reads=[qsB], writes=[xr0b])
        p.op("vector", lambda e, xr0=xr0: e.reduce_sum(out=bd.urow[0:1, 1:2],
                                                       in_=xr0[0:1, :], axis=AX.X),
             reads=[xr0b], writes=[uB])
        p.op("vector", lambda e: e.tensor_scalar(out=bd.urow[0:1, 2:3], in0=bd.urow[0:1, 0:1],
                                                 scalar1=1.0 / L, scalar2=None, op0=ALU.mult),
             reads=[uB], writes=[uB])
        p.op("vector", lambda e: e.tensor_tensor(out=bd.urow[0:1, 3:4], in0=bd.urow[0:1, 0:1],
                                                 in1=bd.urow[0:1, 2:3], op=ALU.mult),
             reads=[uB], writes=[uB])
        p.op("vector", lambda e: e.tensor_tensor(out=bd.urow[0:1, 3:4], in0=bd.urow[0:1, 1:2],
                                                 in1=bd.urow[0:1, 3:4], op=ALU.subtract),
             reads=[uB], writes=[uB])
        p.op("vector", lambda e: e.tensor_scalar(out=bd.urow[0:1, 3:4], in0=bd.urow[0:1, 3:4],
                                                 scalar1=1.0 / (L - 1), scalar2=None,
                                                 op0=ALU.mult),
             reads=[uB], writes=[uB])
        p.op("scalar", lambda e: e.activation(out=bd.urow[0:1, 4:5], in_=bd.urow[0:1, 3:4],
                                              func=AF.Ln, bias=0.0, scale=1.0),
             reads=[uB], writes=[uB])
        p.op("scalar", lambda e: e.activation(out=bd.urow[0:1, 4:5], in_=bd.urow[0:1, 4:5],
                                              func=AF.Exp, bias=0.0, scale=0.5),
             reads=[uB, bd.B("c_cap")], writes=[uB])
        p.op("vector", lambda e: e.tensor_scalar(out=bd.urow[0:1, 5:6], in0=bd.urow[0:1, 2:3],
                                                 scalar1=1e-6, scalar2=None, op0=ALU.add),
             reads=[uB], writes=[uB])
        p.op("vector", lambda e: e.reciprocal(out=bd.urow[0:1, 5:6], in_=bd.urow[0:1, 5:6]),
             reads=[uB], writes=[uB])
        p.op("vector", lambda e: e.tensor_tensor(out=bd.urow[0:1, 6:7], in0=bd.urow[0:1, 4:5],
                                                 in1=bd.urow[0:1, 5:6], op=ALU.mult),
             reads=[uB], writes=[uB])
        p.op("vector", lambda e: e.tensor_scalar(out=bd.thr[0:1, 0:1], in0=bd.urow[0:1, 6:7],
                                                 scalar1=10.0, scalar2=-0.5, op0=ALU.mult,
                                                 op1=ALU.add),
             reads=[uB], writes=[bd.B("thr")])
        ps, pb = bd.ring()
        p.op("tensor", lambda e, ps=ps: e.matmul(ps[0:128, 0:1], bd.c_ones_row[:],
                                                 bd.thr[0:1, 0:1], start=True, stop=True),
             reads=[bd.B("thr"), bd.B("c_ones_row")], writes=[pb])
        p.op("vector", lambda e, ps=ps: e.tensor_tensor(out=bd.vtmp[:], in0=ps[0:128, 0:1],
                                                        in1=bd.c_jmod[:], op=ALU.is_gt),
             reads=[pb, bd.B("c_jmod")], writes=[bd.B("vtmp")])
        p.op("vector", lambda e, b=b: e.tensor_scalar(out=VALID[b][:], in0=bd.c_jmod[:],
                                                      scalar1=2.5, scalar2=None,
                                                      op0=ALU.is_lt),
             reads=[bd.B("c_jmod")], writes=[validb])
        p.op("vector", lambda e, b=b: e.tensor_tensor(out=VALID[b][:], in0=VALID[b][:],
                                                      in1=bd.vtmp[:], op=ALU.max),
             reads=[bd.B("vtmp"), validb], writes=[validb])
        p.op("vector", lambda e, qs=qs: e.match_replace(out=qs[0:16, :], in_to_replace=bd.t8a[:],
                                                        in_values=qs[0:16, :],
                                                        imm_value=-1e30),
             reads=[bd.B("t8a")], writes=[qsB])
        p.op("vector", lambda e, qs=qs: e.max(out=bd.t8b[:], in_=qs[0:16, :]),
             reads=[qsB], writes=[bd.B("t8b")])
        p.op("vector", lambda e, qs=qs: e.max_index(out=bd.i2[:], in_max=bd.t8b[:],
                                                    in_values=qs[0:16, :]),
             reads=[qsB, bd.B("t8b")], writes=[bd.B("i2")])
        p.op("vector", lambda e: e.tensor_copy(out=bd.idxf[:, 0:8], in_=bd.i8[:]),
             reads=[bd.B("i8")], writes=[iB])
        p.op("vector", lambda e: e.tensor_copy(out=bd.idxf[:, 8:10], in_=bd.i2[:, 0:2]),
             reads=[bd.B("i2")], writes=[iB])
        ps, pb = bd.ring()
        p.op("tensor", lambda e, ps=ps: e.transpose(
            ps[0:10, 0:16], bd.idxf[:], bd.c_idnF[0:16, 0:16]),
            reads=[iB, bd.B("c_idnF")], writes=[pb])
        p.op("vector", lambda e, ps=ps, b=b: e.tensor_copy(
            out=IDXT[b][0:10, :], in_=ps[0:10, 0:16]),
            reads=[pb], writes=[idxTb])
        p.op("scalar", lambda e, b=b: e.dma_start(out=IDXT[b][64:74, :], in_=IDXT[b][0:10, :]),
             reads=[idxTb], writes=[idxTb], dma=True)
        p.op("scalar", lambda e: e.dma_start(out=bd.idxflat[:], in_=bd.idxf[:]),
             reads=[iB], writes=[bd.B("idxflat")], dma=True)
        ps, pb = bd.ring()
        p.op("tensor", lambda e, ps=ps: e.matmul(ps[0:128, 0:160], bd.c_ones_row[:],
                                                 bd.idxflat[:], start=True, stop=True),
             reads=[bd.B("idxflat"), bd.B("c_ones_row")], writes=[pb])
        p.op("vector", lambda e, ps=ps: e.tensor_copy(out=bd.idx_bc[:], in_=ps[0:128, 0:160]),
             reads=[pb], writes=[bd.B("idx_bc")])
        for c in range(NCH):
            p.op("vector", lambda e, c=c: e.tensor_scalar(
                out=bd.ohT[:, c, :], in0=bd.idx_bc[:], scalar1=bd.c_iotaPC[:, c:c + 1],
                scalar2=None, op0=ALU.is_equal),
                reads=[bd.B("idx_bc"), bd.B("c_iotaPC")], writes=[bd.B("ohT")])

    def a5(b):
        psGA, pbGA = bd.xbank()
        psGB, pbGB = bd.xbank2()
        qspb = bd.B("qspT%s" % ("" if b == 0 else "2"))
        for c in range(NCH):
            j = c % 2
            qcv, qcb = getattr(bd, "qc_%d" % j), bd.B("qc_%d" % j)
            p.op("sync", lambda e, c=c, b=b, qcv=qcv: e.dma_start(
                out=qcv[:], in_=bd.s_q[b].ap()[c]),
                reads=[bd.B("s_q%d" % b)], writes=[qcb], dma=True)
            for pr in range(NPAIR):
                for hh in range(2):
                    h = 2 * pr + hh
                    fst = (c == 0 and pr == 0)
                    lst = (c == NCH - 1 and pr == NPAIR - 1)
                    ctok = (pr == NPAIR - 1)
                    ps = psGA if hh == 0 else psGB
                    pb = pbGA if hh == 0 else pbGB
                    kw = {} if hh == 0 else {"tile_position": (0, 64)}
                    base = 0 if hh == 0 else 64
                    p.op("tensor", lambda e, ps=ps, c=c, qcv=qcv, pr=pr, h=h, base=base,
                         kw=kw, fst=fst, lst=lst: e.matmul(
                             ps[base:base + 64, pr * 10:(pr + 1) * 10],
                             qcv[:, h * 64:(h + 1) * 64],
                             bd.ohT[:, c, h * 10:(h + 1) * 10],
                             start=fst, stop=lst, **kw),
                         reads=[qcb, bd.B("ohT")], writes=[pb] if lst else (),
                         inc=(lst or ctok))
        p.op("vector", lambda e, ps=psGA, b=b: e.tensor_copy(
            out=QSP[b][0:64, :], in_=ps[0:64, 0:80]),
            reads=[pbGA], writes=[qspb])
        p.op("vector", lambda e, ps=psGB, b=b: e.tensor_copy(
            out=QSP[b][64:128, :], in_=ps[64:128, 0:80]),
            reads=[pbGB], writes=[qspb])

    a4(0)
    bd._tapon = True
    bd._tapsuf = ""
    bd.tap("idxf", bd.idxf[:], bd.B("idxf"))
    bd.tap("qs0", bd.qs0[0:16, :], QSB[0])
    bd.tap("idxT", bd.idxT[:], bd.B("idxT"))
    bd.tap("valid", bd.valid[:], bd.B("valid"))
    bd._tapon = False
    a5(0)
    v_pre()
    for _ in range(12):
        vblocks.pop(0)()
    a4(1)
    for _ in range(10):
        vblocks.pop(0)()
    a5(1)
    while vblocks:
        vblocks.pop(0)()
    bd._ringw = 4

    bd._ringw = 4
    PSB = [bd.p_sb, bd.p_sb2]
    PSBB = [bd.B("p_sb"), bd.B("p_sb2")]
    PRT = [bd.prT, bd.prT2]
    PRTB = [bd.B("prT"), bd.B("prT2")]
    KP = [bd.kpair, bd.kpair2]
    KPB = [bd.B("kpair"), bd.B("kpair2")]
    RV = [bd.rv_all, bd.rv_all2]
    RVB = [bd.B("rv_all"), bd.B("rv_all2")]
    for pr in range(NPAIR):
        for b in range(2):
            st = getattr(bd, "stat_%d" % b)
            stB = bd.B("stat_%d" % b)
            m4, mcol, mbias = st[:, 0:4], st[:, 4:5], st[:, 5:6]
            kp, kpb = KP[b], KPB[b]
            p.op("sync", lambda e, pr=pr, b=b, kp=kp: e.dma_start(
                out=kp[:], in_=bd.s_k[b].ap()[pr]),
                reads=[bd.B("s_k%d" % b)], writes=[kpb], dma=True)
            banks = [bd.ring() for _ in range(4)]
            for jj in range(4):
                ps, pb = banks[jj]
                for hh in range(2):
                    base = 0 if hh == 0 else 64
                    tp = (0, 0) if hh == 0 else (64, 64)
                    p.op("tensor", lambda e, ps=ps, jj=jj, pr=pr, b=b, base=base, tp=tp,
                         kp=kp: e.matmul(
                             ps[base:base + 10, 0:512],
                             QSP[b][base:base + 64, pr * 10:(pr + 1) * 10],
                             kp[base:base + 64, jj * 512:(jj + 1) * 512],
                             start=True, stop=True, tile_position=tp),
                         reads=[bd.B("qspT%s" % ("" if b == 0 else "2")), kpb],
                         writes=[pb])
            for jj in range(4):
                ps, pb = banks[jj]
                p.op("vector", lambda e, ps=ps, jj=jj, m4=m4: e.reduce_max(
                    out=m4[:, jj:jj + 1], in_=ps[0:128, 0:512], axis=AX.X),
                    reads=[pb], writes=[stB])
            p.op("vector", lambda e, m4=m4, mcol=mcol: e.reduce_max(
                out=mcol, in_=m4, axis=AX.X), reads=[stB], writes=[stB])
            p.op("vector", lambda e, mcol=mcol, mbias=mbias: e.tensor_scalar(
                out=mbias, in0=mcol, scalar1=-0.125, scalar2=None, op0=ALU.mult),
                reads=[stB], writes=[stB])
            for jj in range(4):
                ps, pb = banks[jj]
                p.op("scalar", lambda e, ps=ps, jj=jj, b=b, mbias=mbias, m4=m4: e.activation(
                    out=PSB[b][:, jj * 512:(jj + 1) * 512], in_=ps[0:128, 0:512],
                    func=AF.Exp, bias=mbias, scale=0.125, accum_out=m4[:, jj:jj + 1]),
                    reads=[pb, stB, bd.B("c_cap")], writes=[PSBB[b], stB])
            p.op("vector", lambda e, m4=m4, mcol=mcol: e.reduce_sum(
                out=mcol, in_=m4, axis=AX.X), reads=[stB], writes=[stB])
            p.op("vector", lambda e, mcol=mcol: e.reciprocal(out=mcol, in_=mcol),
                 reads=[stB], writes=[stB])
            p.op("vector", lambda e, pr=pr, b=b, mcol=mcol: e.tensor_tensor(
                out=RV[b][:, pr:pr + 1], in0=mcol, in1=VALID[b][:], op=ALU.mult),
                reads=[stB, bd.B("valid%s" % ("" if b == 0 else "2"))], writes=[RVB[b]])
            for c in range(NCH):
                pt, ptb = bd.tring()
                p.op("tensor", lambda e, pt=pt, c=c, b=b: e.transpose(
                    pt[0:128, 0:128], PSB[b][:, c * 128:(c + 1) * 128], bd.c_idn16[:]),
                    reads=[PSBB[b], bd.B("c_idn16")], writes=[ptb])
                p.op("vector", lambda e, pt=pt, c=c, pr=pr, b=b: e.tensor_copy(
                    out=PRT[b][:, c, pr * 20:pr * 20 + 10], in_=pt[0:128, 0:10]),
                    reads=[ptb], writes=[PRTB[b]])
                p.op("vector", lambda e, pt=pt, c=c, pr=pr, b=b: e.tensor_copy(
                    out=PRT[b][:, c, pr * 20 + 10:pr * 20 + 20], in_=pt[0:128, 64:74]),
                    reads=[ptb], writes=[PRTB[b]])

    def a78(b):
        psOA, pbOA = bd.xbank()
        psOB, pbOB = bd.xbank2()
        for c in range(NCH):
            j = c % 2
            vcv, vcb = getattr(bd, "vc_%d" % j), bd.B("vc_%d" % j)
            p.op("sync", lambda e, c=c, b=b, vcv=vcv: e.dma_start(
                out=vcv[:], in_=bd.s_v[b].ap()[c]),
                reads=[bd.B("s_v%d" % b)], writes=[vcb], dma=True)
            for pr in range(NPAIR):
                for hh in range(2):
                    fst = (c == 0 and pr == 0)
                    lst = (c == NCH - 1 and pr == NPAIR - 1)
                    ctok = (pr == NPAIR - 1)
                    ps = psOA if hh == 0 else psOB
                    pb = pbOA if hh == 0 else pbOB
                    base = 0 if hh == 0 else 64
                    kw = {} if hh == 0 else {"tile_position": (0, 64)}
                    p.op("tensor", lambda e, ps=ps, c=c, vcv=vcv, pr=pr, hh=hh, b=b,
                         base=base, kw=kw, fst=fst, lst=lst: e.matmul(
                             ps[base:base + 10, pr * 64:(pr + 1) * 64],
                             PRT[b][:, c, pr * 20 + hh * 10: pr * 20 + (hh + 1) * 10],
                             vcv[:, pr * 128 + hh * 64: pr * 128 + (hh + 1) * 64],
                             start=fst, stop=lst, **kw),
                         reads=[PRTB[b], vcb], writes=[pb] if lst else (),
                         inc=(lst or ctok))
        fB = bd.B("full")
        for pr in range(NPAIR):
            sj = pr % 2
            sc, sb = getattr(bd, "scat_%d" % sj), bd.B("scat_%d" % sj)
            p.op("vector", lambda e, sc=sc: e.memset(sc[:], 0.0), writes=[sb])
            p.op("vector", lambda e, ps=psOA, pr=pr, sc=sc, b=b: e.tensor_scalar(
                out=sc[0:10, 0:64], in0=ps[0:10, pr * 64:(pr + 1) * 64],
                scalar1=RV[b][0:10, pr:pr + 1], scalar2=None, op0=ALU.mult),
                reads=[pbOA, RVB[b]], writes=[sb])
            p.op("vector", lambda e, ps=psOB, pr=pr, sc=sc, b=b: e.tensor_scalar(
                out=sc[64:74, 64:128], in0=ps[64:74, pr * 64:(pr + 1) * 64],
                scalar1=RV[b][64:74, pr:pr + 1], scalar2=None, op0=ALU.mult),
                reads=[pbOB, RVB[b]], writes=[sb])
            ohB = bd.B("oh2")
            p.op("vector", lambda e, pr=pr, b=b: e.tensor_scalar(
                out=bd.oh2[0:10, :], in0=bd.c_iotaF[0:10, :],
                scalar1=IDXT[b][0:10, 2 * pr:2 * pr + 1], scalar2=None, op0=ALU.is_equal),
                reads=[bd.B("c_iotaF"), bd.B("idxT%s" % ("" if b == 0 else "2"))],
                writes=[ohB])
            p.op("vector", lambda e, pr=pr, b=b: e.tensor_scalar(
                out=bd.oh2[64:74, :], in0=bd.c_iotaF[64:74, :],
                scalar1=IDXT[b][64:74, 2 * pr + 1:2 * pr + 2], scalar2=None,
                op0=ALU.is_equal),
                reads=[bd.B("c_iotaF"), bd.B("idxT%s" % ("" if b == 0 else "2"))],
                writes=[ohB])
            for jj in range(4):
                ps, pb = bd.ring()
                p.op("tensor", lambda e, ps=ps, jj=jj, sc=sc: e.matmul(
                    ps[0:128, 0:512], sc[:], bd.oh2[:, jj * 512:(jj + 1) * 512],
                    start=True, stop=True),
                    reads=[sb, ohB], writes=[pb])
                p.op("vector", lambda e, ps=ps, jj=jj, pr=pr: e.tensor_copy(
                    out=bd.full[:, pr, jj * 512:(jj + 1) * 512], in_=ps[0:128, 0:512]),
                    reads=[pb], writes=[fB])
        if b == 0:
            bd._tapon = True
            bd.tap("prT", PRT[0][:], PRTB[0])
            bd._tapon = False
        if b == 0 and "psO" in TAPS:
            bd._tapon = True
            p.op("vector", lambda e, ps=psOA: e.tensor_copy(out=bd.q2s_0[0:10, 0:512],
                                                            in_=ps[0:10, 0:512]),
                 reads=[pbOA], writes=[bd.B("q2s_0")])
            p.op("vector", lambda e, ps=psOB: e.tensor_copy(out=bd.q2s_0[64:74, 0:512],
                                                            in_=ps[64:74, 0:512]),
                 reads=[pbOB], writes=[bd.B("q2s_0")])
            bd.tap("psO", bd.q2s_0[:, 0:512], bd.B("q2s_0"))
            bd._tapon = False
        if b == 0:
            bd._tapon = True
            bd.tap("full", bd.full[:], fB)
            bd.tap("rv0", bd.rv_all[:, 0:1], bd.B("rv_all"))
            bd._tapon = False
        spB = bd.B("s_sp")
        for mc in range(KC):
            j = mc % 2
            xrv, xrb = getattr(bd, "xres_%d" % j), bd.B("xres_%d" % j)
            p.op("scalar", lambda e, mc=mc, b=b, xrv=xrv: e.dma_start(
                out=xrv[:], in_=xt_r[:, mc, BT[b]:BT[b] + L]),
                writes=[xrb], dma=True)
            ws, wsb = getattr(bd, "wstage_%d" % j), bd.B("wstage_%d" % j)
            p.op("sync", lambda e, mc=mc, ws=ws: e.dma_start(
                out=ws[:], in_=bd.s_wo.ap()[:, :, mc * 128:(mc + 1) * 128]),
                writes=[wsb], extra=bd.prep_tok["wo"], dma=True)
            for tt in range(4):
                ps, pb = bd.ring()
                for k in range(KC):
                    p.op("tensor", lambda e, ps=ps, k=k, tt=tt, ws=ws: e.matmul(
                        ps[0:128, 0:512], ws[:, k, :],
                        bd.full[:, k, tt * 512:(tt + 1) * 512],
                        start=(k == 0), stop=(k == KC - 1)),
                        reads=[wsb, fB], writes=[pb], inc=(k == KC - 1))
                si = 2 * b + tt // 2
                if si == 0:
                    p.op("vector", lambda e, ps=ps, tt=tt, mc=mc, xrv=xrv:
                         e.scalar_tensor_tensor(
                             out=bd.strmG[:, mc, tt * 512:(tt + 1) * 512],
                             in0=ps[0:128, 0:512], scalar=bd.c_bo[:, mc:mc + 1],
                             in1=xrv[:, tt * 512:(tt + 1) * 512], op0=ALU.add, op1=ALU.add),
                         reads=[pb, xrb, bd.B("c_bo")], writes=[bd.B("strmG")])
                    continue
                sj = tt % 2
                ssv, ssb = getattr(bd, "spillscr_%d" % sj), bd.B("spillscr_%d" % sj)
                p.op("vector", lambda e, ps=ps, tt=tt, mc=mc, xrv=xrv, ssv=ssv:
                     e.scalar_tensor_tensor(
                         out=ssv[:], in0=ps[0:128, 0:512], scalar=bd.c_bo[:, mc:mc + 1],
                         in1=xrv[:, tt * 512:(tt + 1) * 512], op0=ALU.add, op1=ALU.add),
                     reads=[pb, xrb, bd.B("c_bo")], writes=[ssb])
                p.op("scalar", lambda e, mc=mc, si=si, tt=tt, ssv=ssv: e.dma_start(
                    out=bd.s_sp.ap()[si][:, mc, (tt % 2) * 512:(tt % 2 + 1) * 512],
                    in_=ssv[:]),
                    reads=[ssb], writes=[spB], dma=True)

    a78(0)
    a78(1)


def _ffn_group(bd, si):
    p = bd.p
    par = si % 2
    strm = bd.strmG if par == 0 else bd.strmB
    sgB = bd.B("strmG" if par == 0 else "strmB")
    hbf = bd.hbuf if par == 0 else bd.hbufB
    hbB = bd.B("hbuf" if par == 0 else "hbufB")
    abf = bd.abuf if par == 0 else bd.abufB
    abB = bd.B("abuf" if par == 0 else "abufB")
    bt = si * GTOK
    if si > 0:
        for mc in range(KC):
            p.op("scalar" if mc % 2 == 0 else "sync",
                 lambda e, si=si, strm=strm, mc=mc: e.dma_start(
                     out=strm[:, mc, :], in_=bd.s_sp.ap()[si, :, mc, :]),
                 reads=[bd.B("s_sp")], writes=[sgB], dma=True)

    bd.ln_rows(strm[:], sgB)

    bd.tap("strm_a8", strm[:], sgB)

    def to_strm(mc, tt, scr, scb):
        p.op("vector", lambda e, sc=scr, mc=mc, tt=tt: e.tensor_copy(
            out=strm[:, mc, tt * 512:(tt + 1) * 512], in_=sc[:]),
            reads=[scb], writes=[sgB])
    bd.ln_apply(strm[:], sgB, to_strm)
    bd.tap("y1", strm[:], sgB)

    rR, rW = bd.B("r_row"), bd.B("rowscr")
    p.op("vector", lambda e: e.tensor_tensor(out=bd.rowscr[:], in0=bd.r_row[:],
                                             in1=bd.r_row[:], op=ALU.mult),
         reads=[rR], writes=[rW])
    p.op("vector", lambda e: e.tensor_scalar(out=bd.rowscr[:], in0=bd.rowscr[:],
                                             scalar1=-EPS, scalar2=1.0 + EPS,
                                             op0=ALU.mult, op1=ALU.add),
         reads=[rW], writes=[rW])
    p.op("scalar", lambda e: e.activation(out=bd.rowscr[:], in_=bd.rowscr[:],
                                          func=AF.Ln, bias=0.0, scale=1.0),
         reads=[rW], writes=[rW])
    p.op("scalar", lambda e: e.activation(out=bd.rowscr[:], in_=bd.rowscr[:],
                                          func=AF.Exp, bias=0.0, scale=-0.5),
         reads=[rW, bd.B("c_cap")], writes=[rW])
    bd.bcast_row(bd.rowscr[:], rW, GTOK, bd.r1b[:], bd.B("r1b"))
    p.op("vector", lambda e: e.memset(bd.r2b[:], 0.0), writes=[bd.B("r2b")])
    zB = bd.B("zbuf")

    def to_z(mc, tt, scr, scb):
        p.op("gpsimd", lambda e, sc=scr, mc=mc, tt=tt: e.tensor_scalar(
            out=bd.zbuf[:, mc, tt * 512:(tt + 1) * 512], in0=sc[:],
            scalar1=bd.c_l1g[:, mc:mc + 1], scalar2=bd.c_l1b[:, mc:mc + 1],
            op0=ALU.mult, op1=ALU.add),
            reads=[scb, bd.B("c_l1g"), bd.B("c_l1b")], writes=[zB])
    bd.ln_apply(strm[:], sgB, to_z)

    for mc in range(KC):
        wj = mc % 2
        ws, wsb = getattr(bd, "wstage_%d" % wj), bd.B("wstage_%d" % wj)
        p.op("sync", lambda e, mc=mc, ws=ws: e.dma_start(
            out=ws[:], in_=bd.s_awt.ap()[:, :, mc * 128:(mc + 1) * 128]),
            writes=[wsb], extra=bd.prep_tok["awt"], dma=True)
        for tt in range(2):
            ps, pb = bd.ring()
            for k in range(KC):
                p.op("tensor", lambda e, ps=ps, k=k, tt=tt, ws=ws: e.matmul(
                    ps[0:128, 0:512], ws[:, k, :], bd.zbuf[:, k, tt * 512:(tt + 1) * 512],
                    start=(k == 0), stop=(k == KC - 1)),
                    reads=[wsb, zB], writes=[pb], inc=(k == KC - 1))
            p.op("vector", lambda e, ps=ps, mc=mc, tt=tt: e.tensor_tensor(
                out=hbf[:, mc, tt * 512:(tt + 1) * 512],
                in0=strm[:, mc, tt * 512:(tt + 1) * 512], in1=ps[0:128, 0:512],
                op=ALU.add),
                reads=[pb, sgB], writes=[hbB])

    bd.ln_rows(hbf[:], hbB)
    tB = bd.B("tbuf")
    lgp = [bd.ring(), bd.ring()]
    lgB = bd.B("lgsb")

    def to_t(mc, tt, scr, scb):
        t32, t32b = getattr(bd, "t32scr_%d" % tt), bd.B("t32scr_%d" % tt)
        tlo, tlob = getattr(bd, "tloscr_%d" % tt), bd.B("tloscr_%d" % tt)
        sgs, sgsb = getattr(bd, "sgscr_%d" % tt), bd.B("sgscr_%d" % tt)
        p.op("vector", lambda e, sc=scr, mc=mc, t32=t32: e.tensor_scalar(
            out=t32[:], in0=sc[:], scalar1=bd.c_l2g[:, mc:mc + 1],
            scalar2=bd.c_l2b[:, mc:mc + 1], op0=ALU.mult, op1=ALU.add),
            reads=[scb, bd.B("c_l2g"), bd.B("c_l2b")], writes=[t32b])
        p.op("vector", lambda e, mc=mc, tt=tt, t32=t32: e.tensor_copy(
            out=bd.tbuf[:, mc, tt * 512:(tt + 1) * 512], in_=t32[:]),
            reads=[t32b], writes=[tB])
        p.op("gpsimd", lambda e, mc=mc, tt=tt, sgs=sgs: e.tensor_copy(
            out=sgs[:], in_=bd.tbuf[:, mc, tt * 512:(tt + 1) * 512]),
            reads=[tB], writes=[sgsb])
        p.op("vector", lambda e, t32=t32, tlo=tlo, sgs=sgs: e.tensor_tensor(
            out=tlo[:], in0=t32[:], in1=sgs[:], op=ALU.subtract),
            reads=[t32b, sgsb], writes=[tlob])
        ps, pb = lgp[tt]
        passes = [(bd.c_gwhi, bd.B("c_gwhi"), None, tB),
                  (bd.c_gwhi, bd.B("c_gwhi"), tlo, tlob),
                  (bd.c_gwlo, bd.B("c_gwlo"), None, tB)]
        for pi, (wv, wvB, rhs, rhsB) in enumerate(passes):
            fst = (mc == 0 and pi == 0)
            lst = (mc == KC - 1 and pi == 2)
            p.op("tensor", lambda e, ps=ps, wv=wv, mc=mc, tt=tt, rhs=rhs,
                 fst=fst, lst=lst: e.matmul(
                     ps[0:8, 0:512], wv[:, mc * 8:(mc + 1) * 8],
                     bd.tbuf[:, mc, tt * 512:(tt + 1) * 512] if rhs is None else rhs[:],
                     start=fst, stop=lst),
                 reads=[wvB, rhsB], writes=[pb] if lst else (), inc=True)
    bd.ln_apply(hbf[:], hbB, to_t)
    for tt in range(2):
        ps, pb = lgp[tt]
        p.op("vector", lambda e, ps=ps, tt=tt: e.tensor_copy(
            out=bd.lgsb[0:8, tt * 512:(tt + 1) * 512], in_=ps[0:8, 0:512]),
            reads=[pb], writes=[lgB])

    bd.tap("hbuf", hbf[:], hbB)
    bd.tap("tbuf", bd.tbuf[:], tB)
    bd.tap("lgsb", bd.lgsb[:], lgB)
    for mc in range(KC):
        p.op("vector", lambda e, mc=mc: e.tensor_tensor(
            out=strm[:, mc, :], in0=strm[:, mc, :], in1=hbf[:, mc, :],
            op=ALU.add),
            reads=[hbB, sgB], writes=[sgB])

    cfB, ctB, ltB, gcB = bd.B("combfm"), bd.B("combtm"), bd.B("lgtm"), bd.B("gcols")
    for c in range(KC):
        ps, pb = bd.ring()
        p.op("tensor", lambda e, ps=ps, c=c: e.transpose(
            ps[0:128, 0:8], bd.lgsb[0:8, c * 128:(c + 1) * 128], bd.c_idnF[0:8, 0:8]),
            reads=[lgB, bd.B("c_idnF")], writes=[pb])
        p.op("vector", lambda e, ps=ps, c=c: e.tensor_copy(
            out=bd.lgtm[:, c * 8:(c + 1) * 8], in_=ps[0:128, 0:8]),
            reads=[pb], writes=[ltB])
    mx = bd.gcols[:, 0:8]
    iu = bd.gcols[:, 8:16].bitcast(U32)
    e12 = bd.gcols[:, 16:18]
    w1c = bd.gcols[:, 18:19]
    w2c = bd.gcols[:, 19:20]
    dc_ = bd.gcols[:, 20:21]
    for c in range(KC):
        p.op("vector", lambda e, c=c: e.max(out=mx, in_=bd.lgtm[:, c * 8:(c + 1) * 8]),
             reads=[ltB], writes=[gcB])
        p.op("vector", lambda e, c=c: e.max_index(out=iu, in_max=mx,
                                                  in_values=bd.lgtm[:, c * 8:(c + 1) * 8]),
             reads=[ltB, gcB], writes=[gcB])
        p.op("vector", lambda e: e.tensor_copy(out=e12, in_=iu[:, 0:2]),
             reads=[gcB], writes=[gcB])
        p.op("vector", lambda e: e.tensor_tensor(out=dc_, in0=mx[:, 1:2], in1=mx[:, 0:1],
                                                 op=ALU.subtract),
             reads=[gcB], writes=[gcB])
        p.op("scalar", lambda e: e.activation(out=dc_, in_=dc_, func=AF.Exp, bias=0.0,
                                              scale=1.0),
             reads=[gcB], writes=[gcB])
        p.op("vector", lambda e: e.tensor_scalar(out=dc_, in0=dc_, scalar1=1.0, scalar2=None,
                                                 op0=ALU.add),
             reads=[gcB], writes=[gcB])
        p.op("vector", lambda e: e.reciprocal(out=w1c, in_=dc_),
             reads=[gcB], writes=[gcB])
        p.op("vector", lambda e: e.tensor_scalar(out=w2c, in0=w1c, scalar1=-1.0, scalar2=1.0,
                                                 op0=ALU.mult, op1=ALU.add),
             reads=[gcB], writes=[gcB])
        p.op("vector", lambda e, c=c: e.tensor_scalar(
            out=bd.lgtm[:, c * 8:(c + 1) * 8], in0=bd.c_iotaF[:, 0:8],
            scalar1=e12[:, 0:1], scalar2=w1c, op0=ALU.is_equal, op1=ALU.mult),
            reads=[bd.B("c_iotaF"), gcB], writes=[ltB])
        p.op("vector", lambda e, c=c: e.tensor_scalar(
            out=bd.combtm[:, c * 8:(c + 1) * 8], in0=bd.c_iotaF[:, 0:8],
            scalar1=e12[:, 1:2], scalar2=w2c, op0=ALU.is_equal, op1=ALU.mult),
            reads=[bd.B("c_iotaF"), gcB], writes=[ctB])
        p.op("vector", lambda e, c=c: e.tensor_tensor(
            out=bd.combtm[:, c * 8:(c + 1) * 8], in0=bd.combtm[:, c * 8:(c + 1) * 8],
            in1=bd.lgtm[:, c * 8:(c + 1) * 8], op=ALU.add),
            reads=[ctB, ltB], writes=[ctB])
    for c in range(KC):
        ps, pb = bd.ring()
        p.op("tensor", lambda e, ps=ps, c=c: e.transpose(
            ps[0:8, 0:128], bd.combtm[:, c * 8:(c + 1) * 8], bd.c_idnF[:]),
            reads=[ctB, bd.B("c_idnF")], writes=[pb])
        p.op("vector", lambda e, ps=ps, c=c: e.tensor_copy(
            out=bd.combfm[0:8, c * 128:(c + 1) * 128], in_=ps[0:8, 0:128]),
            reads=[pb], writes=[cfB])

    bd.tap("combfm", bd.combfm[:], cfB)
    cbB = bd.B("combbc")
    for e_ in range(NEXP):
        for tt in range(2):
            ps, pb = bd.ring()
            p.op("tensor", lambda e, ps=ps, e_=e_, tt=tt: e.matmul(
                ps[0:128, 0:512], bd.c_basis[:, e_ * 128:(e_ + 1) * 128],
                bd.combfm[0:8, tt * 512:(tt + 1) * 512], start=True, stop=True),
                reads=[cfB, bd.B("c_basis")], writes=[pb])
            p.op("vector", lambda e, ps=ps, tt=tt: e.tensor_copy(
                out=bd.combbc[:, tt * 512:(tt + 1) * 512], in_=ps[0:128, 0:512]),
                reads=[pb], writes=[cbB])
        NH = ICH // 2
        for half in range(2):
            for il in range(NH):
                ic = half * NH + il
                sj = il % 3
                gst, gsb = getattr(bd, "gstage_%d" % sj), bd.B("gstage_%d" % sj)
                ust, usb = getattr(bd, "ustage_%d" % sj), bd.B("ustage_%d" % sj)
                p.op("gpsimd", lambda e, e_=e_, ic=ic, gst=gst: e.dma_start(
                    out=gst[:], in_=bd.gpt.ap()[e_, ic]),
                    writes=[gsb], dma=True)
                p.op("gpsimd", lambda e, e_=e_, ic=ic, ust=ust: e.dma_start(
                    out=ust[:], in_=bd.upt.ap()[e_, ic]),
                    writes=[usb], dma=True)
                gb = [bd.ring(), bd.ring()]
                ub = [bd.ring(), bd.ring()]
                for (stage, stb, banks) in ((gst, gsb, gb), (ust, usb, ub)):
                    for k in range(KC):
                        for tt in range(2):
                            ps, pb = banks[tt]
                            p.op("tensor", lambda e, ps=ps, stg=stage, k=k, tt=tt: e.matmul(
                                ps[0:128, 0:512], stg[:, k, :],
                                bd.tbuf[:, k, tt * 512:(tt + 1) * 512],
                                start=(k == 0), stop=(k == KC - 1)),
                                reads=[stb, tB], writes=[pb], inc=(k == KC - 1))
                for tt in range(2):
                    psg, pbg = gb[tt]
                    psu, pbu = ub[tt]
                    sgs, sgsb = getattr(bd, "sgscr_%d" % tt), bd.B("sgscr_%d" % tt)
                    p.op("scalar", lambda e, ps=psg, sgs=sgs: e.activation(
                        out=sgs[:], in_=ps[0:128, 0:512], func=AF.Silu, bias=0.0, scale=1.0),
                        reads=[pbg], writes=[sgsb])
                    p.op("vector", lambda e, ps=psu, sgs=sgs, il=il, tt=tt: e.tensor_tensor(
                        out=abf[:, il, tt * 512:(tt + 1) * 512], in0=sgs[:],
                        in1=ps[0:128, 0:512], op=ALU.mult),
                        reads=[sgsb, pbu], writes=[abB])
            for dcc in range(KC):
                sj = dcc % 3
                dst, dsb = getattr(bd, "dstage_%d" % sj), bd.B("dstage_%d" % sj)
                p.op("gpsimd", lambda e, e_=e_, dcc=dcc, dst=dst, half=half: e.dma_start(
                    out=dst[:], in_=bd.dpt.ap()[e_, dcc][:, half * NH:(half + 1) * NH, :]),
                    writes=[dsb], dma=True)
                banks = [bd.ring(), bd.ring()]
                for il in range(NH):
                    for tt in range(2):
                        ps, pb = banks[tt]
                        p.op("tensor", lambda e, ps=ps, dst=dst, il=il, tt=tt: e.matmul(
                            ps[0:128, 0:512], dst[:, il, :],
                            abf[:, il, tt * 512:(tt + 1) * 512],
                            start=(il == 0), stop=(il == NH - 1)),
                            reads=[dsb, abB], writes=[pb], inc=(il == NH - 1))
                for tt in range(2):
                    ps, pb = banks[tt]
                    hs, hsb = getattr(bd, "haccscr_%d" % tt), bd.B("haccscr_%d" % tt)
                    p.op("vector", lambda e, ps=ps, hs=hs, tt=tt: e.tensor_tensor(
                        out=hs[:], in0=ps[0:128, 0:512],
                        in1=bd.combbc[:, tt * 512:(tt + 1) * 512], op=ALU.mult),
                        reads=[pb, cbB], writes=[hsb])
                    p.op("vector", lambda e, hs=hs, dcc=dcc, tt=tt: e.tensor_tensor(
                        out=strm[:, dcc, tt * 512:(tt + 1) * 512],
                        in0=strm[:, dcc, tt * 512:(tt + 1) * 512], in1=hs[:], op=ALU.add),
                        reads=[hsb], writes=[sgB])

    bd.tap("strm_pre", strm[:], sgB)
    bd.ln_rows(strm[:], sgB)

    out_r = bd.out.ap().rearrange("(kc q) t -> q kc t", q=128)

    def to_out(mc, tt, scr, scb):
        p.op("vector", lambda e, sc=scr, mc=mc, tt=tt: e.tensor_copy(
            out=strm[:, mc, tt * 512:(tt + 1) * 512], in_=sc[:]),
            reads=[scb], writes=[sgB])
        if tt == 1:
            p.op("sync", lambda e, mc=mc: e.dma_start(
                out=out_r[:, mc, bt:bt + GTOK], in_=strm[:, mc, :]),
                reads=[sgB], writes=[bd.B("out")], dma=True)
    bd.ln_apply(strm[:], sgB, to_out)


def build_program():
    bd = Build()
    p = bd.p
    bd.build_consts()
    bd.build_prep()
    bd._tapon = False
    _attention_merged(bd)
    for si in range(4):
        bd._tapon = (si < 2)
        bd._tapsuf = "" if si == 0 else "_g1"
        _ffn_group(bd, si)
    alldma = [(s, p.cnt[s]) for q in DSEM for s in DSEM[q] if p.cnt[s]]
    p.items["sync"].append((lambda e: e.nop(), alldma, None, 0))
    for s in SEMS:
        assert p.cnt[s] < 60000, (s, p.cnt[s])
    nc = bd.nc
    import contextlib
    with contextlib.ExitStack() as stk:
        sems = {s: stk.enter_context(nc.semaphore(s)) for s in SEMS}
        block = stk.enter_context(nc.Block())
        p.install(block, sems)
    return bd


def kernel(**inputs):
    f = lambda n: np.asarray(inputs[n], np.float32)
    x = f("L0_attn_x")
    pc = lambda v: np.ascontiguousarray(v.reshape(KC, 128).T)
    row = lambda v: np.ascontiguousarray(v.reshape(1, D))
    def wfm(w):
        return np.ascontiguousarray(w.T.reshape(KC, 128, w.shape[0]).transpose(1, 0, 2))

    gp, up, dp = f("L0_ffn_gate_proj"), f("L0_ffn_up_proj"), f("L0_ffn_down_proj")
    com = {
        "wqt": wfm(f("L0_attn_Wq")), "wkt": wfm(f("L0_attn_Wk")),
        "wvt": wfm(f("L0_attn_Wv")), "wot": wfm(f("L0_attn_Wo")),
        "awt": wfm(f("L0_ffn_attn_w")),
        "gwt": np.ascontiguousarray(
            f("L0_ffn_gate_w").T.reshape(KC, 128, NEXP).transpose(1, 0, 2)),
        "gpt": np.ascontiguousarray(
            gp.reshape(NEXP, ICH, 128, KC, 128).transpose(0, 1, 4, 3, 2)),
        "upt": np.ascontiguousarray(
            up.reshape(NEXP, ICH, 128, KC, 128).transpose(0, 1, 4, 3, 2)),
        "dpt": np.ascontiguousarray(
            dp.reshape(NEXP, KC, 128, ICH, 128).transpose(0, 1, 4, 3, 2)),
        "bq_r": row(f("L0_attn_bq")), "bv_r": row(f("L0_attn_bv")),
        "bk_pc": pc(f("L0_attn_bk")), "bo_pc": pc(f("L0_attn_bo")),
        "l1g": pc(f("L0_ffn_ln1_g")), "l1b": pc(f("L0_ffn_ln1_b")),
        "l2g": pc(f("L0_ffn_ln2_g")), "l2b": pc(f("L0_ffn_ln2_b")),
    }
    in_maps = []
    for c in range(N_CORES):
        m = dict(com)
        xc = np.concatenate([x[2 * c].T, x[2 * c + 1].T], axis=1)
        m["xt"] = np.ascontiguousarray(xc.reshape(KC, 128, TOKS).transpose(1, 0, 2))
        in_maps.append(m)
    bd = build_program()
    res = run_bass_kernel_spmd(bd.nc, in_maps, core_ids=list(range(N_CORES)))
    outs = []
    for c in range(N_CORES):
        o = np.asarray(res.results[c]["out"])
        outs.append(o[:, :L].T)
        outs.append(o[:, L:].T)
    return np.ascontiguousarray(np.stack(outs)).astype(np.float32)
